# version 11
# baseline (speedup 1.0000x reference)
import os
import numpy as np
from contextlib import ExitStack

import concourse.bass as bass
import concourse.mybir as mybir
from concourse.bass_utils import run_bass_kernel_spmd

F32 = mybir.dt.float32
BF16 = mybir.dt.bfloat16
AF = mybir.ActivationFunctionType
ALU = mybir.AluOpType
AX = mybir.AxisListType

NCORE = 8
D = 1024
NB = 8
T = 512
NT = 8
TOK = NT * T
H, HD = 16, 64
FF = 4096
NFB = 32
EPS = 1e-5
M32 = 12582912.0
M64 = 6755399441055744.0

DBG = os.environ.get("MK_DBG", "") == "1"


class Ctr:

    def __init__(self, name):
        self.name = name
        self.n = 0


class B:
    self_sync = {"vector", "scalar", "gpsimd"}

    def __init__(self):
        self.nc = bass.Bass("TRN2", target_bir_lowering=False, debug=False)
        self.items = []
        self.sems = {}
        self.eng_ctr = {}

    def op(self, eng, fn, waits=(), incs=(), is_dma=False):
        incs = list(incs)
        ws = [(c, v) for (c, v) in waits if v > 0]
        if eng in B.self_sync and fn is not None and not is_dma:
            ctr = self.eng_ctr.get(eng)
            if ctr is not None:
                ws.append((ctr, ctr.n))
                if not any(c is ctr for c, _ in incs):
                    incs.append((ctr, 1))
        for c, d in incs:
            c.n += d
        self.items.append((eng, fn, ws, incs))

    def dma(self, eng, out_ap, in_ap, sem, waits=(), slow=False):
        self.op(eng, lambda e, o=out_ap, i=in_ap, s=slow: e.dma_start(
            out=o, in_=i, allow_slow_non_contiguous=s),
                waits=waits, incs=[(sem, 16)], is_dma=True)

    def replay(self, eng_name, eng):
        for name, fn, waits, incs in self.items:
            if name != eng_name:
                continue
            for c, v in waits:
                eng.wait_ge(self.sems[c.name], v)
            instr = fn(eng) if fn is not None else None
            if incs:
                if instr is None:
                    instr = eng.nop()
                for c, d in incs:
                    instr = instr.then_inc(self.sems[c.name], d)


def _fm(v):
    return np.ascontiguousarray(np.asarray(v, np.float32).reshape(NB, 128).T)


def _sw(w):
    return np.ascontiguousarray(
        np.asarray(w, np.float32).reshape(8, 128, 8, 128).transpose(2, 1, 0, 3)
    ).reshape(8, 128, 1024)


def build_program():
    b = B()
    nc = b.nc
    es = ExitStack()
    with es:
        _build(b, es)
    return b


def _build(b, es):
    nc = b.nc
    dget = lambda n, sh, dt=F32, k="ExternalInput": nc.dram_tensor(n, sh, dt, kind=k).ap()
    d_x = dget("xT", [D, TOK + 1])
    d_w = {n: dget(n, [8, 128, 1024]) for n in ("WgT", "WrT", "WkT", "WvT", "WoT")}
    d_wmod = dget("wmodS", [D, 3 * D])
    d_mgi = dget("mgi", [3 * D, 1], F32, "Internal")
    d_mgo = dget("mgo", [6 * D, 1], F32, "Internal")
    d_wfc = dget("WfcT", [D, FF])
    d_wpj = dget("WprojT", [FF, D])
    d_afc = dget("Afc", [16, D])
    d_bfcT = dget("BfcT", [16, FF])
    d_apj = dget("Aproj", [16, FF])
    d_bpjT = dget("BprojT", [16, D])
    d_vpack = dget("vpack_in", [128, 26 * NB])
    d_bfc = dget("bfc", [128, NFB])
    d_bpj = dget("bproj", [128, NB])
    d_misc = dget("misc", [128, 16])
    d_onesK1 = dget("onesK1", [1, 128])
    d_ramp = dget("ramp", [1, T])
    d_gsel = dget("gsel", [128, NB * 16])
    d_gselT = dget("gselT", [16, NB * 128])
    d_out = dget("outT", [D, TOK], F32, "ExternalOutput")

    kd = "ExternalOutput" if DBG else "Internal"
    d_hsp = dget("hsp", [NT, D, T + 1], F32, kd)
    d_scsp = dget("scsp", [NT, D, T], F32, kd)
    d_x2sp = dget("x2sp", [D, TOK], F32, kd)
    d_wbf = dget("wbf", [5, 8, 128, 1024], BF16, "Internal")
    d_wpjbf = dget("wpjbf", [NB, 128, NFB * 128], BF16, "Internal")
    d_powsp = dget("powsp", [D, T], F32, "Internal")
    d_ccin = dget("ccin", [D, 1], F32, "Internal")
    d_ccout = dget("ccout", [2 * D, 1], F32, "Internal")
    d_bounce = dget("bounce", [8192, 1], F32, "Internal")

    sT = Ctr("sT"); sV = Ctr("sV"); sS = Ctr("sS"); sG = Ctr("sG")
    scc = Ctr("scc"); bar = Ctr("bar")
    sxb = Ctr("sxb"); sxx = Ctr("sxx"); sxh = Ctr("sxh"); sxhs = Ctr("sxhs")
    sxc = [Ctr("sxc0"), Ctr("sxc1")]
    sx2 = Ctr("sx2"); sout = Ctr("sout"); sxr = Ctr("sxr")
    sxp = [Ctr("sxp0"), Ctr("sxp1")]
    sxq = [Ctr("sxq0"), Ctr("sxq1")]
    swm = [Ctr("swm0"), Ctr("swm1")]
    swS = [Ctr(f"swS{i}") for i in range(6)]
    swp = [Ctr("swp0"), Ctr("swp1")]
    sgw = [Ctr(f"sgw{i}") for i in range(5)]
    sgc = Ctr("sgc"); sgq = Ctr("sgq"); sgq2 = Ctr("sgq2")
    sgl = [Ctr("sgl0"), Ctr("sgl1"), Ctr("sgl2")]
    sxps = Ctr("sxps")
    ctrs = ([sT, sV, sS, sG, scc, bar, sxb, sxx, sxh, sxhs, sx2, sout, sxr,
             sgc, sgq, sgq2, sxps] + sxp + sxc + sxq + swm + swS + swp + sgw + sgl)
    b.eng_ctr = {"vector": sV, "scalar": sS, "gpsimd": sG}

    sb = lambda n, sh, dt=F32: es.enter_context(nc.sbuf_tensor(n, sh, dt))
    misc = sb("s_misc", [128, 16])
    onesK1 = sb("s_onesK1", [1, 128])
    ones128b = sb("ones128b", [128, 1], BF16)
    gsel = sb("s_gsel", [128, NB * 16], BF16)
    rows = sb("rows", [128, 2048])
    _vec_names = ("mg", "mr", "mk", "mv", "bg2", "br", "bk", "bv", "uv",
                  "ca1", "sa", "cb1", "sb_", "ga", "gb", "gah", "gabo", "gbbp",
                  "e", "dec", "d512", "carry", "Din", "hlast", "recvmask",
                  "modsc")
    vpack = sb("vpack", [128, len(_vec_names) * NB])
    vec = {n: vpack[:, i * NB:(i + 1) * NB]
           for i, n in enumerate(_vec_names)}
    siluc = sb("siluc", [128, NB])
    bfcv = sb("bfcv", [128, NFB])
    bpjv = sb("bpjv", [128, NB])
    q_ = {}
    for n in ("delta", "Mpp", "sfc", "invfc", "spj", "invpj", "mxfc", "mxpj",
              "t1", "t2", "m11", "s11f", "s11p", "i11f", "i11p"):
        q_[n] = sb("q_" + n, [128, 1])


    pmm = es.enter_context(nc.psum_tensor("pmm", [128, 4 * 512], F32))
    pbc = es.enter_context(nc.psum_tensor("pbc", [128, 2 * 512], F32))
    pst = es.enter_context(nc.psum_tensor("pst", [128, 512], F32))
    pmo = es.enter_context(nc.psum_tensor("pmo", [128, 512], F32))

    block = es.enter_context(nc.Block())

    mc = lambda c: misc[:, c:c + 1]
    vb = lambda name, bb: vec[name][:, bb:bb + 1]

    def bc3(ps_slice):
        return ps_slice.unsqueeze(1).broadcast_to([128, NB, T])

    es_a = ExitStack()
    with es_a:
        sba = lambda n, sh, dt=F32: es_a.enter_context(nc.sbuf_tensor(n, sh, dt))
        xbuf = [sba(f"xb{i}", [128, NB, T + 1]) for i in range(2)]
        hb = sba("hb", [128, NB, T + 1])
        hm = [(sba("hmA0", [128, NB, T], BF16), sba("hmB0", [128, NB, T], BF16)),
              (sba("hmA1", [128, NB, T], BF16), sba("hmB1", [128, NB, T], BF16))]
        ka = sba("ka", [128, NB, T])
        va = sba("va", [128, NB, T])
        scb = [sba(f"sc{i}", [128, NB, T]) for i in range(2)]
        x2b = sba("x2b", [128, NB, T])
        pow_ = xbuf[1][:, :, 0:T]
        wch = sba("wch", [128, 6 * 1024], BF16)
        ramp = sba("s_ramp", [1, T])
        gselT = sba("s_gselT", [16, NB * 128])
        qst = sba("qst", [128, 2 * 1024])
        tmp = sba("tmp", [128, 2 * T])

        b.dma("sync", misc[:], d_misc, sxb)
        b.dma("sync", onesK1[:], d_onesK1, sxb)
        b.dma("sync", ramp[:], d_ramp, sxb)
        b.dma("sync", gselT[:], d_gselT, sxb)
        b.dma("sync", vpack[:], d_vpack, sxb)
        b.dma("sync", bfcv[:], d_bfc, sxb)
        b.dma("sync", bpjv[:], d_bpj, sxb)
        n_setup_sync = sxb.n
        b.dma("gpsimd", gsel[:], d_gsel, sgc)
        b.dma("gpsimd", ones128b[:], d_onesK1.rearrange("one p -> p one"), sgc)
        sgc_const = sgc.n
        b.dma("gpsimd", d_wbf[2], d_w["WkT"], sgw[2])
        b.dma("gpsimd", d_wbf[3], d_w["WvT"], sgw[3])
        b.dma("gpsimd", d_wbf[0], d_w["WgT"], sgw[0])
        b.dma("gpsimd", d_wbf[1], d_w["WrT"], sgw[1])
        b.dma("gpsimd", d_wbf[4], d_w["WoT"], sgw[4])
        sg_w = {"g": (sgw[0], 16), "r": (sgw[1], 16), "k": (sgw[2], 16),
                "v": (sgw[3], 16), "o": (sgw[4], 16)}

        wait_c = [(sxb, n_setup_sync)]

        def v(fn, waits=(), incs=((sV, 1),)):
            b.op("vector", fn, waits=waits, incs=incs)

        def s(fn, waits=(), incs=((sS, 1),)):
            b.op("scalar", fn, waits=waits, incs=incs)

        def t(fn, waits=(), incs=()):
            b.op("tensor", fn, waits=waits, incs=incs)

        def g(fn, waits=(), incs=((sG, 1),)):
            b.op("gpsimd", fn, waits=waits, incs=incs)

        v(lambda e: e.tensor_scalar(q_["t1"][:], mc(0), 2.0 ** -30, None, ALU.add),
          waits=wait_c)
        v(lambda e: e.tensor_scalar(q_["delta"][:], q_["t1"][:], 1.0, 2.0 ** 30,
                                    ALU.subtract, ALU.mult))
        v(lambda e: e.tensor_scalar(q_["Mpp"][:], q_["delta"][:], mc(6), mc(5),
                                    ALU.mult, ALU.add))
        v(lambda e: e.tensor_scalar_mul(vec["bg2"], vec["bg2"], 0.5), waits=wait_c)
        s(lambda e: e.activation(vec["e"], vec["e"], AF.Exp), waits=wait_c)
        s(lambda e: e.activation(vec["dec"], vec["e"], AF.Exp, scale=-1.0))
        sS_dec = sS.n
        s(lambda e: e.activation(siluc[:], vec["modsc"], AF.Silu), waits=wait_c)
        sS_silu = sS.n
        v(lambda e: e.memset(vec["carry"], 0.0))
        v(lambda e: e.memset(q_["mxfc"][:], 0.0))
        v(lambda e: e.memset(q_["mxpj"][:], 0.0))

        b.dma("sync", d_bounce[6144:7168].rearrange("(b p) one -> p (b one)", p=128),
              vec["e"], sxb, waits=[(sS, sS_dec)], slow=True)
        sx_eb = sxb.n
        b.dma("sync", rows[0:1, 1024:2048],
              d_bounce[6144:7168].rearrange("(b p) one -> one (b p)", p=128),
              sxb, waits=[(sxb, sx_eb)], slow=True)
        sx_erow = sxb.n
        for bb in range(NB):
            half = bb % 2
            t(lambda e, bb=bb, half=half: e.matmul(
                pbc[:, half * 512:(half + 1) * 512],
                rows[0:1, 1024 + bb * 128:1024 + (bb + 1) * 128], ramp[:],
                start=True, stop=True),
              waits=[(sxb, sx_erow), (sS, sS.n)], incs=[(sT, 1)])
            s(lambda e, bb=bb, half=half: e.activation(
                pow_[:, bb, :], pbc[:, half * 512:(half + 1) * 512], AF.Exp,
                scale=-1.0), waits=[(sT, sT.n)])
        sS_pow = sS.n
        v(lambda e: e.tensor_copy(vec["d512"], pow_[:, :, T - 1]),
          waits=[(sS, sS_pow)])
        sV_d512 = sV.n
        b.dma("sync", d_powsp.rearrange("(b p) t -> p b t", p=128), pow_,
              sxps, waits=[(sS, sS_pow), (sV, sV_d512)])
        sx_powsp = sxps.n

        wm_used = [0, 0]
        for sl in range(6):
            for kc in range(8):
                wh = (sl * 8 + kc) % 2
                b.dma("scalar", tmp[:, wh * 512:(wh + 1) * 512],
                      d_wmod[kc * 128:(kc + 1) * 128, sl * 512:(sl + 1) * 512],
                      swm[wh], waits=[(sT, wm_used[wh])])
                t(lambda e, kc=kc, wh=wh: e.matmul(
                    pmo[0:1, :], siluc[:, kc:kc + 1], tmp[:, wh * 512:(wh + 1) * 512],
                    start=(kc == 0), stop=(kc == 7)),
                  waits=[(swm[wh], swm[wh].n), (sS, sS_silu)] +
                        ([(sS, sS.n)] if kc == 0 else []),
                  incs=[(sT, 1)])
                wm_used[wh] = sT.n
            s(lambda e, sl=sl: e.activation(rows[0:1, 0:512], pmo[0:1, :], AF.Identity),
              waits=[(sT, sT.n)] + ([(sxb, sxb.n)] if sl > 0 else []))
            b.dma("sync",
                  d_mgi[sl * 512:(sl + 1) * 512].rearrange("t one -> one t"),
                  rows[0:1, 0:512], sxb, waits=[(sS, sS.n)])
        sx_modb = sxb.n
        b.op("gpsimd",
             lambda e: e.collective_compute(
                 "AllGather", ALU.bypass,
                 replica_groups=[[0, 1], [2, 3], [4, 5], [6, 7]],
                 ins=[d_mgi], outs=[d_mgo]),
             waits=[(sxb, sx_modb), (sG, sG.n)], incs=[(scc, 1)], is_dma=True)
        mod_names = ["sa", "ca1", "ga", "sb_", "cb1", "gb"]
        for j, nm in enumerate(mod_names):
            b.dma("sync", vec[nm],
                  d_mgo[j * 1024:(j + 1) * 1024].rearrange("(b p) one -> p (b one)", p=128),
                  sxb, waits=[(scc, 1)], slow=True)
        sx_mod = sxb.n
        v(lambda e: e.tensor_scalar_add(vec["ca1"], vec["ca1"], 1.0),
          waits=[(sxb, sx_mod)])
        v(lambda e: e.tensor_scalar_add(vec["cb1"], vec["cb1"], 1.0))
        for nm in ("ga", "gb"):
            v(lambda e, nm=nm: e.tensor_scalar_max(tmp[:, 0:NB], vec[nm], 0.25))
            v(lambda e, nm=nm: e.tensor_scalar_min(tmp[:, NB:2 * NB], vec[nm], -0.25))
            v(lambda e, nm=nm: e.tensor_single_scalar(tmp[:, 2 * NB:3 * NB], vec[nm],
                                                      0.0, ALU.is_ge))
            v(lambda e: e.tensor_sub(tmp[:, 3 * NB:4 * NB], tmp[:, 0:NB], tmp[:, NB:2 * NB]))
            v(lambda e: e.tensor_mul(tmp[:, 2 * NB:3 * NB], tmp[:, 2 * NB:3 * NB],
                                     tmp[:, 3 * NB:4 * NB]))
            v(lambda e, nm=nm: e.tensor_add(vec[nm], tmp[:, NB:2 * NB],
                                            tmp[:, 2 * NB:3 * NB]))
        v(lambda e: e.tensor_scalar_mul(vec["gah"], vec["ga"], 0.5))
        v(lambda e: e.tensor_mul(vec["gabo"], vec["ga"], vec["gabo"]))
        v(lambda e: e.tensor_mul(vec["gbbp"], vec["gb"], bpjv[:]))
        sV_mod = sV.n

        x_cols = lambda j: d_x[:, j * T: j * T + T + 1].rearrange(
            "(b p) t -> p b t", p=128)
        x_cols_b = lambda j: d_x[:, j * T + 1: (j + 1) * T + 1].rearrange(
            "(b p) t -> p b t", p=128)

        def ln_stats(src_bfA, src_bfB):
            for bb in range(NB):
                t(lambda e, bb=bb: e.matmul(pst[0:1, :], ones128b[:], src_bfA[:, bb, :],
                                            start=(bb == 0), stop=(bb == 7)),
                  waits=[(sS, sS.n), (sV, sV.n), (sgc, sgc_const)] if bb == 0 else [],
                  incs=[])
            for bb in range(NB):
                t(lambda e, bb=bb: e.matmul(pmo[0:1, :], ones128b[:], src_bfB[:, bb, :],
                                            start=(bb == 0), stop=(bb == 7)),
                  incs=[(sT, 1)] if bb == 7 else [])
            snap = sT.n
            v(lambda e: e.tensor_scalar_mul(rows[0:1, 1024:1536], pst[0:1, :], 1.0 / D),
              waits=[(sT, snap), (sxr, sxr.n)])
            v(lambda e: e.tensor_scalar_mul(rows[0:1, 1536:2048], pmo[0:1, :], 1.0 / D))
            v(lambda e: e.tensor_mul(rows[0:1, 0:512], rows[0:1, 1024:1536],
                                     rows[0:1, 1024:1536]))
            v(lambda e: e.tensor_sub(rows[0:1, 1536:2048], rows[0:1, 1536:2048],
                                     rows[0:1, 0:512]))
            snapV = sV.n
            s(lambda e: e.activation(rows[0:1, 0:512], rows[0:1, 1536:2048], AF.Ln,
                                     bias=misc[0:1, 2:3]), waits=[(sV, snapV)])
            s(lambda e: e.activation(rows[0:1, 0:512], rows[0:1, 0:512], AF.Exp,
                                     scale=-0.5))
            snapS = sS.n
            v(lambda e: e.tensor_mul(rows[0:1, 512:1024], rows[0:1, 1024:1536],
                                     rows[0:1, 0:512]),
              waits=[(sS, snapS)])
            return sV.n

        def bcast_rows(cA, cB, waitV):
            t(lambda e, cA=cA: e.matmul(pbc[:, 0:512], onesK1[:],
                                        rows[0:1, cA:cA + 512],
                                        start=True, stop=True),
              waits=[(sV, waitV), (sV, pbc_free[0])], incs=[])
            t(lambda e, cB=cB: e.matmul(pbc[:, 512:1024], onesK1[:],
                                        rows[0:1, cB:cB + 512],
                                        start=True, stop=True), incs=[(sT, 1)])
            return sT.n

        pbc_free = [0]

        def mm_group(dst_bank, wsrc, kslices, moving_blocks, wait_extra=()):
            for kc in range(8):
                wts = list(wait_extra) if kc == 0 else []
                t(lambda e, kc=kc: e.matmul(
                    pmm[:, dst_bank * 512:(dst_bank + 1) * 512],
                    wsrc[:, kslices + kc * 128: kslices + (kc + 1) * 128],
                    moving_blocks[:, kc, :],
                    start=(kc == 0), stop=(kc == 7)),
                  waits=wts, incs=[(sT, 1)] if kc == 7 else [])

        wl_snap = {i: (swS[i], 0) for i in range(6)}

        def wload(proj_idx, m, slot, wait_snap):
            b.dma("scalar", wch[:, slot * 1024:(slot + 1) * 1024],
                  d_wbf[proj_idx, m], swS[slot],
                  waits=[sg_w[("g", "r", "k", "v", "o")[proj_idx]],
                         (sT, wait_snap)])
            wl_snap[slot] = (swS[slot], swS[slot].n)

        wstream = []
        for _ in range(NT):
            wstream += [(2, c) for c in range(NB)] + [(3, c) for c in range(NB)]
        for _ in range(NT):
            wstream += ([(0, c) for c in range(NB)] + [(1, c) for c in range(NB)]
                        + [(4, c) for c in range(NB)])
        wpos = [0]

        def wslot():
            return wpos[0] % 6

        def wadvance():
            p = wpos[0]
            if p + 6 < len(wstream):
                pr, c = wstream[p + 6]
                wload(pr, c, (p + 6) % 6, sT.n)
            wpos[0] += 1

        b.dma("sync", xbuf[0][:], x_cols(0), sxx)
        sx_x0 = sxx.n
        s(lambda e: e.activation(hm[0][0][:, :, 0:1], xbuf[0][:, :, 0:1], AF.Copy),
          waits=[(sxx, sx_x0)])
        s(lambda e: e.activation(hm[0][1][:, :, 0:1], xbuf[0][:, :, 0:1], AF.Square))
        sS_b0 = sS.n
        for bb in range(NB):
            t(lambda e, bb=bb: e.matmul(pst[0:1, 0:1], ones128b[:], hm[0][0][:, bb, 0:1],
                                        start=(bb == 0), stop=(bb == 7)),
              waits=[(sS, sS_b0), (sgc, sgc_const)] if bb == 0 else [], incs=[])
        for bb in range(NB):
            t(lambda e, bb=bb: e.matmul(pmo[0:1, 0:1], ones128b[:], hm[0][1][:, bb, 0:1],
                                        start=(bb == 0), stop=(bb == 7)),
              incs=[(sT, 1)] if bb == 7 else [])
        sT_b0 = sT.n
        v(lambda e: e.tensor_scalar_mul(rows[0:1, 1024:1025], pst[0:1, 0:1], 1.0 / D),
          waits=[(sT, sT_b0)])
        v(lambda e: e.tensor_scalar_mul(rows[0:1, 1536:1537], pmo[0:1, 0:1], 1.0 / D))
        v(lambda e: e.tensor_mul(rows[0:1, 0:1], rows[0:1, 1024:1025],
                                 rows[0:1, 1024:1025]))
        v(lambda e: e.tensor_sub(rows[0:1, 1536:1537], rows[0:1, 1536:1537],
                                 rows[0:1, 0:1]))
        sV_b1 = sV.n
        s(lambda e: e.activation(rows[0:1, 0:1], rows[0:1, 1536:1537], AF.Ln,
                                 bias=misc[0:1, 2:3]), waits=[(sV, sV_b1)])
        s(lambda e: e.activation(rows[0:1, 0:1], rows[0:1, 0:1], AF.Exp, scale=-0.5))
        sS_b1 = sS.n
        v(lambda e: e.tensor_mul(rows[0:1, 512:513], rows[0:1, 1024:1025],
                                 rows[0:1, 0:1]),
          waits=[(sS, sS_b1)])
        sV_b2 = sV.n
        t(lambda e: e.matmul(pbc[:, 0:1], onesK1[:], rows[0:1, 0:1], start=True,
                             stop=True), waits=[(sV, sV_b2)], incs=[])
        t(lambda e: e.matmul(pbc[:, 512:513], onesK1[:], rows[0:1, 512:513], start=True,
                             stop=True), incs=[(sT, 1)])
        sT_b2 = sT.n
        v(lambda e: e.tensor_tensor(vec["hlast"], xbuf[0][:, :, 0],
                                    pbc[:, 0:1].broadcast_to([128, NB]), ALU.mult),
          waits=[(sT, sT_b2)])
        v(lambda e: e.tensor_tensor(vec["hlast"], vec["hlast"],
                                    pbc[:, 512:513].broadcast_to([128, NB]),
                                    ALU.subtract))
        v(lambda e: e.tensor_mul(vec["hlast"], vec["hlast"], vec["ca1"]))
        v(lambda e: e.tensor_add(vec["hlast"], vec["hlast"], vec["sa"]))
        pbc_free[0] = sV.n

        stats_psum_free = [sV.n]
        x2b_free = [0]
        xsum_snap = [0, 0]
        h_snap = [0, 0]
        hm_free = [0, 0]
        hsp_snap = [0]
        uadd_snap = [0]
        kv_snap = [0]
        sV_hmk = [0]
        sV_hmv = [0]
        sc_spilled = [0, 0]
        sx_x = [sx_x0, 0]

        def a1_sq(k, sq_waits=()):
            s(lambda e, k=k: e.activation(x2b[:], xbuf[k % 2][:, :, 1:T + 1],
                                          AF.Square),
              waits=list(sq_waits) + [(sT, x2b_free[0])])

        def a1_stats_rest(k):
            par = k % 2
            xcur = xbuf[par]
            sS_sq = sS.n
            for bb in range(NB):
                t(lambda e, bb=bb: e.matmul(pst[0:1, :], mc(0), xcur[:, bb, 1:T + 1],
                                            start=(bb == 0), stop=(bb == 7)),
                  waits=[(sS, sS_sq), (sV, stats_psum_free[0])] if bb == 0 else [],
                  incs=[])
            for bb in range(NB):
                t(lambda e, bb=bb: e.matmul(pmo[0:1, :], mc(0), x2b[:, bb, :],
                                            start=(bb == 0), stop=(bb == 7)),
                  incs=[(sT, 1)] if bb == 7 else [])
            snap = sT.n
            x2b_free[0] = snap
            xsum_snap[par] = snap
            v(lambda e: e.tensor_scalar_mul(rows[0:1, 1024:1536], pst[0:1, :], 1.0 / D),
              waits=[(sT, snap)])
            v(lambda e: e.tensor_scalar_mul(rows[0:1, 1536:2048], pmo[0:1, :], 1.0 / D))
            stats_psum_free[0] = sV.n
            v(lambda e: e.tensor_mul(rows[0:1, 0:512], rows[0:1, 1024:1536],
                                     rows[0:1, 1024:1536]))
            v(lambda e: e.tensor_sub(rows[0:1, 1536:2048], rows[0:1, 1536:2048],
                                     rows[0:1, 0:512]))
            snapV = sV.n
            s(lambda e: e.activation(rows[0:1, 0:512], rows[0:1, 1536:2048], AF.Ln,
                                     bias=misc[0:1, 2:3]), waits=[(sV, snapV)])
            s(lambda e: e.activation(rows[0:1, 0:512], rows[0:1, 0:512], AF.Exp,
                                     scale=-0.5))
            snapS = sS.n
            v(lambda e: e.tensor_mul(rows[0:1, 512:1024], rows[0:1, 1024:1536],
                                     rows[0:1, 0:512]),
              waits=[(sS, snapS)])
            return bcast_rows(0, 512, sV.n)

        def a1_happly(k, bsn):
            par = k % 2
            xcur = xbuf[par]
            v(lambda e: e.tensor_tensor(hb[:, :, 1:T + 1], xcur[:, :, 1:T + 1],
                                        bc3(pbc[:, 0:512]), ALU.mult),
              waits=[(sT, bsn), (sxhs, hsp_snap[0]),
                     ((sxx, sxh)[par], sx_x[par])])
            v(lambda e: e.tensor_tensor(hb[:, :, 1:T + 1], hb[:, :, 1:T + 1],
                                        bc3(pbc[:, 512:1024]), ALU.subtract))
            pbc_free[0] = sV.n
            for bb in range(NB):
                v(lambda e, bb=bb: e.tensor_scalar(hb[:, bb, 1:T + 1],
                                                   hb[:, bb, 1:T + 1],
                                                   vb("ca1", bb), vb("sa", bb),
                                                   ALU.mult, ALU.add))
            v(lambda e: e.tensor_copy(hb[:, :, 0:1], vec["hlast"].unsqueeze(2)))
            v(lambda e: e.tensor_copy(vec["hlast"].unsqueeze(2), hb[:, :, T:T + 1]))
            h_snap[par] = sV.n
            b.dma("sync", d_hsp[k].rearrange("(b p) t -> p b t", p=128), hb[:],
                  sxhs, waits=[(sV, sV.n)])
            hsp_snap[0] = sxhs.n

        def a1_hmix(k):
            par = k % 2
            db = scb[par]
            v(lambda e, db=db: e.tensor_tensor(db[:], hb[:, :, 1:T + 1],
                                               hb[:, :, 0:T], ALU.subtract),
              waits=[(sxc[par], sc_spilled[par])])
            for bb in range(NB):
                v(lambda e, bb=bb, par=par, db=db: e.scalar_tensor_tensor(
                    hm[par][0][:, bb, :], db[:, bb, :], vb("mk", bb),
                    hb[:, bb, 0:T], ALU.mult, ALU.add),
                  waits=[(sT, hm_free[par])] if bb == 0 else [])
            sV_hmk[0] = sV.n
            for bb in range(NB):
                v(lambda e, bb=bb, par=par, db=db: e.scalar_tensor_tensor(
                    hm[par][1][:, bb, :], db[:, bb, :], vb("mv", bb),
                    hb[:, bb, 0:T], ALU.mult, ALU.add))
            sV_hmv[0] = sV.n

        for p0 in range(6):
            pr, c = wstream[p0]
            wload(pr, c, p0, 0)

        qmax_it_issue = [0]
        qmax_it_reduce = [0]
        qmax_vfree = [0, 0]
        qmax_snap = {}

        def qmax_issue():
            it = qmax_it_issue[0]
            if it >= 64:
                return
            assert it - qmax_it_reduce[0] <= 1
            qmax_it_issue[0] += 1
            if it < 32:
                src = d_wfc[(it // 4) * 128:(it // 4 + 1) * 128,
                            (it % 4) * 1024:(it % 4 + 1) * 1024]
            else:
                i2 = it - 32
                src = d_wpj[i2 * 128:(i2 + 1) * 128, :]
            half = it % 2
            qsem = (sgq, sgq2)[half]
            b.dma("gpsimd", qst[:, half * 1024:(half + 1) * 1024],
                  src, qsem, waits=[(sV, qmax_vfree[half])])
            qmax_snap[it] = (qsem, qsem.n)

        def qmax_reduce():
            it = qmax_it_reduce[0]
            if it >= 64 or qmax_it_issue[0] == it:
                return
            qmax_it_reduce[0] += 1
            acc = q_["mxfc"] if it < 32 else q_["mxpj"]
            half = it % 2
            v(lambda e, half=half: e.reduce_max(q_["t2"][:],
                                                qst[:, half * 1024:(half + 1) * 1024],
                                                AX.X, apply_absolute_value=True),
              waits=[qmax_snap[it]])
            v(lambda e, acc=acc: e.tensor_max(acc[:], acc[:], q_["t2"][:]))
            qmax_vfree[half] = sV.n

        a1_sq(0, sq_waits=[(sxx, sx_x0)])
        bsn0 = a1_stats_rest(0)
        a1_happly(0, bsn0)
        b.dma("sync", xbuf[1][:], x_cols(1), sxh, waits=[(sxps, sx_powsp)])
        sx_x[1] = sxh.n
        hm_free[0] = sT_b2
        a1_hmix(0)
        a1_sq(1, sq_waits=[(sxh, sx_x[1])])
        kevac = {m: 0 for m in range(NB)}
        vevac = {m: 0 for m in range(NB)}
        bsn_next = [0]

        for j in range(NT):
            p = j % 2
            hmA_j, hmB_j = hm[p]
            sV_hmk_j, sV_hmv_j = sV_hmk[0], sV_hmv[0]
            if j + 2 < NT:
                b.dma("sync", xbuf[p][:], x_cols(j + 2), (sxx, sxh)[p],
                      waits=[(sV, h_snap[p]), (sT, xsum_snap[p])])
                sx_x[p] = (sxx, sxh)[p].n
            if j + 1 < NT:
                bsn_next[0] = a1_stats_rest(j + 1)
            if j + 2 < NT:
                a1_sq(j + 2, sq_waits=[((sxx, sxh)[p], sx_x[p])])
            qmax_issue()
            qmax_issue()
            for m in range(NB):
                slot = wslot()
                mm_group(m % 4, wch, slot * 1024, hmA_j,
                         wait_extra=[wl_snap[slot], (sV, sV_hmk_j),
                                     (sS, vevac[4 + m % 4] if m < 4
                                      else kevac[m - 4])])
                wadvance()
                snap = sT.n
                s(lambda e, m=m: e.activation(ka[:, m, :],
                                              pmm[:, (m % 4) * 512:(m % 4 + 1) * 512],
                                              AF.Identity, bias=vb("bk", m)),
                  waits=[(sT, snap)] + ([(sV, kv_snap[0])] if m == 0 else []))
                kevac[m] = sS.n
            for m in range(NB):
                slot = wslot()
                mm_group(m % 4, wch, slot * 1024, hmB_j,
                         wait_extra=[wl_snap[slot], (sV, sV_hmv_j),
                                     (sS, kevac[NB - 1] if m < 4 else vevac[m - 4])])
                wadvance()
                snap = sT.n
                s(lambda e, m=m: e.activation(va[:, m, :],
                                              pmm[:, (m % 4) * 512:(m % 4 + 1) * 512],
                                              AF.Identity, bias=vb("bv", m)),
                  waits=[(sT, snap)] + ([(sV, kv_snap[0])] if m == 0 else []))
                vevac[m] = sS.n
            hm_free[p] = sT.n
            if j + 1 < NT:
                a1_happly(j + 1, bsn_next[0])
                a1_hmix(j + 1)
            qmax_reduce()
            qmax_reduce()
            qmax_issue()
            qmax_issue()
            v(lambda e, p=p: e.tensor_tensor(scb[1 - p][:], ka[:], va[:],
                                             ALU.mult),
              waits=[(sS, vevac[NB - 1])] +
                    ([(sxc[1 - p], sc_spilled[1 - p])] if j + 1 >= NT else []))
            kv_snap[0] = sV.n
            for bb in range(NB):
                v(lambda e, bb=bb, p=p: e.tensor_tensor_scan(
                    scb[p][:, bb, :], vb("dec", bb).broadcast_to([128, T]),
                    scb[1 - p][:, bb, :], vb("carry", bb), ALU.mult, ALU.add))
            v(lambda e, p=p: e.tensor_copy(vec["carry"], scb[p][:, :, T - 1]))
            for bb in range(NB):
                v(lambda e, bb=bb, p=p: e.scalar_tensor_tensor(
                    scb[p][:, bb, :], scb[1 - p][:, bb, :], vb("uv", bb),
                    scb[p][:, bb, :], ALU.mult, ALU.add))
            uadd_snap[0] = sV.n
            b.dma("sync", d_scsp[j].rearrange("(b p) t -> p b t", p=128), scb[p][:],
                  sxc[p], waits=[(sV, uadd_snap[0])])
            sc_spilled[p] = sxc[p].n
            qmax_reduce()
            qmax_reduce()

        b.dma("sync", d_ccin.rearrange("(b p) one -> p (b one)", p=128),
              vec["carry"], sxb, waits=[(sV, sV.n)], slow=True)
        sx_cc = sxb.n
        b.op("gpsimd",
             lambda e: e.collective_compute(
                 "AllGather", ALU.bypass,
                 replica_groups=[[0, 1], [2, 3], [4, 5], [6, 7]],
                 ins=[d_ccin], outs=[d_ccout]),
             waits=[(sxb, sx_cc), (sG, sG.n)], incs=[(scc, 1)], is_dma=True)
        b.dma("sync", vec["Din"],
              d_ccout[0:D].rearrange("(b p) one -> p (b one)", p=128), sxb,
              waits=[(scc, 2)], slow=True)
        sx_D = sxb.n
        v(lambda e: e.tensor_mul(vec["Din"], vec["Din"], vec["recvmask"]),
          waits=[(sxb, sx_D)])
        sV_D = sV.n

        b.dma("sync", pow_, d_powsp.rearrange("(b p) t -> p b t", p=128), sxps,
              waits=[(sV, h_snap[1]), (sT, xsum_snap[1]), (sxps, sx_powsp)])
        sx_powrl = sxps.n

        hmA, hmB = hm[0]

        sx_h = [0]
        sx_sc = [0, 0]
        sx_x2free = [0]
        sG_rw_prev = [0]
        b.dma("sync", hb[:], d_hsp[0].rearrange("(b p) t -> p b t", p=128), sxh,
              waits=[(sV, sV.n), (sxhs, sxhs.n)])
        sx_h[0] = sxh.n
        b.dma("sync", scb[0][:], d_scsp[0].rearrange("(b p) t -> p b t", p=128),
              sxc[0], waits=[(sxc[0], sxc[0].n), (sV, sV.n)])
        sx_sc[0] = sxc[0].n
        b.dma("sync", xbuf[0][:, :, 1:T + 1], x_cols_b(0), sxx,
              waits=[(sV, h_snap[0]), (sT, xsum_snap[0])])
        sx_xA2 = [sxx.n]

        for j in range(NT):
            p = j % 2
            sx_hj, sx_scj, sx_xj = sx_h[0], sx_sc[p], sx_xA2[0]
            v(lambda e, p=p: e.tensor_tensor(x2b[:], pow_[:],
                vec["Din"].unsqueeze(2).broadcast_to([128, NB, T]), ALU.mult),
              waits=[(sxc[p], sx_scj), (sV, sV_D), (sx2, sx_x2free[0]),
                     (sG, sG_rw_prev[0])] +
                    ([(sxps, sx_powrl)] if j == 0 else []))
            v(lambda e, p=p: e.tensor_tensor(scb[p][:], scb[p][:], x2b[:],
                                             ALU.add))
            sV_corr = sV.n
            v(lambda e: e.tensor_mul(vec["Din"], vec["Din"], vec["d512"]))
            sV_D = sV.n
            v(lambda e: e.tensor_tensor(va[:], hb[:, :, 1:T + 1], hb[:, :, 0:T],
                                        ALU.subtract),
              waits=[(sxh, sx_hj)])
            v(lambda e: e.tensor_tensor(x2b[:], va[:],
                vec["mg"].unsqueeze(2).broadcast_to([128, NB, T]), ALU.mult))
            v(lambda e, p=p: e.tensor_tensor(hmA[:], x2b[:], hb[:, :, 0:T],
                                             ALU.add), waits=[(sT, sT.n)])
            sV_hmg = sV.n
            qmax_issue()
            qmax_issue()
            gevac = {}
            for m in range(NB):
                slot = wslot()
                mm_group(m % 4, wch, slot * 1024, hmA,
                         wait_extra=[wl_snap[slot], (sV, sV_hmg),
                                     (sS, sS.n if m < 4 else gevac[m - 4])])
                wadvance()
                snap = sT.n
                s(lambda e, m=m: e.activation(ka[:, m, :],
                                              pmm[:, (m % 4) * 512:(m % 4 + 1) * 512],
                                              AF.Tanh, scale=0.5, bias=vb("bg2", m)),
                  waits=[(sT, snap)])
                gevac[m] = sS.n
            v(lambda e, p=p: e.tensor_tensor(x2b[:], va[:],
                vec["mr"].unsqueeze(2).broadcast_to([128, NB, T]), ALU.mult))
            v(lambda e, p=p: e.tensor_tensor(hmB[:], x2b[:], hb[:, :, 0:T],
                                             ALU.add), waits=[(sT, sT.n)])
            sV_hmr = sV.n
            revac = {}
            for m in range(NB):
                slot = wslot()
                mm_group(m % 4, wch, slot * 1024, hmB,
                         wait_extra=[wl_snap[slot], (sV, sV_hmr),
                                     (sS, gevac[NB - 1] if m < 4 else revac[m - 4])])
                wadvance()
                snap = sT.n
                s(lambda e, m=m: e.activation(va[:, m, :],
                                              pmm[:, (m % 4) * 512:(m % 4 + 1) * 512],
                                              AF.Identity, bias=vb("br", m)),
                  waits=[(sT, snap), (sV, sV_hmr)])
                revac[m] = sS.n
            if j + 1 < NT:
                b.dma("sync", hb[:], d_hsp[j + 1].rearrange("(b p) t -> p b t", p=128),
                      sxh, waits=[(sV, sV_hmr)])
                sx_h[0] = sxh.n
            g(lambda e, p=p: e.tensor_tensor(scb[p][:], va[:], scb[p][:], ALU.mult),
              waits=[(sS, revac[NB - 1]), (sV, sV_corr)])
            sG_rw = sG.n
            sG_rw_prev[0] = sG_rw
            s(lambda e, p=p: e.activation(hmA[:], scb[p][:], AF.Copy),
              waits=[(sG, sG_rw), (sT, sT.n)])
            s(lambda e, p=p: e.activation(hmB[:], scb[p][:], AF.Square))
            sS_rwc = sS.n
            for bb in range(NB):
                t(lambda e, bb=bb: e.matmul(pst[0:16, :], gsel[:, bb * 16:(bb + 1) * 16],
                                            hmA[:, bb, :], start=(bb == 0),
                                            stop=(bb == 7)),
                  waits=[(sS, sS_rwc), (sV, sV.n), (sgc, sgc_const)] if bb == 0 else [],
                  incs=[])
            for bb in range(NB):
                t(lambda e, bb=bb: e.matmul(pmo[0:16, :], gsel[:, bb * 16:(bb + 1) * 16],
                                            hmB[:, bb, :], start=(bb == 0),
                                            stop=(bb == 7)),
                  incs=[(sT, 1)] if bb == 7 else [])
            sT_gn = sT.n
            v(lambda e: e.tensor_scalar_mul(rows[0:16, 1024:1536], pst[0:16, :],
                                            1.0 / HD),
              waits=[(sT, sT_gn), (sxr, sxr.n)])
            v(lambda e: e.tensor_scalar_mul(rows[0:16, 1536:2048], pmo[0:16, :],
                                            1.0 / HD))
            v(lambda e: e.tensor_mul(rows[0:16, 0:512], rows[0:16, 1024:1536],
                                     rows[0:16, 1024:1536]))
            v(lambda e: e.tensor_sub(rows[0:16, 1536:2048], rows[0:16, 1536:2048],
                                     rows[0:16, 0:512]))
            sV_gn1 = sV.n
            s(lambda e: e.activation(rows[0:16, 0:512], rows[0:16, 1536:2048], AF.Ln,
                                     bias=misc[0:16, 2:3]), waits=[(sV, sV_gn1)])
            s(lambda e: e.activation(rows[0:16, 0:512], rows[0:16, 0:512], AF.Exp,
                                     scale=-0.5))
            sS_gn = sS.n
            v(lambda e: e.tensor_mul(rows[0:16, 512:1024], rows[0:16, 1024:1536],
                                     rows[0:16, 0:512]),
              waits=[(sS, sS_gn)])
            sV_gn2 = sV.n
            s(lambda e: e.activation(ka[:], ka[:], AF.Identity, bias=mc(0)),
              waits=[(sT, sT.n)])
            sS_ka1 = sS.n
            gnchain = {}
            for bb in range(NB):
                rot = bb % 3
                tgt = (pbc, pmm, pmm)[rot]
                o0 = (0, 0, 2 * 512)[rot]
                o1 = o0 + 512
                t(lambda e, bb=bb, tgt=tgt, o0=o0: e.matmul(
                    tgt[:, o0:o0 + 512], gselT[:, bb * 128:(bb + 1) * 128],
                    rows[0:16, 0:512], start=True, stop=True),
                  waits=[(sV, sV_gn2 if bb < 3 else gnchain[bb - 3])], incs=[])
                t(lambda e, bb=bb, tgt=tgt, o1=o1: e.matmul(
                    tgt[:, o1:o1 + 512], gselT[:, bb * 128:(bb + 1) * 128],
                    rows[0:16, 512:1024], start=True, stop=True), incs=[(sT, 1)])
                snapT = sT.n
                v(lambda e, bb=bb, p=p, tgt=tgt, o0=o0: e.tensor_tensor(
                    tmp[:, 0:T], scb[p][:, bb, :], tgt[:, o0:o0 + 512], ALU.mult),
                  waits=[(sT, snapT), (sG, sG_rw)] + ([(sS, sS_ka1)] if bb == 0 else []))
                v(lambda e, tgt=tgt, o1=o1: e.tensor_tensor(
                    tmp[:, T:2 * T], tmp[:, 0:T], tgt[:, o1:o1 + 512], ALU.subtract))
                v(lambda e, bb=bb: e.tensor_tensor(
                    hmB[:, bb, :], tmp[:, T:2 * T], ka[:, bb, :], ALU.mult))
                gnchain[bb] = sV.n
            sV_apre = sV.n
            qmax_reduce()
            qmax_reduce()
            qmax_issue()
            qmax_issue()
            woevac = {}
            for m in range(NB):
                slot = wslot()
                mm_group(m % 4, wch, slot * 1024, hmB,
                         wait_extra=[wl_snap[slot], (sV, sV_apre),
                                     (sS, sS_rwc if m < 4 else woevac[m - 4])])
                wadvance()
                snap = sT.n
                s(lambda e, m=m: e.activation(va[:, m, :],
                                              pmm[:, (m % 4) * 512:(m % 4 + 1) * 512],
                                              AF.Identity, scale=vb("gah", m),
                                              bias=vb("gabo", m)),
                  waits=[(sT, snap), (sV, gnchain[NB - 1])])
                woevac[m] = sS.n
            v(lambda e: e.tensor_tensor(x2b[:], xbuf[0][:, :, 1:T + 1], va[:],
                                        ALU.add),
              waits=[(sS, woevac[NB - 1]), (sxx, sx_xj)])
            sV_x2 = sV.n
            if j + 1 < NT:
                b.dma("sync", scb[1 - p][:],
                      d_scsp[j + 1].rearrange("(b p) t -> p b t", p=128), sxc[1 - p],
                      waits=[(sV, sV.n), (sxc[1 - p], sxc[1 - p].n)])
                sx_sc[1 - p] = sxc[1 - p].n
                b.dma("sync", xbuf[0][:, :, 1:T + 1], x_cols_b(j + 1), sxx,
                      waits=[(sV, sV_x2)])
                sx_xA2[0] = sxx.n
            b.dma("sync", d_x2sp[:, j * T:(j + 1) * T].rearrange("(b p) t -> p b t", p=128),
                  x2b[:], sx2, waits=[(sV, sV_x2)])
            sx_x2free[0] = sx2.n
            s(lambda e: e.activation(hmA[:], x2b[:], AF.Copy),
              waits=[(sV, sV_x2), (sT, sT.n)])
            s(lambda e: e.activation(hmB[:], x2b[:], AF.Square))
            sS_x2c = sS.n
            _ = ln_stats(hmA, hmB)
            b.dma("sync",
                  d_bounce[j * 512:(j + 1) * 512].rearrange("t one -> one t"),
                  rows[0:1, 0:512], sxr, waits=[(sV, sV.n)])
            b.dma("sync",
                  d_bounce[(8 + j) * 512:(9 + j) * 512].rearrange("t one -> one t"),
                  rows[0:1, 512:1024], sxr)
            qmax_reduce()
            qmax_reduce()

        for _ in range(128):
            qmax_reduce()
            qmax_issue()
        qmax_reduce()
        b.dma("sync", d_bounce[7168:7296], q_["mxfc"][:], sxb, waits=[(sV, sV.n)])
        b.dma("sync", d_bounce[7296:7424], q_["mxpj"][:], sxb)
        sx_mxw = sxb.n
        b.dma("sync", rows[0:1, 0:128],
              d_bounce[7168:7296].rearrange("p one -> one p"), sxb,
              waits=[(sxb, sx_mxw), (sV, sV.n), (sxr, sxr.n)], slow=True)
        b.dma("sync", rows[0:1, 128:256],
              d_bounce[7296:7424].rearrange("p one -> one p"), sxb, slow=True)
        sx_mx = sxb.n
        v(lambda e: e.reduce_max(rows[0:1, 300:301], rows[0:1, 0:128], AX.X),
          waits=[(sxb, sx_mx)])
        v(lambda e: e.reduce_max(rows[0:1, 301:302], rows[0:1, 128:256], AX.X))
        v(lambda e: e.tensor_scalar_mul(rows[0:1, 302:303], rows[0:1, 300:301], 1.0 / 127.0))
        v(lambda e: e.tensor_scalar_mul(rows[0:1, 303:304], rows[0:1, 301:302], 1.0 / 127.0))
        v(lambda e: e.reciprocal(rows[0:1, 304:305], rows[0:1, 300:301]))
        v(lambda e: e.reciprocal(rows[0:1, 305:306], rows[0:1, 301:302]))
        v(lambda e: e.tensor_scalar_mul(rows[0:1, 304:305], rows[0:1, 304:305], 127.0))
        v(lambda e: e.tensor_scalar_mul(rows[0:1, 305:306], rows[0:1, 305:306], 127.0))
        sV_q = sV.n
        t(lambda e: e.matmul(pmo[:, 0:1], onesK1[:], rows[0:1, 302:303], start=True,
                             stop=True), waits=[(sV, sV_q)], incs=[])
        t(lambda e: e.matmul(pmo[:, 1:2], onesK1[:], rows[0:1, 304:305], start=True,
                             stop=True), incs=[])
        t(lambda e: e.matmul(pmo[:, 2:3], onesK1[:], rows[0:1, 303:304], start=True,
                             stop=True), incs=[])
        t(lambda e: e.matmul(pmo[:, 3:4], onesK1[:], rows[0:1, 305:306], start=True,
                             stop=True), incs=[(sT, 1)])
        sT_q = sT.n
        s(lambda e: e.activation(q_["sfc"][:], pmo[:, 0:1], AF.Identity),
          waits=[(sT, sT_q)])
        s(lambda e: e.activation(q_["invfc"][:], pmo[:, 1:2], AF.Identity))
        s(lambda e: e.activation(q_["spj"][:], pmo[:, 2:3], AF.Identity))
        s(lambda e: e.activation(q_["invpj"][:], pmo[:, 3:4], AF.Identity))
        sS_q = sS.n
        b.op("sync", None,
             waits=[(c, c.n) for c in (sxb, sxx, sxh, sxhs, sxc[0], sxc[1],
                                       sx2, sxr, sxps)], incs=[(bar, 1)])
        b.op("gpsimd", None,
             waits=[(sG, sG.n)] + [(c, c.n) for c in sgw + [sgc, sgq, sgq2]],
             incs=[(bar, 1)])
        b.op("vector", None, waits=[(sV, sV.n)], incs=[(bar, 1)])
        b.op("scalar", None,
             waits=[(sS, sS_q)] + [(c, c.n) for c in swS + swm],
             incs=[(bar, 1)])
        b.op("tensor", None, waits=[(sT, sT.n)], incs=[(bar, 1)])

    es_b = ExitStack()
    with es_b:
        sbb = lambda n, sh, dt=F32: es_b.enter_context(nc.sbuf_tensor(n, sh, dt))
        wfce = sbb("wfce", [128, NB * FF], BF16)
        wpjc = sbb("wpjc", [128, 2 * NFB * 128], BF16)
        wps = sbb("wps", [128, 2 * 1024], BF16)
        h1 = sbb("h1", [128, NFB * 512], BF16)
        h2 = sbb("h2", [128, NB, T], BF16)
        x2t = sbb("x2t", [128, NB, T])
        outt = sbb("outt", [128, NB, T])
        qw = sbb("qw", [128, 2 * 1024])
        qt1 = sbb("qt1", [128, 1024])
        lafc = sbb("lafc", [16, D], BF16)
        lbpjT = sbb("lbpjT", [16, D], BF16)
        lbf_s = sbb("lbf_s", [16, 1024], BF16)
        lap_s = sbb("lap_s", [16, 2 * 128], BF16)

        bw = [(bar, 5)]
        first = {"sync": bw, "gpsimd": bw, "vector": bw, "scalar": bw, "tensor": bw}

        def v(fn, waits=(), incs=((sV, 1),)):
            w = list(waits) + first.pop("vector", [])
            b.op("vector", fn, waits=w, incs=incs)

        def s(fn, waits=(), incs=((sS, 1),)):
            w = list(waits) + first.pop("scalar", [])
            b.op("scalar", fn, waits=w, incs=incs)

        def t(fn, waits=(), incs=()):
            w = list(waits) + first.pop("tensor", [])
            b.op("tensor", fn, waits=w, incs=incs)

        def g(fn, waits=(), incs=((sG, 1),)):
            w = list(waits) + first.pop("gpsimd", [])
            b.op("gpsimd", fn, waits=w, incs=incs)

        def dma_q(eng, out_ap, in_ap, sem, waits=()):
            w = list(waits) + first.pop(eng, [])
            b.op(eng, lambda e, o=out_ap, i=in_ap: e.dma_start(out=o, in_=i),
                 waits=w, incs=[(sem, 16)], is_dma=True)

        dma_q("gpsimd", lafc[:], d_afc, sgc)
        dma_q("gpsimd", lbpjT[:], d_bpjT, sgc)
        sg_lora = sgc.n
        v(lambda e: e.tensor_scalar_mul(lafc[:], lafc[:], 2.0), waits=[(sgc, sg_lora)])
        v(lambda e: e.tensor_scalar_mul(lbpjT[:], lbpjT[:], 2.0))
        sV_lora = sV.n

        qunit_v = {}
        lbf_snap = [0]

        def quant_unit(kind, og, i, uidx):
            if kind == "fc":
                wsrc = d_wfc[i * 128:(i + 1) * 128, og * 1024:(og + 1) * 1024]
                stat = lafc[:, i * 128:(i + 1) * 128]
                mov0, mov1 = lbf_s[:, 0:512], lbf_s[:, 512:1024]
                dst = wfce[:, i * FF + og * 1024: i * FF + (og + 1) * 1024]
                sc, iv = q_["sfc"], q_["invfc"]
                wmov = [(sgl[2], lbf_snap[0])]
            else:
                wsrc = d_wpj[i * 128:(i + 1) * 128, 0:1024]
                half2 = i % 2
                dma_q("gpsimd", lap_s[:, half2 * 128:(half2 + 1) * 128],
                      d_apj[:, i * 128:(i + 1) * 128], sgl[half2],
                      waits=[(sT, qunit_t.get(uidx - 2, 0))])
                stat = lap_s[:, half2 * 128:(half2 + 1) * 128]
                mov0, mov1 = lbpjT[:, 0:512], lbpjT[:, 512:1024]
                dst = wps[:, half2 * 1024:(half2 + 1) * 1024]
                sc, iv = q_["spj"], q_["invpj"]
                wmov = [(sgl[half2], sgl[half2].n), (sV, sV_lora),
                        (sxp[half2], wps_free[half2])]
            half = uidx % 2
            dma_q("sync", qw[:, half * 1024:(half + 1) * 1024], wsrc, sxq[half],
                  waits=[(sV, qunit_v.get(uidx - 2, 0))])
            snapx = (sxq[half], sxq[half].n)
            t(lambda e, stat=stat, mov0=mov0: e.matmul(
                pbc[:, 0:512], stat, mov0, start=True, stop=True),
              waits=[(sV, qunit_v.get(uidx - 1, 0)), (sV, sV_lora)] + wmov, incs=[])
            t(lambda e, stat=stat, mov1=mov1: e.matmul(
                pbc[:, 512:1024], stat, mov1, start=True, stop=True),
              incs=[(sT, 1)])
            snapT = sT.n
            qunit_t[uidx] = snapT
            s(lambda e, half=half, iv=iv: e.activation(
                qt1[:], qw[:, half * 1024:(half + 1) * 1024], AF.Identity,
                scale=iv[:], bias=q_["Mpp"][:]),
              waits=[snapx, (sV, qunit_v.get(uidx - 1, 0))])
            snapS = sS.n
            v(lambda e, sc=sc: e.tensor_scalar(qt1[:], qt1[:], q_["Mpp"][:], sc[:],
                                               ALU.subtract, ALU.mult),
              waits=[(sS, snapS)])
            v(lambda e, dst=dst: e.tensor_tensor(dst, qt1[:], pbc[:, 0:1024], ALU.add),
              waits=[(sT, snapT)])
            qunit_v[uidx] = sV.n
            if kind == "pj":
                for mo in range(NB):
                    b.dma("sync", d_wpjbf[mo, :, i * 128:(i + 1) * 128],
                          dst[:, mo * 128:(mo + 1) * 128], sxp[half2],
                          waits=[(sV, qunit_v[uidx])] if mo == 0 else [])
                wps_free[half2] = sxp[half2].n

        qunit_t = {}
        wps_free = [0, 0]
        uidx = 0
        for og in range(4):
            dma_q("gpsimd", lbf_s[:], d_bfcT[:, og * 1024:(og + 1) * 1024], sgl[2],
                  waits=[(sT, qunit_t.get(uidx - 1, 0))])
            lbf_snap[0] = sgl[2].n
            for i in range(NB):
                quant_unit("fc", og, i, uidx)
                uidx += 1
        fc_ready = {og: qunit_v[(og + 1) * NB - 1] for og in range(4)}
        for i in range(NFB):
            quant_unit("pj", 0, i, uidx)
            uidx += 1
        pj_ready = (sxp[0].n, sxp[1].n)

        sx_x2 = [0]
        wpjc_used = [0, 0]
        dma_q("sync", x2t[:], d_x2sp[:, 0:T].rearrange("(b p) t -> p b t", p=128), sxx)
        sx_x2[0] = sxx.n
        outfree = [0]
        rowsnap = [0]
        for j in range(NT):
            sx_xj = sx_x2[0]
            dma_q("sync", rows[0:1, 0:512],
                  d_bounce[j * 512:(j + 1) * 512].rearrange("t one -> one t"),
                  sxr, waits=[(sT, rowsnap[0])])
            dma_q("sync", rows[0:1, 512:1024],
                  d_bounce[(8 + j) * 512:(9 + j) * 512].rearrange("t one -> one t"),
                  sxr)
            sx_rows = sxr.n
            t(lambda e: e.matmul(pmo[:, 0:512], onesK1[:], rows[0:1, 0:512],
                                 start=True, stop=True),
              waits=[(sxr, sx_rows)], incs=[(sT, 1)])
            snapA = sT.n
            v(lambda e: e.tensor_tensor(outt[:], x2t[:], bc3(pmo[:, 0:512]), ALU.mult),
              waits=[(sT, snapA), (sxx, sx_xj), (sout, outfree[0])])
            snapVA = sV.n
            t(lambda e: e.matmul(pmo[:, 0:512], onesK1[:], rows[0:1, 512:1024],
                                  start=True, stop=True),
              waits=[(sV, snapVA)], incs=[(sT, 1)])
            snapB = sT.n
            rowsnap[0] = sT.n
            v(lambda e: e.tensor_tensor(outt[:], outt[:], bc3(pmo[:, 0:512]),
                                        ALU.subtract), waits=[(sT, snapB)])
            for bb in range(NB):
                v(lambda e, bb=bb: e.tensor_scalar(h2[:, bb, :], outt[:, bb, :],
                                                   vb("cb1", bb), vb("sb_", bb),
                                                   ALU.mult, ALU.add))
            sV_h2 = sV.n
            fcev = {}
            for m in range(NFB):
                og = m // 8
                wts = [(sV, sV_h2), (sV, fc_ready[og])]
                if m >= 4:
                    wts.append((sS, fcev[m - 4]))
                else:
                    wts.append((sS, sS.n))
                for kc in range(NB):
                    t(lambda e, m=m, kc=kc: e.matmul(
                        pmm[:, (m % 4) * 512:(m % 4 + 1) * 512],
                        wfce[:, kc * FF + m * 128: kc * FF + (m + 1) * 128],
                        h2[:, kc, :], start=(kc == 0), stop=(kc == 7)),
                      waits=wts if kc == 0 else [], incs=[(sT, 1)] if kc == 7 else [])
                snap = sT.n
                s(lambda e, m=m: e.activation(
                    h1[:, m * 512:(m + 1) * 512],
                    pmm[:, (m % 4) * 512:(m % 4 + 1) * 512], AF.Gelu,
                    bias=bfcv[:, m:m + 1]), waits=[(sT, snap)])
                fcev[m] = sS.n
            pjev = {}
            for mo in range(NB):
                slot = mo % 2
                b.op("scalar",
                     lambda e, mo=mo, slot=slot: e.dma_start(
                         out=wpjc[:, slot * NFB * 128:(slot + 1) * NFB * 128],
                         in_=d_wpjbf[mo]),
                     waits=[(sxp[0], pj_ready[0]), (sxp[1], pj_ready[1]),
                            (sT, wpjc_used[slot])],
                     incs=[(swp[slot], 16)], is_dma=True)
                wts = [(sS, fcev[NFB - 1]), (swp[slot], swp[slot].n)]
                if mo >= 4:
                    wts.append((sS, pjev[mo - 4]))
                for kc in range(NFB):
                    t(lambda e, mo=mo, kc=kc, slot=slot: e.matmul(
                        pmm[:, (mo % 4) * 512:(mo % 4 + 1) * 512],
                        wpjc[:, slot * NFB * 128 + kc * 128:
                             slot * NFB * 128 + (kc + 1) * 128],
                        h1[:, kc * 512:(kc + 1) * 512], start=(kc == 0),
                        stop=(kc == 31)),
                      waits=wts if kc == 0 else [],
                      incs=[(sT, 1)] if kc == 31 else [])
                wpjc_used[slot] = sT.n
                snap = sT.n
                s(lambda e, mo=mo: e.activation(
                    outt[:, mo, :], pmm[:, (mo % 4) * 512:(mo % 4 + 1) * 512],
                    AF.Identity, scale=vb("gb", mo), bias=vb("gbbp", mo)),
                  waits=[(sT, snap), (sV, sV_h2)])
                pjev[mo] = sS.n
            v(lambda e: e.tensor_tensor(outt[:], outt[:], x2t[:], ALU.add),
              waits=[(sS, pjev[NB - 1])])
            sV_out = sV.n
            if j + 1 < NT:
                dma_q("sync", x2t[:],
                      d_x2sp[:, (j + 1) * T:(j + 2) * T].rearrange("(b p) t -> p b t", p=128),
                      sxx, waits=[(sV, sV_out)])
                sx_x2[0] = sxx.n
            dma_q("sync", d_out[:, j * T:(j + 1) * T].rearrange("(b p) t -> p b t", p=128),
                  outt[:], sout, waits=[(sV, sV_out)])
            outfree[0] = sout.n

        b.op("sync", None, waits=[(sout, sout.n), (sxx, sxx.n)], incs=[(bar, 1)])

    nc2 = b.nc
    for c in ctrs:
        b.sems[c.name] = es.enter_context(nc2.semaphore("sem_" + c.name))

    @block.sync
    def _(eng):
        b.replay("sync", eng)

    @block.gpsimd
    def _(eng):
        b.replay("gpsimd", eng)

    @block.vector
    def _(eng):
        b.replay("vector", eng)

    @block.scalar
    def _(eng):
        b.replay("scalar", eng)

    @block.tensor
    def _(eng):
        b.replay("tensor", eng)


def make_in_maps(inputs, ncore=NCORE):
    x = np.asarray(inputs["L0_attn_x"])
    assert x.shape == (4, 8192, 1024)

    misc = np.zeros((128, 16), np.float32)
    misc[:, 0] = 1.0
    misc[:, 2] = EPS
    misc[:, 5] = M32
    misc[:, 6] = M64 - M32

    gsel = np.zeros((128, NB * 16), np.float32)
    gselT = np.zeros((16, NB * 128), np.float32)
    for bb in range(NB):
        for j2 in range(2):
            head = 2 * bb + j2
            gsel[j2 * 64:(j2 + 1) * 64, bb * 16 + head] = 1.0
            gselT[head, bb * 128 + j2 * 64: bb * 128 + (j2 + 1) * 64] = 1.0

    common = {

        "WfcT": np.ascontiguousarray(np.asarray(inputs["L0_ffn_W_fc"], np.float32).T),
        "WprojT": np.ascontiguousarray(np.asarray(inputs["L0_ffn_W_proj"], np.float32).T),
        "Afc": np.ascontiguousarray(inputs["L0_ffn_A_fc"], np.float32),
        "BfcT": np.ascontiguousarray(np.asarray(inputs["L0_ffn_B_fc"], np.float32).T),
        "Aproj": np.ascontiguousarray(inputs["L0_ffn_A_proj"], np.float32),
        "BprojT": np.ascontiguousarray(np.asarray(inputs["L0_ffn_B_proj"], np.float32).T),

        "bfc": np.ascontiguousarray(
            np.asarray(inputs["L0_ffn_b_fc"], np.float32).reshape(NFB, 128).T),
        "bproj": _fm(inputs["L0_ffn_b_proj"]),
        "misc": misc,
        "onesK1": np.ones((1, 128), np.float32),
        "ramp": np.arange(1, T + 1, dtype=np.float32).reshape(1, T),
        "gsel": gsel,
        "gselT": gselT,
    }
    for nm, key in (("WgT", "L0_attn_Wg_w"), ("WrT", "L0_attn_Wr_w"),
                    ("WkT", "L0_attn_Wk_w"), ("WvT", "L0_attn_Wv_w"),
                    ("WoT", "L0_attn_Wo_w")):
        common[nm] = _sw(np.asarray(inputs[key], np.float32).T).reshape(8, 128, 1024)

    vec_names = ("mg", "mr", "mk", "mv", "bg2", "br", "bk", "bv", "uv",
                 "ca1", "sa", "cb1", "sb_", "ga", "gb", "gah", "gabo", "gbbp",
                 "e", "dec", "d512", "carry", "Din", "hlast", "recvmask",
                 "modsc")
    vslot = {n: i for i, n in enumerate(vec_names)}
    vbase = np.zeros((128, 26 * NB), np.float32)

    def vput(vp, name, val):
        vp[:, vslot[name] * NB:(vslot[name] + 1) * NB] = val

    vput(vbase, "mg", _fm(inputs["L0_attn_mg"]))
    vput(vbase, "mr", _fm(inputs["L0_attn_mr"]))
    vput(vbase, "mk", _fm(inputs["L0_attn_mk"]))
    vput(vbase, "mv", _fm(inputs["L0_attn_mv"]))
    vput(vbase, "bg2", _fm(inputs["L0_attn_Wg_b"]))
    vput(vbase, "br", _fm(inputs["L0_attn_Wr_b"]))
    vput(vbase, "bk", _fm(inputs["L0_attn_Wk_b"]))
    vput(vbase, "bv", _fm(inputs["L0_attn_Wv_b"]))
    vput(vbase, "uv", _fm(np.asarray(inputs["L0_attn_u"]).reshape(-1)))
    vput(vbase, "gabo", _fm(inputs["L0_attn_Wo_b"]))
    vput(vbase, "e", _fm(np.asarray(inputs["L0_attn_w"]).reshape(-1)))

    wmod_f = np.asarray(inputs["L0_wmod"], np.float32)
    in_maps = []
    for c in range(NCORE):
        q, half = c // 2, c % 2
        t0 = half * TOK
        xT = np.ascontiguousarray(x[q].T)
        lead = xT[:, (t0 - 1) % 8192:(t0 - 1) % 8192 + 1]
        m = dict(common)
        m["wmodS"] = np.ascontiguousarray(wmod_f[:, half * 3072:(half + 1) * 3072])
        m["xT"] = np.ascontiguousarray(np.concatenate([lead, xT[:, t0:t0 + TOK]], axis=1))
        vp = vbase.copy()
        vput(vp, "modsc", _fm(np.asarray(inputs["L0_c"])[q]))
        vput(vp, "recvmask", np.full((128, NB), float(half), np.float32))
        m["vpack_in"] = vp
        in_maps.append(m)
    return in_maps[:ncore]


def kernel(**inputs):
    in_maps = make_in_maps(inputs)
    bprog = build_program()
    res = run_bass_kernel_spmd(bprog.nc, in_maps, core_ids=list(range(NCORE)))

    out = np.empty((4, 8192, 1024), np.float32)
    for c in range(NCORE):
        q, half = c // 2, c % 2
        t0 = half * TOK
        out[q, t0:t0 + TOK, :] = res.results[c]["outT"].T
    if DBG:
        kernel._dbg = res.results
    return out


# version 12
# speedup vs baseline: 1.0596x; 1.0596x over previous
import os
import numpy as np
from contextlib import ExitStack

import concourse.bass as bass
import concourse.mybir as mybir
from concourse.bass_utils import run_bass_kernel_spmd

F32 = mybir.dt.float32
BF16 = mybir.dt.bfloat16
AF = mybir.ActivationFunctionType
ALU = mybir.AluOpType
AX = mybir.AxisListType

NCORE = 8
D = 1024
NB = 8
T = 512
NT = 8
TOK = NT * T
H, HD = 16, 64
FF = 4096
NFB = 32
EPS = 1e-5
M32 = 12582912.0
M64 = 6755399441055744.0

DBG = os.environ.get("MK_DBG", "") == "1"


class Ctr:

    def __init__(self, name):
        self.name = name
        self.n = 0


class B:
    self_sync = {"vector", "scalar", "gpsimd"}

    def __init__(self):
        self.nc = bass.Bass("TRN2", target_bir_lowering=False, debug=False)
        self.items = []
        self.sems = {}
        self.eng_ctr = {}

    def op(self, eng, fn, waits=(), incs=(), is_dma=False):
        incs = list(incs)
        ws = [(c, v) for (c, v) in waits if v > 0]
        if eng in B.self_sync and fn is not None and not is_dma:
            ctr = self.eng_ctr.get(eng)
            if ctr is not None:
                ws.append((ctr, ctr.n))
                if not any(c is ctr for c, _ in incs):
                    incs.append((ctr, 1))
        for c, d in incs:
            c.n += d
        self.items.append((eng, fn, ws, incs))

    def dma(self, eng, out_ap, in_ap, sem, waits=(), slow=False):
        self.op(eng, lambda e, o=out_ap, i=in_ap, s=slow: e.dma_start(
            out=o, in_=i, allow_slow_non_contiguous=s),
                waits=waits, incs=[(sem, 16)], is_dma=True)

    def replay(self, eng_name, eng):
        for name, fn, waits, incs in self.items:
            if name != eng_name:
                continue
            for c, v in waits:
                eng.wait_ge(self.sems[c.name], v)
            instr = fn(eng) if fn is not None else None
            if incs:
                if instr is None:
                    instr = eng.nop()
                for c, d in incs:
                    instr = instr.then_inc(self.sems[c.name], d)


def _fm(v):
    return np.ascontiguousarray(np.asarray(v, np.float32).reshape(NB, 128).T)


def _sw(w):
    return np.ascontiguousarray(
        np.asarray(w, np.float32).reshape(8, 128, 8, 128).transpose(2, 1, 0, 3)
    ).reshape(8, 128, 1024)


def build_program():
    b = B()
    nc = b.nc
    es = ExitStack()
    with es:
        _build(b, es)
    return b


def _build(b, es):
    nc = b.nc
    dget = lambda n, sh, dt=F32, k="ExternalInput": nc.dram_tensor(n, sh, dt, kind=k).ap()
    d_x = dget("xT", [D, TOK + 1])
    d_w = {n: dget(n, [8, 128, 1024]) for n in ("WgT", "WrT", "WkT", "WvT", "WoT")}
    d_wmod = dget("wmodS", [D, 3 * D])
    d_mgi = dget("mgi", [3 * D, 1], F32, "Internal")
    d_mgo = dget("mgo", [6 * D, 1], F32, "Internal")
    d_wfc = dget("WfcT", [D, FF])
    d_wpj = dget("WprojT", [FF, D])
    d_afc = dget("Afc", [16, D])
    d_bfcT = dget("BfcT", [16, FF])
    d_apj = dget("Aproj", [16, FF])
    d_bpjT = dget("BprojT", [16, D])
    d_vpack = dget("vpack_in", [128, 26 * NB])
    d_bfc = dget("bfc", [128, NFB])
    d_bpj = dget("bproj", [128, NB])
    d_misc = dget("misc", [128, 16])
    d_onesK1 = dget("onesK1", [1, 128])
    d_ramp = dget("ramp", [1, T])
    d_gsel = dget("gsel", [128, NB * 16])
    d_gselT = dget("gselT", [16, NB * 128])
    d_out = dget("outT", [D, TOK], F32, "ExternalOutput")

    kd = "ExternalOutput" if DBG else "Internal"
    d_hsp = dget("hsp", [NT, D, T + 1], F32, kd)
    d_scsp = dget("scsp", [NT, D, T], F32, kd)
    d_x2sp = dget("x2sp", [D, TOK], F32, kd)
    d_a2dbg = dget("a2dbg", [5, D, T], F32, kd)
    d_wbf = dget("wbf", [5, 8, 128, 1024], BF16, "Internal")
    d_wpjbf = dget("wpjbf", [NB, 128, NFB * 128], BF16, "Internal")
    d_powsp = dget("powsp", [D, T], F32, "Internal")
    d_ccin = dget("ccin", [D, 1], F32, "Internal")
    d_ccout = dget("ccout", [2 * D, 1], F32, "Internal")
    d_bounce = dget("bounce", [8192, 1], F32, "Internal")

    sT = Ctr("sT"); sV = Ctr("sV"); sS = Ctr("sS"); sG = Ctr("sG")
    scc = Ctr("scc"); bar = Ctr("bar")
    sxb = Ctr("sxb"); sxx = Ctr("sxx"); sxh = Ctr("sxh"); sxhs = Ctr("sxhs")
    sxc = [Ctr("sxc0"), Ctr("sxc1")]
    sx2 = Ctr("sx2"); sout = Ctr("sout"); sxr = Ctr("sxr")
    sxp = [Ctr("sxp0"), Ctr("sxp1")]
    sxq = [Ctr("sxq0"), Ctr("sxq1")]
    swm = [Ctr("swm0"), Ctr("swm1")]
    swS = [Ctr(f"swS{i}") for i in range(6)]
    swp = [Ctr("swp0"), Ctr("swp1")]
    sgw = [Ctr(f"sgw{i}") for i in range(5)]
    sgc = Ctr("sgc"); sgq = Ctr("sgq"); sgq2 = Ctr("sgq2")
    sgl = [Ctr("sgl0"), Ctr("sgl1"), Ctr("sgl2")]
    sxps = Ctr("sxps")
    ctrs = ([sT, sV, sS, sG, scc, bar, sxb, sxx, sxh, sxhs, sx2, sout, sxr,
             sgc, sgq, sgq2, sxps] + sxp + sxc + sxq + swm + swS + swp + sgw + sgl)
    b.eng_ctr = {"vector": sV, "scalar": sS, "gpsimd": sG}

    sb = lambda n, sh, dt=F32: es.enter_context(nc.sbuf_tensor(n, sh, dt))
    misc = sb("s_misc", [128, 16])
    onesK1 = sb("s_onesK1", [1, 128])
    ones128b = sb("ones128b", [128, 1], BF16)
    gsel = sb("s_gsel", [128, NB * 16], BF16)
    rows = sb("rows", [128, 2048])
    _vec_names = ("mg", "mr", "mk", "mv", "bg2", "br", "bk", "bv", "uv",
                  "ca1", "sa", "cb1", "sb_", "ga", "gb", "gah", "gabo", "gbbp",
                  "e", "dec", "d512", "carry", "Din", "hlast", "recvmask",
                  "modsc")
    vpack = sb("vpack", [128, len(_vec_names) * NB])
    vec = {n: vpack[:, i * NB:(i + 1) * NB]
           for i, n in enumerate(_vec_names)}
    siluc = sb("siluc", [128, NB])
    bfcv = sb("bfcv", [128, NFB])
    bpjv = sb("bpjv", [128, NB])
    q_ = {}
    for n in ("delta", "Mpp", "sfc", "invfc", "spj", "invpj", "mxfc", "mxpj",
              "t1", "t2", "m11", "s11f", "s11p", "i11f", "i11p"):
        q_[n] = sb("q_" + n, [128, 1])


    pmm = es.enter_context(nc.psum_tensor("pmm", [128, 4 * 512], F32))
    pbc = es.enter_context(nc.psum_tensor("pbc", [128, 2 * 512], F32))
    pst = es.enter_context(nc.psum_tensor("pst", [128, 512], F32))
    pmo = es.enter_context(nc.psum_tensor("pmo", [128, 512], F32))

    block = es.enter_context(nc.Block())

    mc = lambda c: misc[:, c:c + 1]
    vb = lambda name, bb: vec[name][:, bb:bb + 1]

    def bc3(ps_slice):
        return ps_slice.unsqueeze(1).broadcast_to([128, NB, T])

    es_a = ExitStack()
    with es_a:
        sba = lambda n, sh, dt=F32: es_a.enter_context(nc.sbuf_tensor(n, sh, dt))
        xbuf = [sba(f"xb{i}", [128, NB, T + 1]) for i in range(2)]
        hb = sba("hb", [128, NB, T + 1])
        hm = [(sba("hmA0", [128, NB, T], BF16), sba("hmB0", [128, NB, T], BF16)),
              (sba("hmA1", [128, NB, T], BF16), sba("hmB1", [128, NB, T], BF16))]
        ka = sba("ka", [128, NB, T])
        va = sba("va", [128, NB, T])
        scb = [sba(f"sc{i}", [128, NB, T]) for i in range(2)]
        x2b = sba("x2b", [128, NB, T])
        pow_ = xbuf[1][:, :, 0:T]
        wch = sba("wch", [128, 6 * 1024], BF16)
        ramp = sba("s_ramp", [1, T])
        gselT = sba("s_gselT", [16, NB * 128])
        qst = sba("qst", [128, 2 * 1024])
        tmp = sba("tmp", [128, 2 * T])

        b.dma("sync", misc[:], d_misc, sxb)
        b.dma("sync", onesK1[:], d_onesK1, sxb)
        b.dma("sync", ramp[:], d_ramp, sxb)
        b.dma("sync", gselT[:], d_gselT, sxb)
        b.dma("sync", vpack[:], d_vpack, sxb)
        b.dma("sync", bfcv[:], d_bfc, sxb)
        b.dma("sync", bpjv[:], d_bpj, sxb)
        n_setup_sync = sxb.n
        b.dma("gpsimd", gsel[:], d_gsel, sgc)
        b.dma("gpsimd", ones128b[:], d_onesK1.rearrange("one p -> p one"), sgc)
        sgc_const = sgc.n
        b.dma("gpsimd", d_wbf[2], d_w["WkT"], sgw[2])
        b.dma("gpsimd", d_wbf[3], d_w["WvT"], sgw[3])
        b.dma("gpsimd", d_wbf[0], d_w["WgT"], sgw[0])
        b.dma("gpsimd", d_wbf[1], d_w["WrT"], sgw[1])
        b.dma("gpsimd", d_wbf[4], d_w["WoT"], sgw[4])
        sg_w = {"g": (sgw[0], 16), "r": (sgw[1], 16), "k": (sgw[2], 16),
                "v": (sgw[3], 16), "o": (sgw[4], 16)}

        wait_c = [(sxb, n_setup_sync)]

        def v(fn, waits=(), incs=((sV, 1),)):
            b.op("vector", fn, waits=waits, incs=incs)

        def s(fn, waits=(), incs=((sS, 1),)):
            b.op("scalar", fn, waits=waits, incs=incs)

        def t(fn, waits=(), incs=()):
            b.op("tensor", fn, waits=waits, incs=incs)

        def g(fn, waits=(), incs=((sG, 1),)):
            b.op("gpsimd", fn, waits=waits, incs=incs)

        v(lambda e: e.tensor_scalar(q_["t1"][:], mc(0), 2.0 ** -30, None, ALU.add),
          waits=wait_c)
        v(lambda e: e.tensor_scalar(q_["delta"][:], q_["t1"][:], 1.0, 2.0 ** 30,
                                    ALU.subtract, ALU.mult))
        v(lambda e: e.tensor_scalar(q_["Mpp"][:], q_["delta"][:], mc(6), mc(5),
                                    ALU.mult, ALU.add))
        v(lambda e: e.tensor_scalar_mul(vec["bg2"], vec["bg2"], 0.5), waits=wait_c)
        s(lambda e: e.activation(vec["e"], vec["e"], AF.Exp), waits=wait_c)
        s(lambda e: e.activation(vec["dec"], vec["e"], AF.Exp, scale=-1.0))
        sS_dec = sS.n
        s(lambda e: e.activation(siluc[:], vec["modsc"], AF.Silu), waits=wait_c)
        sS_silu = sS.n
        v(lambda e: e.memset(vec["carry"], 0.0))
        v(lambda e: e.memset(q_["mxfc"][:], 0.0))
        v(lambda e: e.memset(q_["mxpj"][:], 0.0))

        b.dma("sync", d_bounce[6144:7168].rearrange("(b p) one -> p (b one)", p=128),
              vec["e"], sxb, waits=[(sS, sS_dec)], slow=True)
        sx_eb = sxb.n
        b.dma("sync", rows[0:1, 1024:2048],
              d_bounce[6144:7168].rearrange("(b p) one -> one (b p)", p=128),
              sxb, waits=[(sxb, sx_eb)], slow=True)
        sx_erow = sxb.n
        for bb in range(NB):
            half = bb % 2
            t(lambda e, bb=bb, half=half: e.matmul(
                pbc[:, half * 512:(half + 1) * 512],
                rows[0:1, 1024 + bb * 128:1024 + (bb + 1) * 128], ramp[:],
                start=True, stop=True),
              waits=[(sxb, sx_erow), (sS, sS.n)], incs=[(sT, 1)])
            s(lambda e, bb=bb, half=half: e.activation(
                pow_[:, bb, :], pbc[:, half * 512:(half + 1) * 512], AF.Exp,
                scale=-1.0), waits=[(sT, sT.n)])
        sS_pow = sS.n
        v(lambda e: e.tensor_copy(vec["d512"], pow_[:, :, T - 1]),
          waits=[(sS, sS_pow)])
        sV_d512 = sV.n
        b.dma("sync", d_powsp.rearrange("(b p) t -> p b t", p=128), pow_,
              sxps, waits=[(sS, sS_pow), (sV, sV_d512)])
        sx_powsp = sxps.n

        wm_used = [0, 0]
        for sl in range(6):
            for kc in range(8):
                wh = (sl * 8 + kc) % 2
                b.dma("scalar", tmp[:, wh * 512:(wh + 1) * 512],
                      d_wmod[kc * 128:(kc + 1) * 128, sl * 512:(sl + 1) * 512],
                      swm[wh], waits=[(sT, wm_used[wh])])
                t(lambda e, kc=kc, wh=wh: e.matmul(
                    pmo[0:1, :], siluc[:, kc:kc + 1], tmp[:, wh * 512:(wh + 1) * 512],
                    start=(kc == 0), stop=(kc == 7)),
                  waits=[(swm[wh], swm[wh].n), (sS, sS_silu)] +
                        ([(sS, sS.n)] if kc == 0 else []),
                  incs=[(sT, 1)])
                wm_used[wh] = sT.n
            s(lambda e, sl=sl: e.activation(rows[0:1, 0:512], pmo[0:1, :], AF.Identity),
              waits=[(sT, sT.n)] + ([(sxb, sxb.n)] if sl > 0 else []))
            b.dma("sync",
                  d_mgi[sl * 512:(sl + 1) * 512].rearrange("t one -> one t"),
                  rows[0:1, 0:512], sxb, waits=[(sS, sS.n)])
        sx_modb = sxb.n
        b.op("gpsimd",
             lambda e: e.collective_compute(
                 "AllGather", ALU.bypass,
                 replica_groups=[[0, 1], [2, 3], [4, 5], [6, 7]],
                 ins=[d_mgi], outs=[d_mgo]),
             waits=[(sxb, sx_modb), (sG, sG.n)], incs=[(scc, 1)], is_dma=True)
        mod_names = ["sa", "ca1", "ga", "sb_", "cb1", "gb"]
        for j, nm in enumerate(mod_names):
            b.dma("sync", vec[nm],
                  d_mgo[j * 1024:(j + 1) * 1024].rearrange("(b p) one -> p (b one)", p=128),
                  sxb, waits=[(scc, 1)], slow=True)
        sx_mod = sxb.n
        v(lambda e: e.tensor_scalar_add(vec["ca1"], vec["ca1"], 1.0),
          waits=[(sxb, sx_mod)])
        v(lambda e: e.tensor_scalar_add(vec["cb1"], vec["cb1"], 1.0))
        for nm in ("ga", "gb"):
            v(lambda e, nm=nm: e.tensor_scalar_max(tmp[:, 0:NB], vec[nm], 0.25))
            v(lambda e, nm=nm: e.tensor_scalar_min(tmp[:, NB:2 * NB], vec[nm], -0.25))
            v(lambda e, nm=nm: e.tensor_single_scalar(tmp[:, 2 * NB:3 * NB], vec[nm],
                                                      0.0, ALU.is_ge))
            v(lambda e: e.tensor_sub(tmp[:, 3 * NB:4 * NB], tmp[:, 0:NB], tmp[:, NB:2 * NB]))
            v(lambda e: e.tensor_mul(tmp[:, 2 * NB:3 * NB], tmp[:, 2 * NB:3 * NB],
                                     tmp[:, 3 * NB:4 * NB]))
            v(lambda e, nm=nm: e.tensor_add(vec[nm], tmp[:, NB:2 * NB],
                                            tmp[:, 2 * NB:3 * NB]))
        v(lambda e: e.tensor_scalar_mul(vec["gah"], vec["ga"], 0.5))
        v(lambda e: e.tensor_mul(vec["gabo"], vec["ga"], vec["gabo"]))
        v(lambda e: e.tensor_mul(vec["gbbp"], vec["gb"], bpjv[:]))
        sV_mod = sV.n

        x_cols = lambda j: d_x[:, j * T: j * T + T + 1].rearrange(
            "(b p) t -> p b t", p=128)
        x_cols_b = lambda j: d_x[:, j * T + 1: (j + 1) * T + 1].rearrange(
            "(b p) t -> p b t", p=128)

        def ln_stats(src_bfA, src_bfB):
            for bb in range(NB):
                t(lambda e, bb=bb: e.matmul(pst[0:1, :], ones128b[:], src_bfA[:, bb, :],
                                            start=(bb == 0), stop=(bb == 7)),
                  waits=[(sS, sS.n), (sV, sV.n), (sgc, sgc_const)] if bb == 0 else [],
                  incs=[])
            for bb in range(NB):
                t(lambda e, bb=bb: e.matmul(pmo[0:1, :], ones128b[:], src_bfB[:, bb, :],
                                            start=(bb == 0), stop=(bb == 7)),
                  incs=[(sT, 1)] if bb == 7 else [])
            snap = sT.n
            v(lambda e: e.tensor_scalar_mul(rows[0:1, 1024:1536], pst[0:1, :], 1.0 / D),
              waits=[(sT, snap), (sxr, sxr.n)])
            v(lambda e: e.tensor_scalar_mul(rows[0:1, 1536:2048], pmo[0:1, :], 1.0 / D))
            v(lambda e: e.tensor_mul(rows[0:1, 0:512], rows[0:1, 1024:1536],
                                     rows[0:1, 1024:1536]))
            v(lambda e: e.tensor_sub(rows[0:1, 1536:2048], rows[0:1, 1536:2048],
                                     rows[0:1, 0:512]))
            snapV = sV.n
            s(lambda e: e.activation(rows[0:1, 0:512], rows[0:1, 1536:2048], AF.Ln,
                                     bias=misc[0:1, 2:3]), waits=[(sV, snapV)])
            s(lambda e: e.activation(rows[0:1, 0:512], rows[0:1, 0:512], AF.Exp,
                                     scale=-0.5))
            snapS = sS.n
            v(lambda e: e.tensor_mul(rows[0:1, 512:1024], rows[0:1, 1024:1536],
                                     rows[0:1, 0:512]),
              waits=[(sS, snapS)])
            return sV.n

        def bcast_rows(cA, cB, waitV):
            t(lambda e, cA=cA: e.matmul(pbc[:, 0:512], onesK1[:],
                                        rows[0:1, cA:cA + 512],
                                        start=True, stop=True),
              waits=[(sV, waitV), (sV, pbc_free[0])], incs=[])
            t(lambda e, cB=cB: e.matmul(pbc[:, 512:1024], onesK1[:],
                                        rows[0:1, cB:cB + 512],
                                        start=True, stop=True), incs=[(sT, 1)])
            return sT.n

        pbc_free = [0]

        def mm_group(dst_bank, wsrc, kslices, moving_blocks, wait_extra=()):
            for kc in range(8):
                wts = list(wait_extra) if kc == 0 else []
                t(lambda e, kc=kc: e.matmul(
                    pmm[:, dst_bank * 512:(dst_bank + 1) * 512],
                    wsrc[:, kslices + kc * 128: kslices + (kc + 1) * 128],
                    moving_blocks[:, kc, :],
                    start=(kc == 0), stop=(kc == 7)),
                  waits=wts, incs=[(sT, 1)] if kc == 7 else [])

        wl_snap = {i: (swS[i], 0) for i in range(6)}

        def wload(proj_idx, m, slot, wait_snap):
            b.dma("scalar", wch[:, slot * 1024:(slot + 1) * 1024],
                  d_wbf[proj_idx, m], swS[slot],
                  waits=[sg_w[("g", "r", "k", "v", "o")[proj_idx]],
                         (sT, wait_snap)])
            wl_snap[slot] = (swS[slot], swS[slot].n)

        wstream = []
        for _ in range(NT):
            wstream += [(2, c) for c in range(NB)] + [(3, c) for c in range(NB)]
        for _ in range(NT):
            wstream += ([(0, c) for c in range(NB)] + [(1, c) for c in range(NB)]
                        + [(4, c) for c in range(NB)])
        wpos = [0]

        def wslot():
            return wpos[0] % 6

        def wadvance():
            p = wpos[0]
            if p + 6 < len(wstream):
                pr, c = wstream[p + 6]
                wload(pr, c, (p + 6) % 6, sT.n)
            wpos[0] += 1

        b.dma("sync", xbuf[0][:], x_cols(0), sxx)
        sx_x0 = sxx.n
        s(lambda e: e.activation(hm[0][0][:, :, 0:1], xbuf[0][:, :, 0:1], AF.Copy),
          waits=[(sxx, sx_x0)])
        s(lambda e: e.activation(hm[0][1][:, :, 0:1], xbuf[0][:, :, 0:1], AF.Square))
        sS_b0 = sS.n
        for bb in range(NB):
            t(lambda e, bb=bb: e.matmul(pst[0:1, 0:1], ones128b[:], hm[0][0][:, bb, 0:1],
                                        start=(bb == 0), stop=(bb == 7)),
              waits=[(sS, sS_b0), (sgc, sgc_const)] if bb == 0 else [], incs=[])
        for bb in range(NB):
            t(lambda e, bb=bb: e.matmul(pmo[0:1, 0:1], ones128b[:], hm[0][1][:, bb, 0:1],
                                        start=(bb == 0), stop=(bb == 7)),
              incs=[(sT, 1)] if bb == 7 else [])
        sT_b0 = sT.n
        v(lambda e: e.tensor_scalar_mul(rows[0:1, 1024:1025], pst[0:1, 0:1], 1.0 / D),
          waits=[(sT, sT_b0)])
        v(lambda e: e.tensor_scalar_mul(rows[0:1, 1536:1537], pmo[0:1, 0:1], 1.0 / D))
        v(lambda e: e.tensor_mul(rows[0:1, 0:1], rows[0:1, 1024:1025],
                                 rows[0:1, 1024:1025]))
        v(lambda e: e.tensor_sub(rows[0:1, 1536:1537], rows[0:1, 1536:1537],
                                 rows[0:1, 0:1]))
        sV_b1 = sV.n
        s(lambda e: e.activation(rows[0:1, 0:1], rows[0:1, 1536:1537], AF.Ln,
                                 bias=misc[0:1, 2:3]), waits=[(sV, sV_b1)])
        s(lambda e: e.activation(rows[0:1, 0:1], rows[0:1, 0:1], AF.Exp, scale=-0.5))
        sS_b1 = sS.n
        v(lambda e: e.tensor_mul(rows[0:1, 512:513], rows[0:1, 1024:1025],
                                 rows[0:1, 0:1]),
          waits=[(sS, sS_b1)])
        sV_b2 = sV.n
        t(lambda e: e.matmul(pbc[:, 0:1], onesK1[:], rows[0:1, 0:1], start=True,
                             stop=True), waits=[(sV, sV_b2)], incs=[])
        t(lambda e: e.matmul(pbc[:, 512:513], onesK1[:], rows[0:1, 512:513], start=True,
                             stop=True), incs=[(sT, 1)])
        sT_b2 = sT.n
        v(lambda e: e.tensor_tensor(vec["hlast"], xbuf[0][:, :, 0],
                                    pbc[:, 0:1].broadcast_to([128, NB]), ALU.mult),
          waits=[(sT, sT_b2)])
        v(lambda e: e.tensor_tensor(vec["hlast"], vec["hlast"],
                                    pbc[:, 512:513].broadcast_to([128, NB]),
                                    ALU.subtract))
        v(lambda e: e.tensor_mul(vec["hlast"], vec["hlast"], vec["ca1"]))
        v(lambda e: e.tensor_add(vec["hlast"], vec["hlast"], vec["sa"]))
        pbc_free[0] = sV.n

        stats_psum_free = [sV.n]
        x2b_free = [0]
        xsum_snap = [0, 0]
        h_snap = [0, 0]
        hm_free = [0, 0]
        hsp_snap = [0]
        uadd_snap = [0]
        kv_snap = [0]
        sV_hmk = [0]
        sV_hmv = [0]
        sc_spilled = [0, 0]
        sx_x = [sx_x0, 0]

        def a1_sq(k, sq_waits=()):
            s(lambda e, k=k: e.activation(x2b[:], xbuf[k % 2][:, :, 1:T + 1],
                                          AF.Square),
              waits=list(sq_waits) + [(sT, x2b_free[0])])

        def a1_stats_rest(k):
            par = k % 2
            xcur = xbuf[par]
            sS_sq = sS.n
            for bb in range(NB):
                t(lambda e, bb=bb: e.matmul(pst[0:1, :], mc(0), xcur[:, bb, 1:T + 1],
                                            start=(bb == 0), stop=(bb == 7)),
                  waits=[(sS, sS_sq), (sV, stats_psum_free[0])] if bb == 0 else [],
                  incs=[])
            for bb in range(NB):
                t(lambda e, bb=bb: e.matmul(pmo[0:1, :], mc(0), x2b[:, bb, :],
                                            start=(bb == 0), stop=(bb == 7)),
                  incs=[(sT, 1)] if bb == 7 else [])
            snap = sT.n
            x2b_free[0] = snap
            xsum_snap[par] = snap
            v(lambda e: e.tensor_scalar_mul(rows[0:1, 1024:1536], pst[0:1, :], 1.0 / D),
              waits=[(sT, snap)])
            v(lambda e: e.tensor_scalar_mul(rows[0:1, 1536:2048], pmo[0:1, :], 1.0 / D))
            stats_psum_free[0] = sV.n
            v(lambda e: e.tensor_mul(rows[0:1, 0:512], rows[0:1, 1024:1536],
                                     rows[0:1, 1024:1536]))
            v(lambda e: e.tensor_sub(rows[0:1, 1536:2048], rows[0:1, 1536:2048],
                                     rows[0:1, 0:512]))
            snapV = sV.n
            s(lambda e: e.activation(rows[0:1, 0:512], rows[0:1, 1536:2048], AF.Ln,
                                     bias=misc[0:1, 2:3]), waits=[(sV, snapV)])
            s(lambda e: e.activation(rows[0:1, 0:512], rows[0:1, 0:512], AF.Exp,
                                     scale=-0.5))
            snapS = sS.n
            v(lambda e: e.tensor_mul(rows[0:1, 512:1024], rows[0:1, 1024:1536],
                                     rows[0:1, 0:512]),
              waits=[(sS, snapS)])
            return bcast_rows(0, 512, sV.n)

        def a1_happly(k, bsn):
            par = k % 2
            xcur = xbuf[par]
            v(lambda e: e.tensor_tensor(hb[:, :, 1:T + 1], xcur[:, :, 1:T + 1],
                                        bc3(pbc[:, 0:512]), ALU.mult),
              waits=[(sT, bsn), (sxhs, hsp_snap[0]),
                     ((sxx, sxh)[par], sx_x[par])])
            v(lambda e: e.tensor_tensor(hb[:, :, 1:T + 1], hb[:, :, 1:T + 1],
                                        bc3(pbc[:, 512:1024]), ALU.subtract))
            pbc_free[0] = sV.n
            for bb in range(NB):
                v(lambda e, bb=bb: e.tensor_scalar(hb[:, bb, 1:T + 1],
                                                   hb[:, bb, 1:T + 1],
                                                   vb("ca1", bb), vb("sa", bb),
                                                   ALU.mult, ALU.add))
            v(lambda e: e.tensor_copy(hb[:, :, 0:1], vec["hlast"].unsqueeze(2)))
            v(lambda e: e.tensor_copy(vec["hlast"].unsqueeze(2), hb[:, :, T:T + 1]))
            h_snap[par] = sV.n
            b.dma("sync", d_hsp[k].rearrange("(b p) t -> p b t", p=128), hb[:],
                  sxhs, waits=[(sV, sV.n)])
            hsp_snap[0] = sxhs.n

        def a1_hmix(k):
            par = k % 2
            db = scb[par]
            v(lambda e, db=db: e.tensor_tensor(db[:], hb[:, :, 1:T + 1],
                                               hb[:, :, 0:T], ALU.subtract),
              waits=[(sxc[par], sc_spilled[par])])
            for bb in range(NB):
                v(lambda e, bb=bb, par=par, db=db: e.scalar_tensor_tensor(
                    hm[par][0][:, bb, :], db[:, bb, :], vb("mk", bb),
                    hb[:, bb, 0:T], ALU.mult, ALU.add),
                  waits=[(sT, hm_free[par])] if bb == 0 else [])
            sV_hmk[0] = sV.n
            for bb in range(NB):
                v(lambda e, bb=bb, par=par, db=db: e.scalar_tensor_tensor(
                    hm[par][1][:, bb, :], db[:, bb, :], vb("mv", bb),
                    hb[:, bb, 0:T], ALU.mult, ALU.add))
            sV_hmv[0] = sV.n

        for p0 in range(6):
            pr, c = wstream[p0]
            wload(pr, c, p0, 0)

        qmax_it_issue = [0]
        qmax_it_reduce = [0]
        qmax_vfree = [0, 0]
        qmax_snap = {}

        def qmax_issue():
            it = qmax_it_issue[0]
            if it >= 64:
                return
            assert it - qmax_it_reduce[0] <= 1
            qmax_it_issue[0] += 1
            if it < 32:
                src = d_wfc[(it // 4) * 128:(it // 4 + 1) * 128,
                            (it % 4) * 1024:(it % 4 + 1) * 1024]
            else:
                i2 = it - 32
                src = d_wpj[i2 * 128:(i2 + 1) * 128, :]
            half = it % 2
            qsem = (sgq, sgq2)[half]
            b.dma("gpsimd", qst[:, half * 1024:(half + 1) * 1024],
                  src, qsem, waits=[(sV, qmax_vfree[half])])
            qmax_snap[it] = (qsem, qsem.n)

        def qmax_reduce():
            it = qmax_it_reduce[0]
            if it >= 64 or qmax_it_issue[0] == it:
                return
            qmax_it_reduce[0] += 1
            acc = q_["mxfc"] if it < 32 else q_["mxpj"]
            half = it % 2
            v(lambda e, half=half: e.reduce_max(q_["t2"][:],
                                                qst[:, half * 1024:(half + 1) * 1024],
                                                AX.X, apply_absolute_value=True),
              waits=[qmax_snap[it]])
            v(lambda e, acc=acc: e.tensor_max(acc[:], acc[:], q_["t2"][:]))
            qmax_vfree[half] = sV.n

        a1_sq(0, sq_waits=[(sxx, sx_x0)])
        bsn0 = a1_stats_rest(0)
        a1_happly(0, bsn0)
        b.dma("sync", xbuf[1][:], x_cols(1), sxh, waits=[(sxps, sx_powsp)])
        sx_x[1] = sxh.n
        hm_free[0] = sT_b2
        a1_hmix(0)
        a1_sq(1, sq_waits=[(sxh, sx_x[1])])
        kevac = {m: 0 for m in range(NB)}
        vevac = {m: 0 for m in range(NB)}
        bsn_next = [0]

        for j in range(NT):
            p = j % 2
            hmA_j, hmB_j = hm[p]
            sV_hmk_j, sV_hmv_j = sV_hmk[0], sV_hmv[0]
            if j + 2 < NT:
                b.dma("sync", xbuf[p][:], x_cols(j + 2), (sxx, sxh)[p],
                      waits=[(sV, h_snap[p]), (sT, xsum_snap[p])])
                sx_x[p] = (sxx, sxh)[p].n
            if j + 1 < NT:
                bsn_next[0] = a1_stats_rest(j + 1)
            if j + 2 < NT:
                a1_sq(j + 2, sq_waits=[((sxx, sxh)[p], sx_x[p])])
            qmax_issue()
            qmax_issue()
            for m in range(NB):
                slot = wslot()
                mm_group(m % 4, wch, slot * 1024, hmA_j,
                         wait_extra=[wl_snap[slot], (sV, sV_hmk_j),
                                     (sS, vevac[4 + m % 4] if m < 4
                                      else kevac[m - 4])])
                wadvance()
                snap = sT.n
                s(lambda e, m=m: e.activation(ka[:, m, :],
                                              pmm[:, (m % 4) * 512:(m % 4 + 1) * 512],
                                              AF.Identity, bias=vb("bk", m)),
                  waits=[(sT, snap)] + ([(sV, kv_snap[0])] if m == 0 else []))
                kevac[m] = sS.n
            for m in range(NB):
                slot = wslot()
                mm_group(m % 4, wch, slot * 1024, hmB_j,
                         wait_extra=[wl_snap[slot], (sV, sV_hmv_j),
                                     (sS, kevac[NB - 1] if m < 4 else vevac[m - 4])])
                wadvance()
                snap = sT.n
                s(lambda e, m=m: e.activation(va[:, m, :],
                                              pmm[:, (m % 4) * 512:(m % 4 + 1) * 512],
                                              AF.Identity, bias=vb("bv", m)),
                  waits=[(sT, snap)] + ([(sV, kv_snap[0])] if m == 0 else []))
                vevac[m] = sS.n
            hm_free[p] = sT.n
            if j + 1 < NT:
                a1_happly(j + 1, bsn_next[0])
                a1_hmix(j + 1)
            qmax_reduce()
            qmax_reduce()
            qmax_issue()
            qmax_issue()
            v(lambda e, p=p: e.tensor_tensor(scb[1 - p][:], ka[:], va[:],
                                             ALU.mult),
              waits=[(sS, vevac[NB - 1])] +
                    ([(sxc[1 - p], sc_spilled[1 - p])] if j + 1 >= NT else []))
            kv_snap[0] = sV.n
            for bb in range(NB):
                v(lambda e, bb=bb, p=p: e.tensor_tensor_scan(
                    scb[p][:, bb, :], vb("dec", bb).broadcast_to([128, T]),
                    scb[1 - p][:, bb, :], vb("carry", bb), ALU.mult, ALU.add))
            v(lambda e, p=p: e.tensor_copy(vec["carry"], scb[p][:, :, T - 1]))
            for bb in range(NB):
                v(lambda e, bb=bb, p=p: e.scalar_tensor_tensor(
                    scb[p][:, bb, :], scb[1 - p][:, bb, :], vb("uv", bb),
                    scb[p][:, bb, :], ALU.mult, ALU.add))
            uadd_snap[0] = sV.n
            b.dma("sync", d_scsp[j].rearrange("(b p) t -> p b t", p=128), scb[p][:],
                  sxc[p], waits=[(sV, uadd_snap[0])])
            sc_spilled[p] = sxc[p].n
            qmax_reduce()
            qmax_reduce()

        b.dma("sync", d_ccin.rearrange("(b p) one -> p (b one)", p=128),
              vec["carry"], sxb, waits=[(sV, sV.n)], slow=True)
        sx_cc = sxb.n
        b.op("gpsimd",
             lambda e: e.collective_compute(
                 "AllGather", ALU.bypass,
                 replica_groups=[[0, 1], [2, 3], [4, 5], [6, 7]],
                 ins=[d_ccin], outs=[d_ccout]),
             waits=[(sxb, sx_cc), (sG, sG.n)], incs=[(scc, 1)], is_dma=True)
        b.dma("sync", vec["Din"],
              d_ccout[0:D].rearrange("(b p) one -> p (b one)", p=128), sxb,
              waits=[(scc, 2)], slow=True)
        sx_D = sxb.n
        v(lambda e: e.tensor_mul(vec["Din"], vec["Din"], vec["recvmask"]),
          waits=[(sxb, sx_D)])
        sV_D = sV.n

        b.dma("sync", pow_, d_powsp.rearrange("(b p) t -> p b t", p=128), sxps,
              waits=[(sV, h_snap[1]), (sT, xsum_snap[1]), (sxps, sx_powsp)])
        sx_powrl = sxps.n

        sx_h = [0]
        sx_sc = [0, 0]
        sx_x2free = [0]
        sx_xA2 = [0]
        sG_rw = [0, 0]
        corr_snapA = [0, 0]
        hmg_snapA = [0, 0]
        hmr_snapA = [0, 0]
        gnchain_snap = [0]
        gncast_snap = [0]
        ln2cast_snap = [0]
        ln2sums_snap = [0]
        ln2_vsnap = [0]
        x2add_snap = [0]
        woevac = {m: 0 for m in range(NB)}
        olad_free = [hm_free[0], hm_free[1]]
        sV_D_cur = [sV_D]

        b.dma("sync", hb[:], d_hsp[0].rearrange("(b p) t -> p b t", p=128), sxh,
              waits=[(sV, sV.n), (sxhs, sxhs.n)])
        sx_h[0] = sxh.n
        b.dma("sync", scb[0][:], d_scsp[0].rearrange("(b p) t -> p b t", p=128),
              sxc[0], waits=[(sxc[0], sxc[0].n), (sV, sV.n)])
        sx_sc[0] = sxc[0].n
        b.dma("sync", xbuf[0][:, :, 1:T + 1], x_cols_b(0), sxx,
              waits=[(sV, h_snap[0]), (sT, xsum_snap[0])])
        sx_xA2[0] = sxx.n

        def a2_corr(k):
            par = k % 2
            for bb in range(NB):
                v(lambda e, bb=bb, par=par: e.scalar_tensor_tensor(
                    scb[par][:, bb, :], pow_[:, bb, :], vb("Din", bb),
                    scb[par][:, bb, :], ALU.mult, ALU.add),
                  waits=([(sxc[par], sx_sc[par]), (sV, sV_D_cur[0])] +
                         ([(sxps, sx_powrl)] if k == 0 else [])) if bb == 0 else [])
            corr_snapA[par] = sV.n
            v(lambda e: e.tensor_mul(vec["Din"], vec["Din"], vec["d512"]))
            sV_D_cur[0] = sV.n

        def a2_delta_hmix(k):
            par = k % 2
            v(lambda e: e.tensor_tensor(x2b[:], hb[:, :, 1:T + 1],
                                        hb[:, :, 0:T], ALU.subtract),
              waits=[(sxh, sx_h[0]), (sS, ln2cast_snap[0]),
                     (sx2, sx_x2free[0]), (sT, x2b_free[0])])
            for bb in range(NB):
                v(lambda e, bb=bb, par=par: e.scalar_tensor_tensor(
                    hm[par][0][:, bb, :], x2b[:, bb, :], vb("mg", bb),
                    hb[:, bb, 0:T], ALU.mult, ALU.add),
                  waits=[(sT, olad_free[par])] if bb == 0 else [])
            hmg_snapA[par] = sV.n
            for bb in range(NB):
                v(lambda e, bb=bb, par=par: e.scalar_tensor_tensor(
                    hm[par][1][:, bb, :], x2b[:, bb, :], vb("mr", bb),
                    hb[:, bb, 0:T], ALU.mult, ALU.add))
            hmr_snapA[par] = sV.n

        a2_corr(0)
        a2_delta_hmix(0)
        b.dma("sync", hb[:], d_hsp[1].rearrange("(b p) t -> p b t", p=128), sxh,
              waits=[(sV, hmr_snapA[0])])
        sx_h[0] = sxh.n
        b.dma("sync", scb[1][:], d_scsp[1].rearrange("(b p) t -> p b t", p=128),
              sxc[1], waits=[(sxc[1], sxc[1].n)])
        sx_sc[1] = sxc[1].n

        for j in range(NT):
            p = j % 2
            hmc = hm[p]
            qmax_issue()
            qmax_issue()
            if j + 1 < NT:
                a2_corr(j + 1)
                a2_delta_hmix(j + 1)
            if j >= 1:
                v(lambda e: e.tensor_tensor(x2b[:], xbuf[0][:, :, 1:T + 1],
                                            va[:], ALU.add),
                  waits=[(sS, woevac[NB - 1]), (sxx, sx_xA2[0]),
                         (sx2, sx_x2free[0])])
                if DBG and j == 1:
                    b.dma("scalar", d_a2dbg[4][0:512, :].rearrange(
                        "(b p) t -> p b t", p=128),
                        va[:, 4:8, :], swm[1], waits=[(sV, sV.n)])
                    b.op("vector", None, waits=[(swm[1], swm[1].n)],
                         incs=[(sV, 1)])
                x2add_snap[0] = sV.n
                b.dma("sync",
                      d_x2sp[:, (j - 1) * T:j * T].rearrange("(b p) t -> p b t", p=128),
                      x2b[:], sx2, waits=[(sV, x2add_snap[0])])
                sx_x2free[0] = sx2.n
                b.dma("sync", xbuf[0][:, :, 1:T + 1], x_cols_b(j), sxx,
                      waits=[(sV, x2add_snap[0])])
                sx_xA2[0] = sxx.n
            gevac = {}
            for m in range(NB):
                slot = wslot()
                mm_group(m % 4, wch, slot * 1024, hmc[0],
                         wait_extra=[wl_snap[slot], (sV, hmg_snapA[p])] +
                                    ([(sV, gnchain_snap[0]),
                                      (sS, woevac[4 + m % 4])] if m < 4
                                     else [(sS, gevac[m - 4])]))
                wadvance()
                snap = sT.n
                s(lambda e, m=m: e.activation(ka[:, m, :],
                                              pmm[:, (m % 4) * 512:(m % 4 + 1) * 512],
                                              AF.Tanh, scale=0.5, bias=vb("bg2", m)),
                  waits=[(sT, snap)] + ([(sV, gnchain_snap[0])] if m == 0 else []))
                gevac[m] = sS.n
            revac = {}
            for m in range(NB):
                slot = wslot()
                mm_group(m % 4, wch, slot * 1024, hmc[1],
                         wait_extra=[wl_snap[slot], (sV, hmr_snapA[p]),
                                     (sS, gevac[NB - 1] if m < 4 else revac[m - 4])])
                wadvance()
                snap = sT.n
                s(lambda e, m=m: e.activation(va[:, m, :],
                                              pmm[:, (m % 4) * 512:(m % 4 + 1) * 512],
                                              AF.Identity, bias=vb("br", m)),
                  waits=[(sT, snap)] +
                        ([(sV, x2add_snap[0] if j >= 1 else sV.n)] if m == 0 else []))
                revac[m] = sS.n
            rlad_snap = sT.n
            if j >= 1:
                s(lambda e, hmc=hmc: e.activation(hmc[0][:], x2b[:], AF.Copy),
                  waits=[(sV, x2add_snap[0]), (sT, rlad_snap)])
                s(lambda e, hmc=hmc: e.activation(hmc[1][:], x2b[:], AF.Square))
                ln2cast_snap[0] = sS.n
                ln2_vsnap[0] = ln_stats(hmc[0], hmc[1])
                ln2sums_snap[0] = sT.n
                b.dma("sync",
                      d_bounce[(j - 1) * 512:j * 512].rearrange("t one -> one t"),
                      rows[0:1, 0:512], sxr, waits=[(sV, sV.n)])
                b.dma("sync",
                      d_bounce[(8 + j - 1) * 512:(9 + j - 1) * 512].rearrange("t one -> one t"),
                      rows[0:1, 512:1024], sxr)
            g(lambda e, p=p: e.tensor_tensor(scb[p][:], va[:], scb[p][:], ALU.mult),
              waits=[(sS, revac[NB - 1]), (sV, corr_snapA[p])])
            sG_rw[p] = sG.n
            s(lambda e, p=p, hmc=hmc: e.activation(hmc[0][:], scb[p][:], AF.Copy),
              waits=[(sG, sG_rw[p]), (sT, ln2sums_snap[0] if j >= 1 else rlad_snap)])
            s(lambda e, p=p, hmc=hmc: e.activation(hmc[1][:], scb[p][:], AF.Square))
            sS_rwc = sS.n
            for bb in range(NB):
                t(lambda e, bb=bb, hmc=hmc: e.matmul(pst[0:16, :], gsel[:, bb * 16:(bb + 1) * 16],
                                            hmc[0][:, bb, :], start=(bb == 0),
                                            stop=(bb == 7)),
                  waits=[(sS, sS_rwc), (sV, ln2_vsnap[0] if j >= 1 else sV.n),
                         (sgc, sgc_const)] if bb == 0 else [],
                  incs=[])
            for bb in range(NB):
                t(lambda e, bb=bb, hmc=hmc: e.matmul(pmo[0:16, :], gsel[:, bb * 16:(bb + 1) * 16],
                                            hmc[1][:, bb, :], start=(bb == 0),
                                            stop=(bb == 7)),
                  incs=[(sT, 1)] if bb == 7 else [])
            sT_gn = sT.n
            v(lambda e: e.tensor_scalar_mul(rows[0:16, 1024:1536], pst[0:16, :],
                                            1.0 / HD),
              waits=[(sT, sT_gn), (sxr, sxr.n)])
            v(lambda e: e.tensor_scalar_mul(rows[0:16, 1536:2048], pmo[0:16, :],
                                            1.0 / HD))
            v(lambda e: e.tensor_mul(rows[0:16, 0:512], rows[0:16, 1024:1536],
                                     rows[0:16, 1024:1536]))
            v(lambda e: e.tensor_sub(rows[0:16, 1536:2048], rows[0:16, 1536:2048],
                                     rows[0:16, 0:512]))
            sV_gn1 = sV.n
            s(lambda e: e.activation(rows[0:16, 0:512], rows[0:16, 1536:2048], AF.Ln,
                                     bias=misc[0:16, 2:3]), waits=[(sV, sV_gn1)])
            s(lambda e: e.activation(rows[0:16, 0:512], rows[0:16, 0:512], AF.Exp,
                                     scale=-0.5))
            sS_gn = sS.n
            v(lambda e: e.tensor_mul(rows[0:16, 512:1024], rows[0:16, 1024:1536],
                                     rows[0:16, 0:512]),
              waits=[(sS, sS_gn)])
            sV_gn2 = sV.n
            if DBG and j == 0:
                b.dma("sync", d_a2dbg[4][0:16, :], rows[0:16, 0:512], sxb,
                      waits=[(sV, sV_gn2)])
                b.dma("sync", d_a2dbg[4][16:32, :], rows[0:16, 512:1024], sxb)
                b.op("vector", None, waits=[(sxb, sxb.n)], incs=[(sV, 1)])
                b.op("scalar", None, waits=[(sxb, sxb.n)], incs=[(sS, 1)])
            s(lambda e: e.activation(ka[:], ka[:], AF.Identity, bias=mc(0)),
              waits=[(sT, sT.n)])
            sS_ka1 = sS.n
            if DBG and j == 0:
                b.dma("sync", d_a2dbg[0].rearrange("(b p) t -> p b t", p=128),
                      ka[:], sxb, waits=[(sS, sS_ka1)])
                b.dma("sync", d_a2dbg[1].rearrange("(b p) t -> p b t", p=128),
                      va[:], sxb, waits=[(sS, revac[NB - 1])])
                b.dma("sync", d_a2dbg[2].rearrange("(b p) t -> p b t", p=128),
                      scb[p][:], sxb, waits=[(sG, sG_rw[p])])
                b.op("scalar", None, waits=[(sxb, sxb.n)], incs=[(sS, 1)])
                b.op("vector", None, waits=[(sxb, sxb.n)], incs=[(sV, 1)])
                b.op("gpsimd", None, waits=[(sxb, sxb.n)], incs=[(sG, 1)])
            gnchain = {}
            for bb in range(NB):
                rot = bb % 3
                tgt = (pbc, pmm, pmm)[rot]
                o0 = (0, 0, 2 * 512)[rot]
                o1 = o0 + 512
                t(lambda e, bb=bb, tgt=tgt, o0=o0: e.matmul(
                    tgt[:, o0:o0 + 512], gselT[:, bb * 128:(bb + 1) * 128],
                    rows[0:16, 0:512], start=True, stop=True),
                  waits=[(sV, sV_gn2 if bb < 3 else gnchain[bb - 3])], incs=[])
                t(lambda e, bb=bb, tgt=tgt, o1=o1: e.matmul(
                    tgt[:, o1:o1 + 512], gselT[:, bb * 128:(bb + 1) * 128],
                    rows[0:16, 512:1024], start=True, stop=True), incs=[(sT, 1)])
                snapT = sT.n
                v(lambda e, bb=bb, p=p, tgt=tgt, o0=o0: e.tensor_tensor(
                    tmp[:, 0:T], scb[p][:, bb, :], tgt[:, o0:o0 + 512], ALU.mult),
                  waits=[(sT, snapT), (sG, sG_rw[p])] +
                        ([(sS, sS_ka1)] if bb == 0 else []))
                v(lambda e, tgt=tgt, o1=o1: e.tensor_tensor(
                    tmp[:, T:2 * T], tmp[:, 0:T], tgt[:, o1:o1 + 512], ALU.subtract))
                v(lambda e, bb=bb, hmc=hmc: e.tensor_tensor(
                    hmc[1][:, bb, :], tmp[:, T:2 * T], ka[:, bb, :], ALU.mult))
                gnchain[bb] = sV.n
            sV_apre = sV.n
            gnchain_snap[0] = sV_apre
            if DBG and j == 0:
                for bb2 in range(NB):
                    s(lambda e, bb2=bb2, hmc=hmc: e.activation(x2b[:, bb2, :],
                                                      hmc[1][:, bb2, :], AF.Copy),
                      waits=[(sV, sV_apre)] if bb2 == 0 else [])
                b.dma("sync", d_a2dbg[3].rearrange("(b p) t -> p b t", p=128),
                      x2b[:], sxb, waits=[(sS, sS.n)])
                b.op("vector", None, waits=[(sxb, sxb.n)], incs=[(sV, 1)])
                b.op("tensor", None, waits=[(sxb, sxb.n)])
            for m in range(NB):
                slot = wslot()
                mm_group(m % 4, wch, slot * 1024, hmc[1],
                         wait_extra=[wl_snap[slot], (sV, sV_apre),
                                     (sS, sS_rwc if m < 4 else woevac[m - 4])])
                wadvance()
                snap = sT.n
                s(lambda e, m=m: e.activation(va[:, m, :],
                                              pmm[:, (m % 4) * 512:(m % 4 + 1) * 512],
                                              AF.Identity, scale=vb("gah", m),
                                              bias=vb("gabo", m)),
                  waits=[(sT, snap), (sV, gnchain[NB - 1]), (sG, sG_rw[p])])
                woevac[m] = sS.n
            olad_free[p] = sT.n
            if DBG and j == 0:
                b.dma("scalar", d_a2dbg[4][512:1024, :].rearrange(
                    "(b p) t -> p b t", p=128),
                    va[:, 0:4, :], swm[1], waits=[(sS, woevac[NB - 1])])
                b.op("vector", None, waits=[(swm[1], swm[1].n)], incs=[(sV, 1)])
            qmax_reduce()
            qmax_reduce()
            qmax_issue()
            qmax_issue()
            if j + 2 < NT:
                b.dma("sync", hb[:], d_hsp[j + 2].rearrange("(b p) t -> p b t", p=128),
                      sxh, waits=[(sV, hmr_snapA[(j + 1) % 2])])
                sx_h[0] = sxh.n
                b.dma("sync", scb[p][:],
                      d_scsp[j + 2].rearrange("(b p) t -> p b t", p=128), sxc[p],
                      waits=[(sxc[p], sxc[p].n), (sV, gnchain_snap[0]),
                             (sS, sS_rwc), (sG, sG_rw[p])])
                sx_sc[p] = sxc[p].n
            qmax_reduce()
            qmax_reduce()

        v(lambda e: e.tensor_tensor(x2b[:], xbuf[0][:, :, 1:T + 1], va[:], ALU.add),
          waits=[(sS, woevac[NB - 1]), (sxx, sx_xA2[0]), (sx2, sx_x2free[0])])
        x2add_snap[0] = sV.n
        b.dma("sync", d_x2sp[:, 7 * T:8 * T].rearrange("(b p) t -> p b t", p=128),
              x2b[:], sx2, waits=[(sV, x2add_snap[0])])
        sx_x2free[0] = sx2.n
        s(lambda e: e.activation(hm[1][0][:], x2b[:], AF.Copy),
          waits=[(sV, x2add_snap[0]), (sT, sT.n)])
        s(lambda e: e.activation(hm[1][1][:], x2b[:], AF.Square))
        _ = ln_stats(hm[1][0], hm[1][1])
        b.dma("sync",
              d_bounce[7 * 512:8 * 512].rearrange("t one -> one t"),
              rows[0:1, 0:512], sxr, waits=[(sV, sV.n)])
        b.dma("sync",
              d_bounce[15 * 512:16 * 512].rearrange("t one -> one t"),
              rows[0:1, 512:1024], sxr)

        for _ in range(128):
            qmax_reduce()
            qmax_issue()
        qmax_reduce()
        b.dma("sync", d_bounce[7168:7296], q_["mxfc"][:], sxb, waits=[(sV, sV.n)])
        b.dma("sync", d_bounce[7296:7424], q_["mxpj"][:], sxb)
        sx_mxw = sxb.n
        b.dma("sync", rows[0:1, 0:128],
              d_bounce[7168:7296].rearrange("p one -> one p"), sxb,
              waits=[(sxb, sx_mxw), (sV, sV.n), (sxr, sxr.n)], slow=True)
        b.dma("sync", rows[0:1, 128:256],
              d_bounce[7296:7424].rearrange("p one -> one p"), sxb, slow=True)
        sx_mx = sxb.n
        v(lambda e: e.reduce_max(rows[0:1, 300:301], rows[0:1, 0:128], AX.X),
          waits=[(sxb, sx_mx)])
        v(lambda e: e.reduce_max(rows[0:1, 301:302], rows[0:1, 128:256], AX.X))
        v(lambda e: e.tensor_scalar_mul(rows[0:1, 302:303], rows[0:1, 300:301], 1.0 / 127.0))
        v(lambda e: e.tensor_scalar_mul(rows[0:1, 303:304], rows[0:1, 301:302], 1.0 / 127.0))
        v(lambda e: e.reciprocal(rows[0:1, 304:305], rows[0:1, 300:301]))
        v(lambda e: e.reciprocal(rows[0:1, 305:306], rows[0:1, 301:302]))
        v(lambda e: e.tensor_scalar_mul(rows[0:1, 304:305], rows[0:1, 304:305], 127.0))
        v(lambda e: e.tensor_scalar_mul(rows[0:1, 305:306], rows[0:1, 305:306], 127.0))
        sV_q = sV.n
        t(lambda e: e.matmul(pmo[:, 0:1], onesK1[:], rows[0:1, 302:303], start=True,
                             stop=True), waits=[(sV, sV_q)], incs=[])
        t(lambda e: e.matmul(pmo[:, 1:2], onesK1[:], rows[0:1, 304:305], start=True,
                             stop=True), incs=[])
        t(lambda e: e.matmul(pmo[:, 2:3], onesK1[:], rows[0:1, 303:304], start=True,
                             stop=True), incs=[])
        t(lambda e: e.matmul(pmo[:, 3:4], onesK1[:], rows[0:1, 305:306], start=True,
                             stop=True), incs=[(sT, 1)])
        sT_q = sT.n
        s(lambda e: e.activation(q_["sfc"][:], pmo[:, 0:1], AF.Identity),
          waits=[(sT, sT_q)])
        s(lambda e: e.activation(q_["invfc"][:], pmo[:, 1:2], AF.Identity))
        s(lambda e: e.activation(q_["spj"][:], pmo[:, 2:3], AF.Identity))
        s(lambda e: e.activation(q_["invpj"][:], pmo[:, 3:4], AF.Identity))
        sS_q = sS.n
        b.op("sync", None,
             waits=[(c, c.n) for c in (sxb, sxx, sxh, sxhs, sxc[0], sxc[1],
                                       sx2, sxr, sxps)], incs=[(bar, 1)])
        b.op("gpsimd", None,
             waits=[(sG, sG.n)] + [(c, c.n) for c in sgw + [sgc, sgq, sgq2]],
             incs=[(bar, 1)])
        b.op("vector", None, waits=[(sV, sV.n)], incs=[(bar, 1)])
        b.op("scalar", None,
             waits=[(sS, sS_q)] + [(c, c.n) for c in swS + swm],
             incs=[(bar, 1)])
        b.op("tensor", None, waits=[(sT, sT.n)], incs=[(bar, 1)])

    es_b = ExitStack()
    with es_b:
        sbb = lambda n, sh, dt=F32: es_b.enter_context(nc.sbuf_tensor(n, sh, dt))
        wfce = sbb("wfce", [128, NB * FF], BF16)
        wpjc = sbb("wpjc", [128, 2 * NFB * 128], BF16)
        wps = sbb("wps", [128, 2 * 1024], BF16)
        h1 = sbb("h1", [128, NFB * 512], BF16)
        h2 = sbb("h2", [128, NB, T], BF16)
        x2t = sbb("x2t", [128, NB, T])
        outt = sbb("outt", [128, NB, T])
        qw = sbb("qw", [128, 2 * 1024])
        qt1 = sbb("qt1", [128, 1024])
        lafc = sbb("lafc", [16, D], BF16)
        lbpjT = sbb("lbpjT", [16, D], BF16)
        lbf_s = sbb("lbf_s", [16, 1024], BF16)
        lap_s = sbb("lap_s", [16, 2 * 128], BF16)

        bw = [(bar, 5)]
        first = {"sync": bw, "gpsimd": bw, "vector": bw, "scalar": bw, "tensor": bw}

        def v(fn, waits=(), incs=((sV, 1),)):
            w = list(waits) + first.pop("vector", [])
            b.op("vector", fn, waits=w, incs=incs)

        def s(fn, waits=(), incs=((sS, 1),)):
            w = list(waits) + first.pop("scalar", [])
            b.op("scalar", fn, waits=w, incs=incs)

        def t(fn, waits=(), incs=()):
            w = list(waits) + first.pop("tensor", [])
            b.op("tensor", fn, waits=w, incs=incs)

        def g(fn, waits=(), incs=((sG, 1),)):
            w = list(waits) + first.pop("gpsimd", [])
            b.op("gpsimd", fn, waits=w, incs=incs)

        def dma_q(eng, out_ap, in_ap, sem, waits=()):
            w = list(waits) + first.pop(eng, [])
            b.op(eng, lambda e, o=out_ap, i=in_ap: e.dma_start(out=o, in_=i),
                 waits=w, incs=[(sem, 16)], is_dma=True)

        dma_q("gpsimd", lafc[:], d_afc, sgc)
        dma_q("gpsimd", lbpjT[:], d_bpjT, sgc)
        sg_lora = sgc.n
        v(lambda e: e.tensor_scalar_mul(lafc[:], lafc[:], 2.0), waits=[(sgc, sg_lora)])
        v(lambda e: e.tensor_scalar_mul(lbpjT[:], lbpjT[:], 2.0))
        sV_lora = sV.n

        qunit_v = {}
        lbf_snap = [0]

        def quant_unit(kind, og, i, uidx):
            if kind == "fc":
                wsrc = d_wfc[i * 128:(i + 1) * 128, og * 1024:(og + 1) * 1024]
                stat = lafc[:, i * 128:(i + 1) * 128]
                mov0, mov1 = lbf_s[:, 0:512], lbf_s[:, 512:1024]
                dst = wfce[:, i * FF + og * 1024: i * FF + (og + 1) * 1024]
                sc, iv = q_["sfc"], q_["invfc"]
                wmov = [(sgl[2], lbf_snap[0])]
            else:
                wsrc = d_wpj[i * 128:(i + 1) * 128, 0:1024]
                half2 = i % 2
                dma_q("gpsimd", lap_s[:, half2 * 128:(half2 + 1) * 128],
                      d_apj[:, i * 128:(i + 1) * 128], sgl[half2],
                      waits=[(sT, qunit_t.get(uidx - 2, 0))])
                stat = lap_s[:, half2 * 128:(half2 + 1) * 128]
                mov0, mov1 = lbpjT[:, 0:512], lbpjT[:, 512:1024]
                dst = wps[:, half2 * 1024:(half2 + 1) * 1024]
                sc, iv = q_["spj"], q_["invpj"]
                wmov = [(sgl[half2], sgl[half2].n), (sV, sV_lora),
                        (sxp[half2], wps_free[half2])]
            half = uidx % 2
            dma_q("sync", qw[:, half * 1024:(half + 1) * 1024], wsrc, sxq[half],
                  waits=[(sV, qunit_v.get(uidx - 2, 0))])
            snapx = (sxq[half], sxq[half].n)
            t(lambda e, stat=stat, mov0=mov0: e.matmul(
                pbc[:, 0:512], stat, mov0, start=True, stop=True),
              waits=[(sV, qunit_v.get(uidx - 1, 0)), (sV, sV_lora)] + wmov, incs=[])
            t(lambda e, stat=stat, mov1=mov1: e.matmul(
                pbc[:, 512:1024], stat, mov1, start=True, stop=True),
              incs=[(sT, 1)])
            snapT = sT.n
            qunit_t[uidx] = snapT
            s(lambda e, half=half, iv=iv: e.activation(
                qt1[:], qw[:, half * 1024:(half + 1) * 1024], AF.Identity,
                scale=iv[:], bias=q_["Mpp"][:]),
              waits=[snapx, (sV, qunit_v.get(uidx - 1, 0))])
            snapS = sS.n
            v(lambda e, sc=sc: e.tensor_scalar(qt1[:], qt1[:], q_["Mpp"][:], sc[:],
                                               ALU.subtract, ALU.mult),
              waits=[(sS, snapS)])
            v(lambda e, dst=dst: e.tensor_tensor(dst, qt1[:], pbc[:, 0:1024], ALU.add),
              waits=[(sT, snapT)])
            qunit_v[uidx] = sV.n
            if kind == "pj":
                for mo in range(NB):
                    b.dma("sync", d_wpjbf[mo, :, i * 128:(i + 1) * 128],
                          dst[:, mo * 128:(mo + 1) * 128], sxp[half2],
                          waits=[(sV, qunit_v[uidx])] if mo == 0 else [])
                wps_free[half2] = sxp[half2].n

        qunit_t = {}
        wps_free = [0, 0]
        uidx = 0
        for og in range(4):
            dma_q("gpsimd", lbf_s[:], d_bfcT[:, og * 1024:(og + 1) * 1024], sgl[2],
                  waits=[(sT, qunit_t.get(uidx - 1, 0))])
            lbf_snap[0] = sgl[2].n
            for i in range(NB):
                quant_unit("fc", og, i, uidx)
                uidx += 1
        fc_ready = {og: qunit_v[(og + 1) * NB - 1] for og in range(4)}
        for i in range(NFB):
            quant_unit("pj", 0, i, uidx)
            uidx += 1
        pj_ready = (sxp[0].n, sxp[1].n)

        sx_x2 = [0]
        wpjc_used = [0, 0]
        dma_q("sync", x2t[:], d_x2sp[:, 0:T].rearrange("(b p) t -> p b t", p=128), sxx)
        sx_x2[0] = sxx.n
        outfree = [0]
        rowsnap = [0]
        for j in range(NT):
            sx_xj = sx_x2[0]
            dma_q("sync", rows[0:1, 0:512],
                  d_bounce[j * 512:(j + 1) * 512].rearrange("t one -> one t"),
                  sxr, waits=[(sT, rowsnap[0])])
            dma_q("sync", rows[0:1, 512:1024],
                  d_bounce[(8 + j) * 512:(9 + j) * 512].rearrange("t one -> one t"),
                  sxr)
            sx_rows = sxr.n
            t(lambda e: e.matmul(pmo[:, 0:512], onesK1[:], rows[0:1, 0:512],
                                 start=True, stop=True),
              waits=[(sxr, sx_rows)], incs=[(sT, 1)])
            snapA = sT.n
            v(lambda e: e.tensor_tensor(outt[:], x2t[:], bc3(pmo[:, 0:512]), ALU.mult),
              waits=[(sT, snapA), (sxx, sx_xj), (sout, outfree[0])])
            snapVA = sV.n
            t(lambda e: e.matmul(pmo[:, 0:512], onesK1[:], rows[0:1, 512:1024],
                                  start=True, stop=True),
              waits=[(sV, snapVA)], incs=[(sT, 1)])
            snapB = sT.n
            rowsnap[0] = sT.n
            v(lambda e: e.tensor_tensor(outt[:], outt[:], bc3(pmo[:, 0:512]),
                                        ALU.subtract), waits=[(sT, snapB)])
            for bb in range(NB):
                v(lambda e, bb=bb: e.tensor_scalar(h2[:, bb, :], outt[:, bb, :],
                                                   vb("cb1", bb), vb("sb_", bb),
                                                   ALU.mult, ALU.add))
            sV_h2 = sV.n
            fcev = {}
            for m in range(NFB):
                og = m // 8
                wts = [(sV, sV_h2), (sV, fc_ready[og])]
                if m >= 4:
                    wts.append((sS, fcev[m - 4]))
                else:
                    wts.append((sS, sS.n))
                for kc in range(NB):
                    t(lambda e, m=m, kc=kc: e.matmul(
                        pmm[:, (m % 4) * 512:(m % 4 + 1) * 512],
                        wfce[:, kc * FF + m * 128: kc * FF + (m + 1) * 128],
                        h2[:, kc, :], start=(kc == 0), stop=(kc == 7)),
                      waits=wts if kc == 0 else [], incs=[(sT, 1)] if kc == 7 else [])
                snap = sT.n
                s(lambda e, m=m: e.activation(
                    h1[:, m * 512:(m + 1) * 512],
                    pmm[:, (m % 4) * 512:(m % 4 + 1) * 512], AF.Gelu,
                    bias=bfcv[:, m:m + 1]), waits=[(sT, snap)])
                fcev[m] = sS.n
            pjev = {}
            for mo in range(NB):
                slot = mo % 2
                b.op("scalar",
                     lambda e, mo=mo, slot=slot: e.dma_start(
                         out=wpjc[:, slot * NFB * 128:(slot + 1) * NFB * 128],
                         in_=d_wpjbf[mo]),
                     waits=[(sxp[0], pj_ready[0]), (sxp[1], pj_ready[1]),
                            (sT, wpjc_used[slot])],
                     incs=[(swp[slot], 16)], is_dma=True)
                wts = [(sS, fcev[NFB - 1]), (swp[slot], swp[slot].n)]
                if mo >= 4:
                    wts.append((sS, pjev[mo - 4]))
                for kc in range(NFB):
                    t(lambda e, mo=mo, kc=kc, slot=slot: e.matmul(
                        pmm[:, (mo % 4) * 512:(mo % 4 + 1) * 512],
                        wpjc[:, slot * NFB * 128 + kc * 128:
                             slot * NFB * 128 + (kc + 1) * 128],
                        h1[:, kc * 512:(kc + 1) * 512], start=(kc == 0),
                        stop=(kc == 31)),
                      waits=wts if kc == 0 else [],
                      incs=[(sT, 1)] if kc == 31 else [])
                wpjc_used[slot] = sT.n
                snap = sT.n
                s(lambda e, mo=mo: e.activation(
                    outt[:, mo, :], pmm[:, (mo % 4) * 512:(mo % 4 + 1) * 512],
                    AF.Identity, scale=vb("gb", mo), bias=vb("gbbp", mo)),
                  waits=[(sT, snap), (sV, sV_h2)])
                pjev[mo] = sS.n
            v(lambda e: e.tensor_tensor(outt[:], outt[:], x2t[:], ALU.add),
              waits=[(sS, pjev[NB - 1])])
            sV_out = sV.n
            if j + 1 < NT:
                dma_q("sync", x2t[:],
                      d_x2sp[:, (j + 1) * T:(j + 2) * T].rearrange("(b p) t -> p b t", p=128),
                      sxx, waits=[(sV, sV_out)])
                sx_x2[0] = sxx.n
            dma_q("sync", d_out[:, j * T:(j + 1) * T].rearrange("(b p) t -> p b t", p=128),
                  outt[:], sout, waits=[(sV, sV_out)])
            outfree[0] = sout.n

        b.op("sync", None, waits=[(sout, sout.n), (sxx, sxx.n)], incs=[(bar, 1)])

    nc2 = b.nc
    for c in ctrs:
        b.sems[c.name] = es.enter_context(nc2.semaphore("sem_" + c.name))

    @block.sync
    def _(eng):
        b.replay("sync", eng)

    @block.gpsimd
    def _(eng):
        b.replay("gpsimd", eng)

    @block.vector
    def _(eng):
        b.replay("vector", eng)

    @block.scalar
    def _(eng):
        b.replay("scalar", eng)

    @block.tensor
    def _(eng):
        b.replay("tensor", eng)


def make_in_maps(inputs, ncore=NCORE):
    x = np.asarray(inputs["L0_attn_x"])
    assert x.shape == (4, 8192, 1024)

    misc = np.zeros((128, 16), np.float32)
    misc[:, 0] = 1.0
    misc[:, 2] = EPS
    misc[:, 5] = M32
    misc[:, 6] = M64 - M32

    gsel = np.zeros((128, NB * 16), np.float32)
    gselT = np.zeros((16, NB * 128), np.float32)
    for bb in range(NB):
        for j2 in range(2):
            head = 2 * bb + j2
            gsel[j2 * 64:(j2 + 1) * 64, bb * 16 + head] = 1.0
            gselT[head, bb * 128 + j2 * 64: bb * 128 + (j2 + 1) * 64] = 1.0

    common = {

        "WfcT": np.ascontiguousarray(np.asarray(inputs["L0_ffn_W_fc"], np.float32).T),
        "WprojT": np.ascontiguousarray(np.asarray(inputs["L0_ffn_W_proj"], np.float32).T),
        "Afc": np.ascontiguousarray(inputs["L0_ffn_A_fc"], np.float32),
        "BfcT": np.ascontiguousarray(np.asarray(inputs["L0_ffn_B_fc"], np.float32).T),
        "Aproj": np.ascontiguousarray(inputs["L0_ffn_A_proj"], np.float32),
        "BprojT": np.ascontiguousarray(np.asarray(inputs["L0_ffn_B_proj"], np.float32).T),

        "bfc": np.ascontiguousarray(
            np.asarray(inputs["L0_ffn_b_fc"], np.float32).reshape(NFB, 128).T),
        "bproj": _fm(inputs["L0_ffn_b_proj"]),
        "misc": misc,
        "onesK1": np.ones((1, 128), np.float32),
        "ramp": np.arange(1, T + 1, dtype=np.float32).reshape(1, T),
        "gsel": gsel,
        "gselT": gselT,
    }
    for nm, key in (("WgT", "L0_attn_Wg_w"), ("WrT", "L0_attn_Wr_w"),
                    ("WkT", "L0_attn_Wk_w"), ("WvT", "L0_attn_Wv_w"),
                    ("WoT", "L0_attn_Wo_w")):
        common[nm] = _sw(np.asarray(inputs[key], np.float32).T).reshape(8, 128, 1024)

    vec_names = ("mg", "mr", "mk", "mv", "bg2", "br", "bk", "bv", "uv",
                 "ca1", "sa", "cb1", "sb_", "ga", "gb", "gah", "gabo", "gbbp",
                 "e", "dec", "d512", "carry", "Din", "hlast", "recvmask",
                 "modsc")
    vslot = {n: i for i, n in enumerate(vec_names)}
    vbase = np.zeros((128, 26 * NB), np.float32)

    def vput(vp, name, val):
        vp[:, vslot[name] * NB:(vslot[name] + 1) * NB] = val

    vput(vbase, "mg", _fm(inputs["L0_attn_mg"]))
    vput(vbase, "mr", _fm(inputs["L0_attn_mr"]))
    vput(vbase, "mk", _fm(inputs["L0_attn_mk"]))
    vput(vbase, "mv", _fm(inputs["L0_attn_mv"]))
    vput(vbase, "bg2", _fm(inputs["L0_attn_Wg_b"]))
    vput(vbase, "br", _fm(inputs["L0_attn_Wr_b"]))
    vput(vbase, "bk", _fm(inputs["L0_attn_Wk_b"]))
    vput(vbase, "bv", _fm(inputs["L0_attn_Wv_b"]))
    vput(vbase, "uv", _fm(np.asarray(inputs["L0_attn_u"]).reshape(-1)))
    vput(vbase, "gabo", _fm(inputs["L0_attn_Wo_b"]))
    vput(vbase, "e", _fm(np.asarray(inputs["L0_attn_w"]).reshape(-1)))

    wmod_f = np.asarray(inputs["L0_wmod"], np.float32)
    in_maps = []
    for c in range(NCORE):
        q, half = c // 2, c % 2
        t0 = half * TOK
        xT = np.ascontiguousarray(x[q].T)
        lead = xT[:, (t0 - 1) % 8192:(t0 - 1) % 8192 + 1]
        m = dict(common)
        m["wmodS"] = np.ascontiguousarray(wmod_f[:, half * 3072:(half + 1) * 3072])
        m["xT"] = np.ascontiguousarray(np.concatenate([lead, xT[:, t0:t0 + TOK]], axis=1))
        vp = vbase.copy()
        vput(vp, "modsc", _fm(np.asarray(inputs["L0_c"])[q]))
        vput(vp, "recvmask", np.full((128, NB), float(half), np.float32))
        m["vpack_in"] = vp
        in_maps.append(m)
    return in_maps[:ncore]


def kernel(**inputs):
    in_maps = make_in_maps(inputs)
    bprog = build_program()
    res = run_bass_kernel_spmd(bprog.nc, in_maps, core_ids=list(range(NCORE)))

    out = np.empty((4, 8192, 1024), np.float32)
    for c in range(NCORE):
        q, half = c // 2, c % 2
        t0 = half * TOK
        out[q, t0:t0 + TOK, :] = res.results[c]["outT"].T
    if DBG:
        kernel._dbg = res.results
    return out


# version 13 (final)
# speedup vs baseline: 1.1138x; 1.0511x over previous
import os
import numpy as np
from contextlib import ExitStack

import concourse.bass as bass
import concourse.mybir as mybir
from concourse.bass_utils import run_bass_kernel_spmd

F32 = mybir.dt.float32
BF16 = mybir.dt.bfloat16
AF = mybir.ActivationFunctionType
ALU = mybir.AluOpType
AX = mybir.AxisListType

NCORE = 8
D = 1024
NB = 8
T = 512
NT = 8
TOK = NT * T
H, HD = 16, 64
FF = 4096
NFB = 32
EPS = 1e-5
M32 = 12582912.0
M64 = 6755399441055744.0

DBG = os.environ.get("MK_DBG", "") == "1"


class Ctr:

    def __init__(self, name):
        self.name = name
        self.n = 0


class B:
    self_sync = {"vector", "scalar", "gpsimd"}

    def __init__(self):
        self.nc = bass.Bass("TRN2", target_bir_lowering=False, debug=False)
        self.items = []
        self.sems = {}
        self.eng_ctr = {}

    def op(self, eng, fn, waits=(), incs=(), is_dma=False):
        incs = list(incs)
        ws = [(c, v) for (c, v) in waits if v > 0]
        if eng in B.self_sync and fn is not None and not is_dma:
            ctr = self.eng_ctr.get(eng)
            if ctr is not None:
                ws.append((ctr, ctr.n))
                if not any(c is ctr for c, _ in incs):
                    incs.append((ctr, 1))
        for c, d in incs:
            c.n += d
        self.items.append((eng, fn, ws, incs))

    def dma(self, eng, out_ap, in_ap, sem, waits=(), slow=False):
        self.op(eng, lambda e, o=out_ap, i=in_ap, s=slow: e.dma_start(
            out=o, in_=i, allow_slow_non_contiguous=s),
                waits=waits, incs=[(sem, 16)], is_dma=True)

    def replay(self, eng_name, eng):
        for name, fn, waits, incs in self.items:
            if name != eng_name:
                continue
            for c, v in waits:
                eng.wait_ge(self.sems[c.name], v)
            instr = fn(eng) if fn is not None else None
            if incs:
                if instr is None:
                    instr = eng.nop()
                for c, d in incs:
                    instr = instr.then_inc(self.sems[c.name], d)


def _fm(v):
    return np.ascontiguousarray(np.asarray(v, np.float32).reshape(NB, 128).T)


def _sw(w):
    return np.ascontiguousarray(
        np.asarray(w, np.float32).reshape(8, 128, 8, 128).transpose(2, 1, 0, 3)
    ).reshape(8, 128, 1024)


def build_program():
    b = B()
    nc = b.nc
    es = ExitStack()
    with es:
        _build(b, es)
    return b


def _build(b, es):
    nc = b.nc
    dget = lambda n, sh, dt=F32, k="ExternalInput": nc.dram_tensor(n, sh, dt, kind=k).ap()
    d_x = dget("xT", [D, TOK + 1])
    d_w = {n: dget(n, [8, 128, 1024]) for n in ("WgT", "WrT", "WkT", "WvT", "WoT")}
    d_wmod = dget("wmodS", [D, 3 * D])
    d_mgi = dget("mgi", [3 * D, 1], F32, "Internal")
    d_mgo = dget("mgo", [6 * D, 1], F32, "Internal")
    d_wfc = dget("WfcT", [D, FF])
    d_wpj = dget("WprojT", [FF, D])
    d_afc = dget("Afc", [16, D])
    d_bfcT = dget("BfcT", [16, FF])
    d_apj = dget("Aproj", [16, FF])
    d_bpjT = dget("BprojT", [16, D])
    d_vpack = dget("vpack_in", [128, 26 * NB])
    d_bfc = dget("bfc", [128, NFB])
    d_bpj = dget("bproj", [128, NB])
    d_misc = dget("misc", [128, 16])
    d_onesK1 = dget("onesK1", [1, 128])
    d_ramp = dget("ramp", [1, T])
    d_gsel = dget("gsel", [128, NB * 16])
    d_gselT = dget("gselT", [16, NB * 128])
    d_out = dget("outT", [D, TOK], F32, "ExternalOutput")

    kd = "ExternalOutput" if DBG else "Internal"
    d_hsp = dget("hsp", [NT, D, T + 1], F32, kd)
    d_scsp = dget("scsp", [NT, D, T], F32, kd)
    d_x2sp = dget("x2sp", [D, TOK], F32, kd)
    d_a2dbg = dget("a2dbg", [5, D, T], F32, kd)
    d_wbf = dget("wbf", [5, 8, 128, 1024], BF16, "Internal")
    d_wpjbf = dget("wpjbf", [NB, 128, NFB * 128], BF16, "Internal")
    d_powsp = dget("powsp", [D, T], F32, "Internal")
    d_ccin = dget("ccin", [D, 1], F32, "Internal")
    d_ccout = dget("ccout", [2 * D, 1], F32, "Internal")
    d_bounce = dget("bounce", [8192, 1], F32, "Internal")

    sT = Ctr("sT"); sV = Ctr("sV"); sS = Ctr("sS"); sG = Ctr("sG")
    scc = Ctr("scc"); bar = Ctr("bar")
    sxb = Ctr("sxb"); sxx = Ctr("sxx"); sxh = Ctr("sxh"); sxhs = Ctr("sxhs")
    sxc = [Ctr("sxc0"), Ctr("sxc1")]
    sx2 = Ctr("sx2"); sout = Ctr("sout"); sxr = Ctr("sxr")
    sxp = [Ctr("sxp0"), Ctr("sxp1")]
    sxq = [Ctr("sxq0"), Ctr("sxq1")]
    swm = [Ctr("swm0"), Ctr("swm1")]
    swS = [Ctr(f"swS{i}") for i in range(6)]
    swp = [Ctr("swp0"), Ctr("swp1")]
    sgw = [Ctr(f"sgw{i}") for i in range(5)]
    sgc = Ctr("sgc"); sgq = Ctr("sgq"); sgq2 = Ctr("sgq2")
    sgl = [Ctr("sgl0"), Ctr("sgl1"), Ctr("sgl2")]
    sxps = Ctr("sxps")
    ctrs = ([sT, sV, sS, sG, scc, bar, sxb, sxx, sxh, sxhs, sx2, sout, sxr,
             sgc, sgq, sgq2, sxps] + sxp + sxc + sxq + swm + swS + swp + sgw + sgl)
    b.eng_ctr = {"vector": sV, "scalar": sS, "gpsimd": sG}

    sb = lambda n, sh, dt=F32: es.enter_context(nc.sbuf_tensor(n, sh, dt))
    misc = sb("s_misc", [128, 16])
    onesK1 = sb("s_onesK1", [1, 128])
    ones128b = sb("ones128b", [128, 1], BF16)
    gsel = sb("s_gsel", [128, NB * 16], BF16)
    rows = sb("rows", [128, 2048])
    _vec_names = ("mg", "mr", "mk", "mv", "bg2", "br", "bk", "bv", "uv",
                  "ca1", "sa", "cb1", "sb_", "ga", "gb", "gah", "gabo", "gbbp",
                  "e", "dec", "d512", "carry", "Din", "hlast", "recvmask",
                  "modsc")
    vpack = sb("vpack", [128, len(_vec_names) * NB])
    vec = {n: vpack[:, i * NB:(i + 1) * NB]
           for i, n in enumerate(_vec_names)}
    siluc = sb("siluc", [128, NB])
    bfcv = sb("bfcv", [128, NFB])
    bpjv = sb("bpjv", [128, NB])
    q_ = {}
    for n in ("delta", "Mpp", "sfc", "invfc", "spj", "invpj", "mxfc", "mxpj",
              "t1", "t2", "m11", "s11f", "s11p", "i11f", "i11p"):
        q_[n] = sb("q_" + n, [128, 1])


    pmm = es.enter_context(nc.psum_tensor("pmm", [128, 4 * 512], F32))
    pbc = es.enter_context(nc.psum_tensor("pbc", [128, 2 * 512], F32))
    pst = es.enter_context(nc.psum_tensor("pst", [128, 512], F32))
    pmo = es.enter_context(nc.psum_tensor("pmo", [128, 512], F32))

    block = es.enter_context(nc.Block())

    mc = lambda c: misc[:, c:c + 1]
    vb = lambda name, bb: vec[name][:, bb:bb + 1]

    def bc3(ps_slice):
        return ps_slice.unsqueeze(1).broadcast_to([128, NB, T])

    es_a = ExitStack()
    with es_a:
        sba = lambda n, sh, dt=F32: es_a.enter_context(nc.sbuf_tensor(n, sh, dt))
        xbuf = [sba(f"xb{i}", [128, NB, T + 1]) for i in range(2)]
        hb = sba("hb", [128, NB, T + 1])
        hm = [(sba("hmA0", [128, NB, T], BF16), sba("hmB0", [128, NB, T], BF16)),
              (sba("hmA1", [128, NB, T], BF16), sba("hmB1", [128, NB, T], BF16))]
        ka = sba("ka", [128, NB, T])
        va = sba("va", [128, NB, T])
        scb = [sba(f"sc{i}", [128, NB, T]) for i in range(2)]
        x2b = sba("x2b", [128, NB, T])
        pow_ = xbuf[1][:, :, 0:T]
        wch = sba("wch", [128, 6 * 1024], BF16)
        ramp = sba("s_ramp", [1, T])
        gselT = sba("s_gselT", [16, NB * 128])
        qst = sba("qst", [128, 2 * 1024])
        tmp = sba("tmp", [128, 2 * T])

        b.dma("sync", misc[:], d_misc, sxb)
        b.dma("sync", onesK1[:], d_onesK1, sxb)
        b.dma("sync", ramp[:], d_ramp, sxb)
        b.dma("sync", gselT[:], d_gselT, sxb)
        b.dma("sync", vpack[:], d_vpack, sxb)
        b.dma("sync", bfcv[:], d_bfc, sxb)
        b.dma("sync", bpjv[:], d_bpj, sxb)
        n_setup_sync = sxb.n
        b.dma("gpsimd", gsel[:], d_gsel, sgc)
        b.dma("gpsimd", ones128b[:], d_onesK1.rearrange("one p -> p one"), sgc)
        sgc_const = sgc.n
        b.dma("gpsimd", d_wbf[2], d_w["WkT"], sgw[2])
        b.dma("gpsimd", d_wbf[3], d_w["WvT"], sgw[3])
        b.dma("gpsimd", d_wbf[0], d_w["WgT"], sgw[0])
        b.dma("gpsimd", d_wbf[1], d_w["WrT"], sgw[1])
        b.dma("gpsimd", d_wbf[4], d_w["WoT"], sgw[4])
        sg_w = {"g": (sgw[0], 16), "r": (sgw[1], 16), "k": (sgw[2], 16),
                "v": (sgw[3], 16), "o": (sgw[4], 16)}

        wait_c = [(sxb, n_setup_sync)]

        def v(fn, waits=(), incs=((sV, 1),)):
            b.op("vector", fn, waits=waits, incs=incs)

        def s(fn, waits=(), incs=((sS, 1),)):
            b.op("scalar", fn, waits=waits, incs=incs)

        def t(fn, waits=(), incs=()):
            b.op("tensor", fn, waits=waits, incs=incs)

        def g(fn, waits=(), incs=((sG, 1),)):
            b.op("gpsimd", fn, waits=waits, incs=incs)

        v(lambda e: e.tensor_scalar(q_["t1"][:], mc(0), 2.0 ** -30, None, ALU.add),
          waits=wait_c)
        v(lambda e: e.tensor_scalar(q_["delta"][:], q_["t1"][:], 1.0, 2.0 ** 30,
                                    ALU.subtract, ALU.mult))
        v(lambda e: e.tensor_scalar(q_["Mpp"][:], q_["delta"][:], mc(6), mc(5),
                                    ALU.mult, ALU.add))
        v(lambda e: e.tensor_scalar_mul(vec["bg2"], vec["bg2"], 0.5), waits=wait_c)
        s(lambda e: e.activation(vec["e"], vec["e"], AF.Exp), waits=wait_c)
        s(lambda e: e.activation(vec["dec"], vec["e"], AF.Exp, scale=-1.0))
        sS_dec = sS.n
        s(lambda e: e.activation(siluc[:], vec["modsc"], AF.Silu), waits=wait_c)
        sS_silu = sS.n
        v(lambda e: e.memset(vec["carry"], 0.0))
        v(lambda e: e.memset(q_["mxfc"][:], 0.0))
        v(lambda e: e.memset(q_["mxpj"][:], 0.0))

        b.dma("sync", d_bounce[6144:7168].rearrange("(b p) one -> p (b one)", p=128),
              vec["e"], sxb, waits=[(sS, sS_dec)], slow=True)
        sx_eb = sxb.n
        b.dma("sync", rows[0:1, 1024:2048],
              d_bounce[6144:7168].rearrange("(b p) one -> one (b p)", p=128),
              sxb, waits=[(sxb, sx_eb)], slow=True)
        sx_erow = sxb.n
        for bb in range(NB):
            half = bb % 2
            t(lambda e, bb=bb, half=half: e.matmul(
                pbc[:, half * 512:(half + 1) * 512],
                rows[0:1, 1024 + bb * 128:1024 + (bb + 1) * 128], ramp[:],
                start=True, stop=True),
              waits=[(sxb, sx_erow), (sS, sS.n)], incs=[(sT, 1)])
            s(lambda e, bb=bb, half=half: e.activation(
                pow_[:, bb, :], pbc[:, half * 512:(half + 1) * 512], AF.Exp,
                scale=-1.0), waits=[(sT, sT.n)])
        sS_pow = sS.n
        v(lambda e: e.tensor_copy(vec["d512"], pow_[:, :, T - 1]),
          waits=[(sS, sS_pow)])
        sV_d512 = sV.n
        b.dma("sync", d_powsp.rearrange("(b p) t -> p b t", p=128), pow_,
              sxps, waits=[(sS, sS_pow), (sV, sV_d512)])
        sx_powsp = sxps.n

        wm_used = [0, 0]
        for sl in range(6):
            for kc in range(8):
                wh = (sl * 8 + kc) % 2
                b.dma("scalar", tmp[:, wh * 512:(wh + 1) * 512],
                      d_wmod[kc * 128:(kc + 1) * 128, sl * 512:(sl + 1) * 512],
                      swm[wh], waits=[(sT, wm_used[wh])])
                t(lambda e, kc=kc, wh=wh: e.matmul(
                    pmo[0:1, :], siluc[:, kc:kc + 1], tmp[:, wh * 512:(wh + 1) * 512],
                    start=(kc == 0), stop=(kc == 7)),
                  waits=[(swm[wh], swm[wh].n), (sS, sS_silu)] +
                        ([(sS, sS.n)] if kc == 0 else []),
                  incs=[(sT, 1)])
                wm_used[wh] = sT.n
            s(lambda e, sl=sl: e.activation(rows[0:1, 0:512], pmo[0:1, :], AF.Identity),
              waits=[(sT, sT.n)] + ([(sxb, sxb.n)] if sl > 0 else []))
            b.dma("sync",
                  d_mgi[sl * 512:(sl + 1) * 512].rearrange("t one -> one t"),
                  rows[0:1, 0:512], sxb, waits=[(sS, sS.n)])
        sx_modb = sxb.n
        b.op("gpsimd",
             lambda e: e.collective_compute(
                 "AllGather", ALU.bypass,
                 replica_groups=[[0, 1], [2, 3], [4, 5], [6, 7]],
                 ins=[d_mgi], outs=[d_mgo]),
             waits=[(sxb, sx_modb), (sG, sG.n)], incs=[(scc, 1)], is_dma=True)
        mod_names = ["sa", "ca1", "ga", "sb_", "cb1", "gb"]
        for j, nm in enumerate(mod_names):
            b.dma("sync", vec[nm],
                  d_mgo[j * 1024:(j + 1) * 1024].rearrange("(b p) one -> p (b one)", p=128),
                  sxb, waits=[(scc, 1)], slow=True)
        sx_mod = sxb.n
        v(lambda e: e.tensor_scalar_add(vec["ca1"], vec["ca1"], 1.0),
          waits=[(sxb, sx_mod)])
        v(lambda e: e.tensor_scalar_add(vec["cb1"], vec["cb1"], 1.0))
        for nm in ("ga", "gb"):
            v(lambda e, nm=nm: e.tensor_scalar_max(tmp[:, 0:NB], vec[nm], 0.25))
            v(lambda e, nm=nm: e.tensor_scalar_min(tmp[:, NB:2 * NB], vec[nm], -0.25))
            v(lambda e, nm=nm: e.tensor_single_scalar(tmp[:, 2 * NB:3 * NB], vec[nm],
                                                      0.0, ALU.is_ge))
            v(lambda e: e.tensor_sub(tmp[:, 3 * NB:4 * NB], tmp[:, 0:NB], tmp[:, NB:2 * NB]))
            v(lambda e: e.tensor_mul(tmp[:, 2 * NB:3 * NB], tmp[:, 2 * NB:3 * NB],
                                     tmp[:, 3 * NB:4 * NB]))
            v(lambda e, nm=nm: e.tensor_add(vec[nm], tmp[:, NB:2 * NB],
                                            tmp[:, 2 * NB:3 * NB]))
        v(lambda e: e.tensor_scalar_mul(vec["gah"], vec["ga"], 0.5))
        v(lambda e: e.tensor_mul(vec["gabo"], vec["ga"], vec["gabo"]))
        v(lambda e: e.tensor_mul(vec["gbbp"], vec["gb"], bpjv[:]))
        sV_mod = sV.n

        x_cols = lambda j: d_x[:, j * T: j * T + T + 1].rearrange(
            "(b p) t -> p b t", p=128)
        x_cols_b = lambda j: d_x[:, j * T + 1: (j + 1) * T + 1].rearrange(
            "(b p) t -> p b t", p=128)

        def ln_stats(src_bfA, src_bfB):
            for bb in range(NB):
                t(lambda e, bb=bb: e.matmul(pst[0:1, :], ones128b[:], src_bfA[:, bb, :],
                                            start=(bb == 0), stop=(bb == 7)),
                  waits=[(sS, sS.n), (sV, sV.n), (sgc, sgc_const)] if bb == 0 else [],
                  incs=[])
            for bb in range(NB):
                t(lambda e, bb=bb: e.matmul(pmo[0:1, :], ones128b[:], src_bfB[:, bb, :],
                                            start=(bb == 0), stop=(bb == 7)),
                  incs=[(sT, 1)] if bb == 7 else [])
            snap = sT.n
            v(lambda e: e.tensor_scalar_mul(rows[0:1, 1024:1536], pst[0:1, :], 1.0 / D),
              waits=[(sT, snap), (sxr, sxr.n)])
            v(lambda e: e.tensor_scalar_mul(rows[0:1, 1536:2048], pmo[0:1, :], 1.0 / D))
            v(lambda e: e.tensor_mul(rows[0:1, 0:512], rows[0:1, 1024:1536],
                                     rows[0:1, 1024:1536]))
            v(lambda e: e.tensor_sub(rows[0:1, 1536:2048], rows[0:1, 1536:2048],
                                     rows[0:1, 0:512]))
            snapV = sV.n
            s(lambda e: e.activation(rows[0:1, 0:512], rows[0:1, 1536:2048], AF.Ln,
                                     bias=misc[0:1, 2:3]), waits=[(sV, snapV)])
            s(lambda e: e.activation(rows[0:1, 0:512], rows[0:1, 0:512], AF.Exp,
                                     scale=-0.5))
            snapS = sS.n
            v(lambda e: e.tensor_mul(rows[0:1, 512:1024], rows[0:1, 1024:1536],
                                     rows[0:1, 0:512]),
              waits=[(sS, snapS)])
            return sV.n

        def bcast_rows(cA, cB, waitV):
            t(lambda e, cA=cA: e.matmul(pbc[:, 0:512], onesK1[:],
                                        rows[0:1, cA:cA + 512],
                                        start=True, stop=True),
              waits=[(sV, waitV), (sV, pbc_free[0])], incs=[])
            t(lambda e, cB=cB: e.matmul(pbc[:, 512:1024], onesK1[:],
                                        rows[0:1, cB:cB + 512],
                                        start=True, stop=True), incs=[(sT, 1)])
            return sT.n

        pbc_free = [0]

        def mm_group(dst_bank, wsrc, kslices, moving_blocks, wait_extra=()):
            for kc in range(8):
                wts = list(wait_extra) if kc == 0 else []
                t(lambda e, kc=kc: e.matmul(
                    pmm[:, dst_bank * 512:(dst_bank + 1) * 512],
                    wsrc[:, kslices + kc * 128: kslices + (kc + 1) * 128],
                    moving_blocks[:, kc, :],
                    start=(kc == 0), stop=(kc == 7)),
                  waits=wts, incs=[(sT, 1)] if kc == 7 else [])

        wl_snap = {i: (swS[i], 0) for i in range(6)}

        def wload(proj_idx, m, slot, wait_snap):
            b.dma("scalar", wch[:, slot * 1024:(slot + 1) * 1024],
                  d_wbf[proj_idx, m], swS[slot],
                  waits=[sg_w[("g", "r", "k", "v", "o")[proj_idx]],
                         (sT, wait_snap)])
            wl_snap[slot] = (swS[slot], swS[slot].n)

        wstream = []
        for _ in range(NT):
            wstream += [(2, c) for c in range(NB)] + [(3, c) for c in range(NB)]
        for _ in range(NT):
            wstream += ([(0, c) for c in range(NB)] + [(1, c) for c in range(NB)]
                        + [(4, c) for c in range(NB)])
        wpos = [0]

        def wslot():
            return wpos[0] % 6

        def wadvance():
            p = wpos[0]
            if p + 6 < len(wstream):
                pr, c = wstream[p + 6]
                wload(pr, c, (p + 6) % 6, sT.n)
            wpos[0] += 1

        b.dma("sync", xbuf[0][:], x_cols(0), sxx)
        sx_x0 = sxx.n
        s(lambda e: e.activation(hm[0][0][:, :, 0:1], xbuf[0][:, :, 0:1], AF.Copy),
          waits=[(sxx, sx_x0)])
        s(lambda e: e.activation(hm[0][1][:, :, 0:1], xbuf[0][:, :, 0:1], AF.Square))
        sS_b0 = sS.n
        for bb in range(NB):
            t(lambda e, bb=bb: e.matmul(pst[0:1, 0:1], ones128b[:], hm[0][0][:, bb, 0:1],
                                        start=(bb == 0), stop=(bb == 7)),
              waits=[(sS, sS_b0), (sgc, sgc_const)] if bb == 0 else [], incs=[])
        for bb in range(NB):
            t(lambda e, bb=bb: e.matmul(pmo[0:1, 0:1], ones128b[:], hm[0][1][:, bb, 0:1],
                                        start=(bb == 0), stop=(bb == 7)),
              incs=[(sT, 1)] if bb == 7 else [])
        sT_b0 = sT.n
        v(lambda e: e.tensor_scalar_mul(rows[0:1, 1024:1025], pst[0:1, 0:1], 1.0 / D),
          waits=[(sT, sT_b0)])
        v(lambda e: e.tensor_scalar_mul(rows[0:1, 1536:1537], pmo[0:1, 0:1], 1.0 / D))
        v(lambda e: e.tensor_mul(rows[0:1, 0:1], rows[0:1, 1024:1025],
                                 rows[0:1, 1024:1025]))
        v(lambda e: e.tensor_sub(rows[0:1, 1536:1537], rows[0:1, 1536:1537],
                                 rows[0:1, 0:1]))
        sV_b1 = sV.n
        s(lambda e: e.activation(rows[0:1, 0:1], rows[0:1, 1536:1537], AF.Ln,
                                 bias=misc[0:1, 2:3]), waits=[(sV, sV_b1)])
        s(lambda e: e.activation(rows[0:1, 0:1], rows[0:1, 0:1], AF.Exp, scale=-0.5))
        sS_b1 = sS.n
        v(lambda e: e.tensor_mul(rows[0:1, 512:513], rows[0:1, 1024:1025],
                                 rows[0:1, 0:1]),
          waits=[(sS, sS_b1)])
        sV_b2 = sV.n
        t(lambda e: e.matmul(pbc[:, 0:1], onesK1[:], rows[0:1, 0:1], start=True,
                             stop=True), waits=[(sV, sV_b2)], incs=[])
        t(lambda e: e.matmul(pbc[:, 512:513], onesK1[:], rows[0:1, 512:513], start=True,
                             stop=True), incs=[(sT, 1)])
        sT_b2 = sT.n
        v(lambda e: e.tensor_tensor(vec["hlast"], xbuf[0][:, :, 0],
                                    pbc[:, 0:1].broadcast_to([128, NB]), ALU.mult),
          waits=[(sT, sT_b2)])
        v(lambda e: e.tensor_tensor(vec["hlast"], vec["hlast"],
                                    pbc[:, 512:513].broadcast_to([128, NB]),
                                    ALU.subtract))
        v(lambda e: e.tensor_mul(vec["hlast"], vec["hlast"], vec["ca1"]))
        v(lambda e: e.tensor_add(vec["hlast"], vec["hlast"], vec["sa"]))
        pbc_free[0] = sV.n

        stats_psum_free = [sV.n]
        x2b_free = [0]
        xsum_snap = [0, 0]
        h_snap = [0, 0]
        hm_free = [0, 0]
        hsp_snap = [0]
        uadd_snap = [0]
        kv_snap = [0]
        sV_hmk = [0]
        sV_hmv = [0]
        sc_spilled = [0, 0]
        sx_x = [sx_x0, 0]

        def a1_sq(k, sq_waits=()):
            s(lambda e, k=k: e.activation(x2b[:], xbuf[k % 2][:, :, 1:T + 1],
                                          AF.Square),
              waits=list(sq_waits) + [(sT, x2b_free[0])])

        def a1_stats_rest(k):
            par = k % 2
            xcur = xbuf[par]
            sS_sq = sS.n
            for bb in range(NB):
                t(lambda e, bb=bb: e.matmul(pst[0:1, :], mc(0), xcur[:, bb, 1:T + 1],
                                            start=(bb == 0), stop=(bb == 7)),
                  waits=[(sS, sS_sq), (sV, stats_psum_free[0])] if bb == 0 else [],
                  incs=[])
            for bb in range(NB):
                t(lambda e, bb=bb: e.matmul(pmo[0:1, :], mc(0), x2b[:, bb, :],
                                            start=(bb == 0), stop=(bb == 7)),
                  incs=[(sT, 1)] if bb == 7 else [])
            snap = sT.n
            x2b_free[0] = snap
            xsum_snap[par] = snap
            v(lambda e: e.tensor_scalar_mul(rows[0:1, 1024:1536], pst[0:1, :], 1.0 / D),
              waits=[(sT, snap)])
            v(lambda e: e.tensor_scalar_mul(rows[0:1, 1536:2048], pmo[0:1, :], 1.0 / D))
            stats_psum_free[0] = sV.n
            v(lambda e: e.tensor_mul(rows[0:1, 0:512], rows[0:1, 1024:1536],
                                     rows[0:1, 1024:1536]))
            v(lambda e: e.tensor_sub(rows[0:1, 1536:2048], rows[0:1, 1536:2048],
                                     rows[0:1, 0:512]))
            snapV = sV.n
            s(lambda e: e.activation(rows[0:1, 0:512], rows[0:1, 1536:2048], AF.Ln,
                                     bias=misc[0:1, 2:3]), waits=[(sV, snapV)])
            s(lambda e: e.activation(rows[0:1, 0:512], rows[0:1, 0:512], AF.Exp,
                                     scale=-0.5))
            snapS = sS.n
            v(lambda e: e.tensor_mul(rows[0:1, 512:1024], rows[0:1, 1024:1536],
                                     rows[0:1, 0:512]),
              waits=[(sS, snapS)])
            return bcast_rows(0, 512, sV.n)

        def a1_happly(k, bsn):
            par = k % 2
            xcur = xbuf[par]
            v(lambda e: e.tensor_tensor(hb[:, :, 1:T + 1], xcur[:, :, 1:T + 1],
                                        bc3(pbc[:, 0:512]), ALU.mult),
              waits=[(sT, bsn), (sxhs, hsp_snap[0]),
                     ((sxx, sxh)[par], sx_x[par])])
            v(lambda e: e.tensor_tensor(hb[:, :, 1:T + 1], hb[:, :, 1:T + 1],
                                        bc3(pbc[:, 512:1024]), ALU.subtract))
            pbc_free[0] = sV.n
            for bb in range(NB):
                v(lambda e, bb=bb: e.tensor_scalar(hb[:, bb, 1:T + 1],
                                                   hb[:, bb, 1:T + 1],
                                                   vb("ca1", bb), vb("sa", bb),
                                                   ALU.mult, ALU.add))
            v(lambda e: e.tensor_copy(hb[:, :, 0:1], vec["hlast"].unsqueeze(2)))
            v(lambda e: e.tensor_copy(vec["hlast"].unsqueeze(2), hb[:, :, T:T + 1]))
            h_snap[par] = sV.n
            b.dma("sync", d_hsp[k].rearrange("(b p) t -> p b t", p=128), hb[:],
                  sxhs, waits=[(sV, sV.n)])
            hsp_snap[0] = sxhs.n

        def a1_hmix(k):
            par = k % 2
            db = scb[par]
            v(lambda e, db=db: e.tensor_tensor(db[:], hb[:, :, 1:T + 1],
                                               hb[:, :, 0:T], ALU.subtract),
              waits=[(sxc[par], sc_spilled[par])])
            for bb in range(NB):
                v(lambda e, bb=bb, par=par, db=db: e.scalar_tensor_tensor(
                    hm[par][0][:, bb, :], db[:, bb, :], vb("mk", bb),
                    hb[:, bb, 0:T], ALU.mult, ALU.add),
                  waits=[(sT, hm_free[par])] if bb == 0 else [])
            sV_hmk[0] = sV.n
            for bb in range(NB):
                v(lambda e, bb=bb, par=par, db=db: e.scalar_tensor_tensor(
                    hm[par][1][:, bb, :], db[:, bb, :], vb("mv", bb),
                    hb[:, bb, 0:T], ALU.mult, ALU.add))
            sV_hmv[0] = sV.n

        for p0 in range(6):
            pr, c = wstream[p0]
            wload(pr, c, p0, 0)

        qmax_it_issue = [0]
        qmax_it_reduce = [0]
        qmax_vfree = [0, 0]
        qmax_snap = {}

        def qmax_issue():
            it = qmax_it_issue[0]
            if it >= 64:
                return
            assert it - qmax_it_reduce[0] <= 1
            qmax_it_issue[0] += 1
            if it < 32:
                src = d_wfc[(it // 4) * 128:(it // 4 + 1) * 128,
                            (it % 4) * 1024:(it % 4 + 1) * 1024]
            else:
                i2 = it - 32
                src = d_wpj[i2 * 128:(i2 + 1) * 128, :]
            half = it % 2
            qsem = (sgq, sgq2)[half]
            b.dma("gpsimd", qst[:, half * 1024:(half + 1) * 1024],
                  src, qsem, waits=[(sV, qmax_vfree[half])])
            qmax_snap[it] = (qsem, qsem.n)

        def qmax_reduce():
            it = qmax_it_reduce[0]
            if it >= 64 or qmax_it_issue[0] == it:
                return
            qmax_it_reduce[0] += 1
            acc = q_["mxfc"] if it < 32 else q_["mxpj"]
            half = it % 2
            v(lambda e, half=half: e.reduce_max(q_["t2"][:],
                                                qst[:, half * 1024:(half + 1) * 1024],
                                                AX.X, apply_absolute_value=True),
              waits=[qmax_snap[it]])
            v(lambda e, acc=acc: e.tensor_max(acc[:], acc[:], q_["t2"][:]))
            qmax_vfree[half] = sV.n

        a1_sq(0, sq_waits=[(sxx, sx_x0)])
        bsn0 = a1_stats_rest(0)
        a1_happly(0, bsn0)
        b.dma("sync", xbuf[1][:], x_cols(1), sxh, waits=[(sxps, sx_powsp)])
        sx_x[1] = sxh.n
        hm_free[0] = sT_b2
        a1_hmix(0)
        a1_sq(1, sq_waits=[(sxh, sx_x[1])])
        kevac = {m: 0 for m in range(NB)}
        vevac = {m: 0 for m in range(NB)}
        bsn_next = [0]

        for j in range(NT):
            p = j % 2
            hmA_j, hmB_j = hm[p]
            sV_hmk_j, sV_hmv_j = sV_hmk[0], sV_hmv[0]
            if j + 2 < NT:
                b.dma("sync", xbuf[p][:], x_cols(j + 2), (sxx, sxh)[p],
                      waits=[(sV, h_snap[p]), (sT, xsum_snap[p])])
                sx_x[p] = (sxx, sxh)[p].n
            if j + 1 < NT:
                bsn_next[0] = a1_stats_rest(j + 1)
            if j + 2 < NT:
                a1_sq(j + 2, sq_waits=[((sxx, sxh)[p], sx_x[p])])
            qmax_issue()
            qmax_issue()
            for m in range(NB):
                slot = wslot()
                mm_group(m % 4, wch, slot * 1024, hmA_j,
                         wait_extra=[wl_snap[slot], (sV, sV_hmk_j),
                                     (sS, vevac[4 + m % 4] if m < 4
                                      else kevac[m - 4])])
                wadvance()
                snap = sT.n
                s(lambda e, m=m: e.activation(ka[:, m, :],
                                              pmm[:, (m % 4) * 512:(m % 4 + 1) * 512],
                                              AF.Identity, bias=vb("bk", m)),
                  waits=[(sT, snap)] + ([(sV, kv_snap[0])] if m == 0 else []))
                kevac[m] = sS.n
            for m in range(NB):
                slot = wslot()
                mm_group(m % 4, wch, slot * 1024, hmB_j,
                         wait_extra=[wl_snap[slot], (sV, sV_hmv_j),
                                     (sS, kevac[NB - 1] if m < 4 else vevac[m - 4])])
                wadvance()
                snap = sT.n
                s(lambda e, m=m: e.activation(va[:, m, :],
                                              pmm[:, (m % 4) * 512:(m % 4 + 1) * 512],
                                              AF.Identity, bias=vb("bv", m)),
                  waits=[(sT, snap)] + ([(sV, kv_snap[0])] if m == 0 else []))
                vevac[m] = sS.n
            hm_free[p] = sT.n
            if j + 1 < NT:
                a1_happly(j + 1, bsn_next[0])
                a1_hmix(j + 1)
            qmax_reduce()
            qmax_reduce()
            qmax_issue()
            qmax_issue()
            v(lambda e, p=p: e.tensor_tensor(scb[1 - p][:], ka[:], va[:],
                                             ALU.mult),
              waits=[(sS, vevac[NB - 1])] +
                    ([(sxc[1 - p], sc_spilled[1 - p])] if j + 1 >= NT else []))
            kv_snap[0] = sV.n
            for bb in range(NB):
                v(lambda e, bb=bb, p=p: e.tensor_tensor_scan(
                    scb[p][:, bb, :], vb("dec", bb).broadcast_to([128, T]),
                    scb[1 - p][:, bb, :], vb("carry", bb), ALU.mult, ALU.add))
            v(lambda e, p=p: e.tensor_copy(vec["carry"], scb[p][:, :, T - 1]))
            for bb in range(NB):
                v(lambda e, bb=bb, p=p: e.scalar_tensor_tensor(
                    scb[p][:, bb, :], scb[1 - p][:, bb, :], vb("uv", bb),
                    scb[p][:, bb, :], ALU.mult, ALU.add))
            uadd_snap[0] = sV.n
            b.dma("sync", d_scsp[j].rearrange("(b p) t -> p b t", p=128), scb[p][:],
                  sxc[p], waits=[(sV, uadd_snap[0])])
            sc_spilled[p] = sxc[p].n
            qmax_reduce()
            qmax_reduce()

        b.dma("sync", d_ccin.rearrange("(b p) one -> p (b one)", p=128),
              vec["carry"], sxb, waits=[(sV, sV.n)], slow=True)
        sx_cc = sxb.n
        b.op("gpsimd",
             lambda e: e.collective_compute(
                 "AllGather", ALU.bypass,
                 replica_groups=[[0, 1], [2, 3], [4, 5], [6, 7]],
                 ins=[d_ccin], outs=[d_ccout]),
             waits=[(sxb, sx_cc), (sG, sG.n)], incs=[(scc, 1)], is_dma=True)
        b.dma("sync", vec["Din"],
              d_ccout[0:D].rearrange("(b p) one -> p (b one)", p=128), sxb,
              waits=[(scc, 2)], slow=True)
        sx_D = sxb.n
        v(lambda e: e.tensor_mul(vec["Din"], vec["Din"], vec["recvmask"]),
          waits=[(sxb, sx_D)])
        sV_D = sV.n

        b.dma("sync", pow_, d_powsp.rearrange("(b p) t -> p b t", p=128), sxps,
              waits=[(sV, h_snap[1]), (sT, xsum_snap[1]), (sxps, sx_powsp)])
        sx_powrl = sxps.n

        sx_h = [0]
        sx_sc = [0, 0]
        sx_x2free = [0]
        sx_xA2 = [0]
        sG_rw = [0, 0]
        corr_snapA = [0, 0]
        hmg_snapA = [0, 0]
        hmr_snapA = [0, 0]
        gnchain_snap = [0]
        gncast_snap = [0]
        ln2cast_snap = [0]
        ln2sums_snap = [0]
        ln2_vsnap = [0]
        x2add_snap = [0]
        woevac = {m: 0 for m in range(NB)}
        olad_free = [hm_free[0], hm_free[1]]
        sV_D_cur = [sV_D]

        b.dma("sync", hb[:], d_hsp[0].rearrange("(b p) t -> p b t", p=128), sxh,
              waits=[(sV, sV.n), (sxhs, sxhs.n)])
        sx_h[0] = sxh.n
        b.dma("sync", scb[0][:], d_scsp[0].rearrange("(b p) t -> p b t", p=128),
              sxc[0], waits=[(sxc[0], sxc[0].n), (sV, sV.n)])
        sx_sc[0] = sxc[0].n
        b.dma("sync", xbuf[0][:, :, 1:T + 1], x_cols_b(0), sxx,
              waits=[(sV, h_snap[0]), (sT, xsum_snap[0])])
        sx_xA2[0] = sxx.n

        def a2_corr(k):
            par = k % 2
            for bb in range(NB):
                v(lambda e, bb=bb, par=par: e.scalar_tensor_tensor(
                    scb[par][:, bb, :], pow_[:, bb, :], vb("Din", bb),
                    scb[par][:, bb, :], ALU.mult, ALU.add),
                  waits=([(sxc[par], sx_sc[par]), (sV, sV_D_cur[0])] +
                         ([(sxps, sx_powrl)] if k == 0 else [])) if bb == 0 else [])
            corr_snapA[par] = sV.n
            v(lambda e: e.tensor_mul(vec["Din"], vec["Din"], vec["d512"]))
            sV_D_cur[0] = sV.n

        def a2_delta_hmix(k):
            par = k % 2
            v(lambda e: e.tensor_tensor(x2b[:], hb[:, :, 1:T + 1],
                                        hb[:, :, 0:T], ALU.subtract),
              waits=[(sxh, sx_h[0]), (sS, ln2cast_snap[0]),
                     (sx2, sx_x2free[0]), (sT, x2b_free[0])])
            for bb in range(NB):
                v(lambda e, bb=bb, par=par: e.scalar_tensor_tensor(
                    hm[par][0][:, bb, :], x2b[:, bb, :], vb("mg", bb),
                    hb[:, bb, 0:T], ALU.mult, ALU.add),
                  waits=[(sT, olad_free[par])] if bb == 0 else [])
            hmg_snapA[par] = sV.n
            for bb in range(NB):
                v(lambda e, bb=bb, par=par: e.scalar_tensor_tensor(
                    hm[par][1][:, bb, :], x2b[:, bb, :], vb("mr", bb),
                    hb[:, bb, 0:T], ALU.mult, ALU.add))
            hmr_snapA[par] = sV.n

        a2_corr(0)
        a2_delta_hmix(0)
        b.dma("sync", hb[:], d_hsp[1].rearrange("(b p) t -> p b t", p=128), sxh,
              waits=[(sV, hmr_snapA[0])])
        sx_h[0] = sxh.n
        b.dma("sync", scb[1][:], d_scsp[1].rearrange("(b p) t -> p b t", p=128),
              sxc[1], waits=[(sxc[1], sxc[1].n)])
        sx_sc[1] = sxc[1].n

        for j in range(NT):
            p = j % 2
            hmc = hm[p]
            qmax_issue()
            qmax_issue()
            if j + 1 < NT:
                a2_corr(j + 1)
                a2_delta_hmix(j + 1)
            if j >= 1:
                v(lambda e: e.tensor_tensor(x2b[:], xbuf[0][:, :, 1:T + 1],
                                            va[:], ALU.add),
                  waits=[(sS, woevac[NB - 1]), (sxx, sx_xA2[0]),
                         (sx2, sx_x2free[0])])
                if DBG and j == 1:
                    b.dma("scalar", d_a2dbg[4][0:512, :].rearrange(
                        "(b p) t -> p b t", p=128),
                        va[:, 4:8, :], swm[1], waits=[(sV, sV.n)])
                    b.op("vector", None, waits=[(swm[1], swm[1].n)],
                         incs=[(sV, 1)])
                x2add_snap[0] = sV.n
                b.dma("sync",
                      d_x2sp[:, (j - 1) * T:j * T].rearrange("(b p) t -> p b t", p=128),
                      x2b[:], sx2, waits=[(sV, x2add_snap[0])])
                sx_x2free[0] = sx2.n
                b.dma("sync", xbuf[0][:, :, 1:T + 1], x_cols_b(j), sxx,
                      waits=[(sV, x2add_snap[0])])
                sx_xA2[0] = sxx.n
            gevac = {}
            for m in range(NB):
                slot = wslot()
                mm_group(m % 4, wch, slot * 1024, hmc[0],
                         wait_extra=[wl_snap[slot], (sV, hmg_snapA[p])] +
                                    ([(sV, gnchain_snap[0]),
                                      (sS, woevac[4 + m % 4])] if m < 4
                                     else [(sS, gevac[m - 4])]))
                wadvance()
                snap = sT.n
                s(lambda e, m=m: e.activation(ka[:, m, :],
                                              pmm[:, (m % 4) * 512:(m % 4 + 1) * 512],
                                              AF.Tanh, scale=0.5, bias=vb("bg2", m)),
                  waits=[(sT, snap)] + ([(sV, gnchain_snap[0])] if m == 0 else []))
                gevac[m] = sS.n
            revac = {}
            for m in range(NB):
                slot = wslot()
                mm_group(m % 4, wch, slot * 1024, hmc[1],
                         wait_extra=[wl_snap[slot], (sV, hmr_snapA[p]),
                                     (sS, gevac[NB - 1] if m < 4 else revac[m - 4])])
                wadvance()
                snap = sT.n
                s(lambda e, m=m: e.activation(va[:, m, :],
                                              pmm[:, (m % 4) * 512:(m % 4 + 1) * 512],
                                              AF.Identity, bias=vb("br", m)),
                  waits=[(sT, snap)] +
                        ([(sV, x2add_snap[0] if j >= 1 else sV.n)] if m == 0 else []))
                revac[m] = sS.n
            rlad_snap = sT.n
            if j >= 1:
                s(lambda e, hmc=hmc: e.activation(hmc[0][:], x2b[:], AF.Copy),
                  waits=[(sV, x2add_snap[0]), (sT, rlad_snap)])
                s(lambda e, hmc=hmc: e.activation(hmc[1][:], x2b[:], AF.Square))
                ln2cast_snap[0] = sS.n
                ln2_vsnap[0] = ln_stats(hmc[0], hmc[1])
                ln2sums_snap[0] = sT.n
                b.dma("sync",
                      d_bounce[(j - 1) * 512:j * 512].rearrange("t one -> one t"),
                      rows[0:1, 0:512], sxr, waits=[(sV, sV.n)])
                b.dma("sync",
                      d_bounce[(8 + j - 1) * 512:(9 + j - 1) * 512].rearrange("t one -> one t"),
                      rows[0:1, 512:1024], sxr)
            g(lambda e, p=p: e.tensor_tensor(scb[p][:], va[:], scb[p][:], ALU.mult),
              waits=[(sS, revac[NB - 1]), (sV, corr_snapA[p])])
            sG_rw[p] = sG.n
            s(lambda e, p=p, hmc=hmc: e.activation(hmc[0][:], scb[p][:], AF.Copy),
              waits=[(sG, sG_rw[p]), (sT, ln2sums_snap[0] if j >= 1 else rlad_snap)])
            s(lambda e, p=p, hmc=hmc: e.activation(hmc[1][:], scb[p][:], AF.Square))
            sS_rwc = sS.n
            for bb in range(NB):
                t(lambda e, bb=bb, hmc=hmc: e.matmul(pst[0:16, :], gsel[:, bb * 16:(bb + 1) * 16],
                                            hmc[0][:, bb, :], start=(bb == 0),
                                            stop=(bb == 7)),
                  waits=[(sS, sS_rwc), (sV, ln2_vsnap[0] if j >= 1 else sV.n),
                         (sgc, sgc_const)] if bb == 0 else [],
                  incs=[])
            for bb in range(NB):
                t(lambda e, bb=bb, hmc=hmc: e.matmul(pmo[0:16, :], gsel[:, bb * 16:(bb + 1) * 16],
                                            hmc[1][:, bb, :], start=(bb == 0),
                                            stop=(bb == 7)),
                  incs=[(sT, 1)] if bb == 7 else [])
            sT_gn = sT.n
            v(lambda e: e.tensor_scalar_mul(rows[0:16, 1024:1536], pst[0:16, :],
                                            1.0 / HD),
              waits=[(sT, sT_gn), (sxr, sxr.n)])
            v(lambda e: e.tensor_scalar_mul(rows[0:16, 1536:2048], pmo[0:16, :],
                                            1.0 / HD))
            v(lambda e: e.tensor_mul(rows[0:16, 0:512], rows[0:16, 1024:1536],
                                     rows[0:16, 1024:1536]))
            v(lambda e: e.tensor_sub(rows[0:16, 1536:2048], rows[0:16, 1536:2048],
                                     rows[0:16, 0:512]))
            sV_gn1 = sV.n
            s(lambda e: e.activation(rows[0:16, 0:512], rows[0:16, 1536:2048], AF.Ln,
                                     bias=misc[0:16, 2:3]), waits=[(sV, sV_gn1)])
            s(lambda e: e.activation(rows[0:16, 0:512], rows[0:16, 0:512], AF.Exp,
                                     scale=-0.5))
            sS_gn = sS.n
            v(lambda e: e.tensor_mul(rows[0:16, 512:1024], rows[0:16, 1024:1536],
                                     rows[0:16, 0:512]),
              waits=[(sS, sS_gn)])
            sV_gn2 = sV.n
            if DBG and j == 0:
                b.dma("sync", d_a2dbg[4][0:16, :], rows[0:16, 0:512], sxb,
                      waits=[(sV, sV_gn2)])
                b.dma("sync", d_a2dbg[4][16:32, :], rows[0:16, 512:1024], sxb)
                b.op("vector", None, waits=[(sxb, sxb.n)], incs=[(sV, 1)])
                b.op("scalar", None, waits=[(sxb, sxb.n)], incs=[(sS, 1)])
            s(lambda e: e.activation(ka[:], ka[:], AF.Identity, bias=mc(0)),
              waits=[(sT, sT.n)])
            sS_ka1 = sS.n
            if DBG and j == 0:
                b.dma("sync", d_a2dbg[0].rearrange("(b p) t -> p b t", p=128),
                      ka[:], sxb, waits=[(sS, sS_ka1)])
                b.dma("sync", d_a2dbg[1].rearrange("(b p) t -> p b t", p=128),
                      va[:], sxb, waits=[(sS, revac[NB - 1])])
                b.dma("sync", d_a2dbg[2].rearrange("(b p) t -> p b t", p=128),
                      scb[p][:], sxb, waits=[(sG, sG_rw[p])])
                b.op("scalar", None, waits=[(sxb, sxb.n)], incs=[(sS, 1)])
                b.op("vector", None, waits=[(sxb, sxb.n)], incs=[(sV, 1)])
                b.op("gpsimd", None, waits=[(sxb, sxb.n)], incs=[(sG, 1)])
            gnchain = {}
            for bb in range(NB):
                rot = bb % 3
                tgt = (pbc, pmm, pmm)[rot]
                o0 = (0, 0, 2 * 512)[rot]
                o1 = o0 + 512
                t(lambda e, bb=bb, tgt=tgt, o0=o0: e.matmul(
                    tgt[:, o0:o0 + 512], gselT[:, bb * 128:(bb + 1) * 128],
                    rows[0:16, 0:512], start=True, stop=True),
                  waits=[(sV, sV_gn2 if bb < 3 else gnchain[bb - 3])], incs=[])
                t(lambda e, bb=bb, tgt=tgt, o1=o1: e.matmul(
                    tgt[:, o1:o1 + 512], gselT[:, bb * 128:(bb + 1) * 128],
                    rows[0:16, 512:1024], start=True, stop=True), incs=[(sT, 1)])
                snapT = sT.n
                v(lambda e, bb=bb, p=p, tgt=tgt, o0=o0: e.tensor_tensor(
                    tmp[:, 0:T], scb[p][:, bb, :], tgt[:, o0:o0 + 512], ALU.mult),
                  waits=[(sT, snapT), (sG, sG_rw[p])] +
                        ([(sS, sS_ka1)] if bb == 0 else []))
                v(lambda e, tgt=tgt, o1=o1: e.tensor_tensor(
                    tmp[:, T:2 * T], tmp[:, 0:T], tgt[:, o1:o1 + 512], ALU.subtract))
                v(lambda e, bb=bb, hmc=hmc: e.tensor_tensor(
                    hmc[1][:, bb, :], tmp[:, T:2 * T], ka[:, bb, :], ALU.mult))
                gnchain[bb] = sV.n
            sV_apre = sV.n
            gnchain_snap[0] = sV_apre
            if DBG and j == 0:
                for bb2 in range(NB):
                    s(lambda e, bb2=bb2, hmc=hmc: e.activation(x2b[:, bb2, :],
                                                      hmc[1][:, bb2, :], AF.Copy),
                      waits=[(sV, sV_apre)] if bb2 == 0 else [])
                b.dma("sync", d_a2dbg[3].rearrange("(b p) t -> p b t", p=128),
                      x2b[:], sxb, waits=[(sS, sS.n)])
                b.op("vector", None, waits=[(sxb, sxb.n)], incs=[(sV, 1)])
                b.op("tensor", None, waits=[(sxb, sxb.n)])
            for m in range(NB):
                slot = wslot()
                mm_group(m % 4, wch, slot * 1024, hmc[1],
                         wait_extra=[wl_snap[slot], (sV, sV_apre),
                                     (sS, sS_rwc if m < 4 else woevac[m - 4])])
                wadvance()
                snap = sT.n
                s(lambda e, m=m: e.activation(va[:, m, :],
                                              pmm[:, (m % 4) * 512:(m % 4 + 1) * 512],
                                              AF.Identity, scale=vb("gah", m),
                                              bias=vb("gabo", m)),
                  waits=[(sT, snap), (sV, gnchain[NB - 1]), (sG, sG_rw[p])])
                woevac[m] = sS.n
            olad_free[p] = sT.n
            if DBG and j == 0:
                b.dma("scalar", d_a2dbg[4][512:1024, :].rearrange(
                    "(b p) t -> p b t", p=128),
                    va[:, 0:4, :], swm[1], waits=[(sS, woevac[NB - 1])])
                b.op("vector", None, waits=[(swm[1], swm[1].n)], incs=[(sV, 1)])
            qmax_reduce()
            qmax_reduce()
            qmax_issue()
            qmax_issue()
            if j + 2 < NT:
                b.dma("sync", hb[:], d_hsp[j + 2].rearrange("(b p) t -> p b t", p=128),
                      sxh, waits=[(sV, hmr_snapA[(j + 1) % 2])])
                sx_h[0] = sxh.n
                b.dma("sync", scb[p][:],
                      d_scsp[j + 2].rearrange("(b p) t -> p b t", p=128), sxc[p],
                      waits=[(sxc[p], sxc[p].n), (sV, gnchain_snap[0]),
                             (sS, sS_rwc), (sG, sG_rw[p])])
                sx_sc[p] = sxc[p].n
            qmax_reduce()
            qmax_reduce()

        v(lambda e: e.tensor_tensor(x2b[:], xbuf[0][:, :, 1:T + 1], va[:], ALU.add),
          waits=[(sS, woevac[NB - 1]), (sxx, sx_xA2[0]), (sx2, sx_x2free[0])])
        x2add_snap[0] = sV.n
        b.dma("sync", d_x2sp[:, 7 * T:8 * T].rearrange("(b p) t -> p b t", p=128),
              x2b[:], sx2, waits=[(sV, x2add_snap[0])])
        sx_x2free[0] = sx2.n
        s(lambda e: e.activation(hm[1][0][:], x2b[:], AF.Copy),
          waits=[(sV, x2add_snap[0]), (sT, sT.n)])
        s(lambda e: e.activation(hm[1][1][:], x2b[:], AF.Square))
        _ = ln_stats(hm[1][0], hm[1][1])
        b.dma("sync",
              d_bounce[7 * 512:8 * 512].rearrange("t one -> one t"),
              rows[0:1, 0:512], sxr, waits=[(sV, sV.n)])
        b.dma("sync",
              d_bounce[15 * 512:16 * 512].rearrange("t one -> one t"),
              rows[0:1, 512:1024], sxr)

        for _ in range(128):
            qmax_reduce()
            qmax_issue()
        qmax_reduce()
        b.dma("sync", d_bounce[7168:7296], q_["mxfc"][:], sxb, waits=[(sV, sV.n)])
        b.dma("sync", d_bounce[7296:7424], q_["mxpj"][:], sxb)
        sx_mxw = sxb.n
        b.dma("sync", rows[0:1, 0:128],
              d_bounce[7168:7296].rearrange("p one -> one p"), sxb,
              waits=[(sxb, sx_mxw), (sV, sV.n), (sxr, sxr.n)], slow=True)
        b.dma("sync", rows[0:1, 128:256],
              d_bounce[7296:7424].rearrange("p one -> one p"), sxb, slow=True)
        sx_mx = sxb.n
        v(lambda e: e.reduce_max(rows[0:1, 300:301], rows[0:1, 0:128], AX.X),
          waits=[(sxb, sx_mx)])
        v(lambda e: e.reduce_max(rows[0:1, 301:302], rows[0:1, 128:256], AX.X))
        v(lambda e: e.tensor_scalar_mul(rows[0:1, 302:303], rows[0:1, 300:301], 1.0 / 127.0))
        v(lambda e: e.tensor_scalar_mul(rows[0:1, 303:304], rows[0:1, 301:302], 1.0 / 127.0))
        v(lambda e: e.reciprocal(rows[0:1, 304:305], rows[0:1, 300:301]))
        v(lambda e: e.reciprocal(rows[0:1, 305:306], rows[0:1, 301:302]))
        v(lambda e: e.tensor_scalar_mul(rows[0:1, 304:305], rows[0:1, 304:305], 127.0))
        v(lambda e: e.tensor_scalar_mul(rows[0:1, 305:306], rows[0:1, 305:306], 127.0))
        sV_q = sV.n
        t(lambda e: e.matmul(pmo[:, 0:1], onesK1[:], rows[0:1, 302:303], start=True,
                             stop=True), waits=[(sV, sV_q)], incs=[])
        t(lambda e: e.matmul(pmo[:, 1:2], onesK1[:], rows[0:1, 304:305], start=True,
                             stop=True), incs=[])
        t(lambda e: e.matmul(pmo[:, 2:3], onesK1[:], rows[0:1, 303:304], start=True,
                             stop=True), incs=[])
        t(lambda e: e.matmul(pmo[:, 3:4], onesK1[:], rows[0:1, 305:306], start=True,
                             stop=True), incs=[(sT, 1)])
        sT_q = sT.n
        s(lambda e: e.activation(q_["sfc"][:], pmo[:, 0:1], AF.Identity),
          waits=[(sT, sT_q)])
        s(lambda e: e.activation(q_["invfc"][:], pmo[:, 1:2], AF.Identity))
        s(lambda e: e.activation(q_["spj"][:], pmo[:, 2:3], AF.Identity))
        s(lambda e: e.activation(q_["invpj"][:], pmo[:, 3:4], AF.Identity))
        sS_q = sS.n
        b.op("sync", None,
             waits=[(c, c.n) for c in (sxb, sxx, sxh, sxhs, sxc[0], sxc[1],
                                       sx2, sxr, sxps)], incs=[(bar, 1)])
        b.op("gpsimd", None,
             waits=[(sG, sG.n)] + [(c, c.n) for c in sgw + [sgc, sgq, sgq2]],
             incs=[(bar, 1)])
        b.op("vector", None, waits=[(sV, sV.n)], incs=[(bar, 1)])
        b.op("scalar", None,
             waits=[(sS, sS_q)] + [(c, c.n) for c in swS + swm],
             incs=[(bar, 1)])
        b.op("tensor", None, waits=[(sT, sT.n)], incs=[(bar, 1)])

    es_b = ExitStack()
    with es_b:
        sbb = lambda n, sh, dt=F32: es_b.enter_context(nc.sbuf_tensor(n, sh, dt))
        wfce = sbb("wfce", [128, NB * FF], BF16)
        wpjc = sbb("wpjc", [128, 2 * NFB * 128], BF16)
        wps = sbb("wps", [128, 2 * 1024], BF16)
        h1 = sbb("h1", [128, NFB * 512], BF16)
        h2 = sbb("h2", [128, NB, T], BF16)
        x2t = sbb("x2t", [128, NB, T])
        outt = sbb("outt", [128, NB, T])
        qw = sbb("qw", [128, 2 * 1024])
        qt1 = sbb("qt1", [128, 1024])
        lafc = sbb("lafc", [16, D], BF16)
        lbpjT = sbb("lbpjT", [16, D], BF16)
        lbf_s = sbb("lbf_s", [16, 1024], BF16)
        lap_s = sbb("lap_s", [16, 2 * 128], BF16)

        bw = [(bar, 5)]
        first = {"sync": bw, "gpsimd": bw, "vector": bw, "scalar": bw, "tensor": bw}

        def v(fn, waits=(), incs=((sV, 1),)):
            w = list(waits) + first.pop("vector", [])
            b.op("vector", fn, waits=w, incs=incs)

        def s(fn, waits=(), incs=((sS, 1),)):
            w = list(waits) + first.pop("scalar", [])
            b.op("scalar", fn, waits=w, incs=incs)

        def t(fn, waits=(), incs=()):
            w = list(waits) + first.pop("tensor", [])
            b.op("tensor", fn, waits=w, incs=incs)

        def g(fn, waits=(), incs=((sG, 1),)):
            w = list(waits) + first.pop("gpsimd", [])
            b.op("gpsimd", fn, waits=w, incs=incs)

        def dma_q(eng, out_ap, in_ap, sem, waits=()):
            w = list(waits) + first.pop(eng, [])
            b.op(eng, lambda e, o=out_ap, i=in_ap: e.dma_start(out=o, in_=i),
                 waits=w, incs=[(sem, 16)], is_dma=True)

        dma_q("gpsimd", lafc[:], d_afc, sgc)
        dma_q("gpsimd", lbpjT[:], d_bpjT, sgc)
        sg_lora = sgc.n
        v(lambda e: e.tensor_scalar_mul(lafc[:], lafc[:], 2.0), waits=[(sgc, sg_lora)])
        v(lambda e: e.tensor_scalar_mul(lbpjT[:], lbpjT[:], 2.0))
        sV_lora = sV.n

        qunit_v = {}
        lbf_snap = [0]

        def quant_unit(kind, og, i, uidx):
            if kind == "fc":
                wsrc = d_wfc[i * 128:(i + 1) * 128, og * 1024:(og + 1) * 1024]
                stat = lafc[:, i * 128:(i + 1) * 128]
                mov0, mov1 = lbf_s[:, 0:512], lbf_s[:, 512:1024]
                dst = wfce[:, i * FF + og * 1024: i * FF + (og + 1) * 1024]
                sc, iv = q_["sfc"], q_["invfc"]
                wmov = [(sgl[2], lbf_snap[0])]
            else:
                wsrc = d_wpj[i * 128:(i + 1) * 128, 0:1024]
                half2 = i % 2
                dma_q("gpsimd", lap_s[:, half2 * 128:(half2 + 1) * 128],
                      d_apj[:, i * 128:(i + 1) * 128], sgl[half2],
                      waits=[(sT, qunit_t.get(uidx - 2, 0))])
                stat = lap_s[:, half2 * 128:(half2 + 1) * 128]
                mov0, mov1 = lbpjT[:, 0:512], lbpjT[:, 512:1024]
                dst = wps[:, half2 * 1024:(half2 + 1) * 1024]
                sc, iv = q_["spj"], q_["invpj"]
                wmov = [(sgl[half2], sgl[half2].n), (sV, sV_lora),
                        (sxp[half2], wps_free[half2])]
            half = uidx % 2
            dma_q(("sync", "scalar")[half],
                  qw[:, half * 1024:(half + 1) * 1024], wsrc, sxq[half],
                  waits=[(sV, qunit_v.get(uidx - 2, 0))])
            snapx = (sxq[half], sxq[half].n)
            t(lambda e, stat=stat, mov0=mov0: e.matmul(
                pbc[:, 0:512], stat, mov0, start=True, stop=True),
              waits=[(sV, qunit_v.get(uidx - 1, 0)), (sV, sV_lora)] + wmov, incs=[])
            t(lambda e, stat=stat, mov1=mov1: e.matmul(
                pbc[:, 512:1024], stat, mov1, start=True, stop=True),
              incs=[(sT, 1)])
            snapT = sT.n
            qunit_t[uidx] = snapT
            s(lambda e, half=half, iv=iv: e.activation(
                qt1[:], qw[:, half * 1024:(half + 1) * 1024], AF.Identity,
                scale=iv[:], bias=q_["Mpp"][:]),
              waits=[snapx, (sV, qunit_v.get(uidx - 1, 0))])
            snapS = sS.n
            v(lambda e, sc=sc: e.tensor_scalar(qt1[:], qt1[:], q_["Mpp"][:], sc[:],
                                               ALU.subtract, ALU.mult),
              waits=[(sS, snapS)])
            v(lambda e, dst=dst: e.tensor_tensor(dst, qt1[:], pbc[:, 0:1024], ALU.add),
              waits=[(sT, snapT)])
            qunit_v[uidx] = sV.n
            if kind == "pj":
                for mo in range(NB):
                    b.dma("sync", d_wpjbf[mo, :, i * 128:(i + 1) * 128],
                          dst[:, mo * 128:(mo + 1) * 128], sxp[half2],
                          waits=[(sV, qunit_v[uidx])] if mo == 0 else [])
                wps_free[half2] = sxp[half2].n

        qunit_t = {}
        wps_free = [0, 0]
        uidx = 0
        for og in range(4):
            dma_q("gpsimd", lbf_s[:], d_bfcT[:, og * 1024:(og + 1) * 1024], sgl[2],
                  waits=[(sT, qunit_t.get(uidx - 1, 0))])
            lbf_snap[0] = sgl[2].n
            for i in range(NB):
                quant_unit("fc", og, i, uidx)
                uidx += 1
        fc_ready = {og: qunit_v[(og + 1) * NB - 1] for og in range(4)}
        for i in range(NFB):
            quant_unit("pj", 0, i, uidx)
            uidx += 1
        pj_ready = (sxp[0].n, sxp[1].n)

        sx_x2 = [0]
        wpjc_used = [0, 0]
        dma_q("sync", x2t[:], d_x2sp[:, 0:T].rearrange("(b p) t -> p b t", p=128), sxx)
        sx_x2[0] = sxx.n
        outfree = [0]
        rowsnap = [0]
        for j in range(NT):
            sx_xj = sx_x2[0]
            dma_q("sync", rows[0:1, 0:512],
                  d_bounce[j * 512:(j + 1) * 512].rearrange("t one -> one t"),
                  sxr, waits=[(sT, rowsnap[0])])
            dma_q("sync", rows[0:1, 512:1024],
                  d_bounce[(8 + j) * 512:(9 + j) * 512].rearrange("t one -> one t"),
                  sxr)
            sx_rows = sxr.n
            t(lambda e: e.matmul(pmo[:, 0:512], onesK1[:], rows[0:1, 0:512],
                                 start=True, stop=True),
              waits=[(sxr, sx_rows)], incs=[(sT, 1)])
            snapA = sT.n
            v(lambda e: e.tensor_tensor(outt[:], x2t[:], bc3(pmo[:, 0:512]), ALU.mult),
              waits=[(sT, snapA), (sxx, sx_xj), (sout, outfree[0])])
            snapVA = sV.n
            t(lambda e: e.matmul(pmo[:, 0:512], onesK1[:], rows[0:1, 512:1024],
                                  start=True, stop=True),
              waits=[(sV, snapVA)], incs=[(sT, 1)])
            snapB = sT.n
            rowsnap[0] = sT.n
            v(lambda e: e.tensor_tensor(outt[:], outt[:], bc3(pmo[:, 0:512]),
                                        ALU.subtract), waits=[(sT, snapB)])
            for bb in range(NB):
                v(lambda e, bb=bb: e.tensor_scalar(h2[:, bb, :], outt[:, bb, :],
                                                   vb("cb1", bb), vb("sb_", bb),
                                                   ALU.mult, ALU.add))
            sV_h2 = sV.n
            fcev = {}
            for m in range(NFB):
                og = m // 8
                wts = [(sV, sV_h2), (sV, fc_ready[og])]
                if m >= 4:
                    wts.append((sS, fcev[m - 4]))
                else:
                    wts.append((sS, sS.n))
                for kc in range(NB):
                    t(lambda e, m=m, kc=kc: e.matmul(
                        pmm[:, (m % 4) * 512:(m % 4 + 1) * 512],
                        wfce[:, kc * FF + m * 128: kc * FF + (m + 1) * 128],
                        h2[:, kc, :], start=(kc == 0), stop=(kc == 7)),
                      waits=wts if kc == 0 else [], incs=[(sT, 1)] if kc == 7 else [])
                snap = sT.n
                s(lambda e, m=m: e.activation(
                    h1[:, m * 512:(m + 1) * 512],
                    pmm[:, (m % 4) * 512:(m % 4 + 1) * 512], AF.Gelu,
                    bias=bfcv[:, m:m + 1]), waits=[(sT, snap)])
                fcev[m] = sS.n
            pjev = {}
            for mo in range(NB):
                slot = mo % 2
                b.op("scalar",
                     lambda e, mo=mo, slot=slot: e.dma_start(
                         out=wpjc[:, slot * NFB * 128:(slot + 1) * NFB * 128],
                         in_=d_wpjbf[mo]),
                     waits=[(sxp[0], pj_ready[0]), (sxp[1], pj_ready[1]),
                            (sT, wpjc_used[slot])],
                     incs=[(swp[slot], 16)], is_dma=True)
                wts = [(sS, fcev[NFB - 1]), (swp[slot], swp[slot].n)]
                if mo >= 4:
                    wts.append((sS, pjev[mo - 4]))
                for kc in range(NFB):
                    t(lambda e, mo=mo, kc=kc, slot=slot: e.matmul(
                        pmm[:, (mo % 4) * 512:(mo % 4 + 1) * 512],
                        wpjc[:, slot * NFB * 128 + kc * 128:
                             slot * NFB * 128 + (kc + 1) * 128],
                        h1[:, kc * 512:(kc + 1) * 512], start=(kc == 0),
                        stop=(kc == 31)),
                      waits=wts if kc == 0 else [],
                      incs=[(sT, 1)] if kc == 31 else [])
                wpjc_used[slot] = sT.n
                snap = sT.n
                s(lambda e, mo=mo: e.activation(
                    outt[:, mo, :], pmm[:, (mo % 4) * 512:(mo % 4 + 1) * 512],
                    AF.Identity, scale=vb("gb", mo), bias=vb("gbbp", mo)),
                  waits=[(sT, snap), (sV, sV_h2)])
                pjev[mo] = sS.n
            v(lambda e: e.tensor_tensor(outt[:], outt[:], x2t[:], ALU.add),
              waits=[(sS, pjev[NB - 1])])
            sV_out = sV.n
            if j + 1 < NT:
                dma_q("sync", x2t[:],
                      d_x2sp[:, (j + 1) * T:(j + 2) * T].rearrange("(b p) t -> p b t", p=128),
                      sxx, waits=[(sV, sV_out)])
                sx_x2[0] = sxx.n
            dma_q("sync", d_out[:, j * T:(j + 1) * T].rearrange("(b p) t -> p b t", p=128),
                  outt[:], sout, waits=[(sV, sV_out)])
            outfree[0] = sout.n

        b.op("sync", None, waits=[(sout, sout.n), (sxx, sxx.n)], incs=[(bar, 1)])

    nc2 = b.nc
    for c in ctrs:
        b.sems[c.name] = es.enter_context(nc2.semaphore("sem_" + c.name))

    @block.sync
    def _(eng):
        b.replay("sync", eng)

    @block.gpsimd
    def _(eng):
        b.replay("gpsimd", eng)

    @block.vector
    def _(eng):
        b.replay("vector", eng)

    @block.scalar
    def _(eng):
        b.replay("scalar", eng)

    @block.tensor
    def _(eng):
        b.replay("tensor", eng)


def make_in_maps(inputs, ncore=NCORE):
    x = np.asarray(inputs["L0_attn_x"])
    assert x.shape == (4, 8192, 1024)

    misc = np.zeros((128, 16), np.float32)
    misc[:, 0] = 1.0
    misc[:, 2] = EPS
    misc[:, 5] = M32
    misc[:, 6] = M64 - M32

    gsel = np.zeros((128, NB * 16), np.float32)
    gselT = np.zeros((16, NB * 128), np.float32)
    for bb in range(NB):
        for j2 in range(2):
            head = 2 * bb + j2
            gsel[j2 * 64:(j2 + 1) * 64, bb * 16 + head] = 1.0
            gselT[head, bb * 128 + j2 * 64: bb * 128 + (j2 + 1) * 64] = 1.0

    common = {

        "WfcT": np.ascontiguousarray(np.asarray(inputs["L0_ffn_W_fc"], np.float32).T),
        "WprojT": np.ascontiguousarray(np.asarray(inputs["L0_ffn_W_proj"], np.float32).T),
        "Afc": np.ascontiguousarray(inputs["L0_ffn_A_fc"], np.float32),
        "BfcT": np.ascontiguousarray(np.asarray(inputs["L0_ffn_B_fc"], np.float32).T),
        "Aproj": np.ascontiguousarray(inputs["L0_ffn_A_proj"], np.float32),
        "BprojT": np.ascontiguousarray(np.asarray(inputs["L0_ffn_B_proj"], np.float32).T),

        "bfc": np.ascontiguousarray(
            np.asarray(inputs["L0_ffn_b_fc"], np.float32).reshape(NFB, 128).T),
        "bproj": _fm(inputs["L0_ffn_b_proj"]),
        "misc": misc,
        "onesK1": np.ones((1, 128), np.float32),
        "ramp": np.arange(1, T + 1, dtype=np.float32).reshape(1, T),
        "gsel": gsel,
        "gselT": gselT,
    }
    for nm, key in (("WgT", "L0_attn_Wg_w"), ("WrT", "L0_attn_Wr_w"),
                    ("WkT", "L0_attn_Wk_w"), ("WvT", "L0_attn_Wv_w"),
                    ("WoT", "L0_attn_Wo_w")):
        common[nm] = _sw(np.asarray(inputs[key], np.float32).T).reshape(8, 128, 1024)

    vec_names = ("mg", "mr", "mk", "mv", "bg2", "br", "bk", "bv", "uv",
                 "ca1", "sa", "cb1", "sb_", "ga", "gb", "gah", "gabo", "gbbp",
                 "e", "dec", "d512", "carry", "Din", "hlast", "recvmask",
                 "modsc")
    vslot = {n: i for i, n in enumerate(vec_names)}
    vbase = np.zeros((128, 26 * NB), np.float32)

    def vput(vp, name, val):
        vp[:, vslot[name] * NB:(vslot[name] + 1) * NB] = val

    vput(vbase, "mg", _fm(inputs["L0_attn_mg"]))
    vput(vbase, "mr", _fm(inputs["L0_attn_mr"]))
    vput(vbase, "mk", _fm(inputs["L0_attn_mk"]))
    vput(vbase, "mv", _fm(inputs["L0_attn_mv"]))
    vput(vbase, "bg2", _fm(inputs["L0_attn_Wg_b"]))
    vput(vbase, "br", _fm(inputs["L0_attn_Wr_b"]))
    vput(vbase, "bk", _fm(inputs["L0_attn_Wk_b"]))
    vput(vbase, "bv", _fm(inputs["L0_attn_Wv_b"]))
    vput(vbase, "uv", _fm(np.asarray(inputs["L0_attn_u"]).reshape(-1)))
    vput(vbase, "gabo", _fm(inputs["L0_attn_Wo_b"]))
    vput(vbase, "e", _fm(np.asarray(inputs["L0_attn_w"]).reshape(-1)))

    wmod_f = np.asarray(inputs["L0_wmod"], np.float32)
    in_maps = []
    for c in range(NCORE):
        q, half = c // 2, c % 2
        t0 = half * TOK
        xT = np.ascontiguousarray(x[q].T)
        lead = xT[:, (t0 - 1) % 8192:(t0 - 1) % 8192 + 1]
        m = dict(common)
        m["wmodS"] = np.ascontiguousarray(wmod_f[:, half * 3072:(half + 1) * 3072])
        m["xT"] = np.ascontiguousarray(np.concatenate([lead, xT[:, t0:t0 + TOK]], axis=1))
        vp = vbase.copy()
        vput(vp, "modsc", _fm(np.asarray(inputs["L0_c"])[q]))
        vput(vp, "recvmask", np.full((128, NB), float(half), np.float32))
        m["vpack_in"] = vp
        in_maps.append(m)
    return in_maps[:ncore]


def kernel(**inputs):
    in_maps = make_in_maps(inputs)
    bprog = build_program()
    res = run_bass_kernel_spmd(bprog.nc, in_maps, core_ids=list(range(NCORE)))

    out = np.empty((4, 8192, 1024), np.float32)
    for c in range(NCORE):
        q, half = c // 2, c % 2
        t0 = half * TOK
        out[q, t0:t0 + TOK, :] = res.results[c]["outT"].T
    if DBG:
        kernel._dbg = res.results
    return out


# version 14
# speedup vs baseline: 1.1455x; 1.0285x over previous
import os
import numpy as np
from contextlib import ExitStack

import concourse.bass as bass
import concourse.mybir as mybir
from concourse.bass_utils import run_bass_kernel_spmd

F32 = mybir.dt.float32
BF16 = mybir.dt.bfloat16
AF = mybir.ActivationFunctionType
ALU = mybir.AluOpType
AX = mybir.AxisListType

NCORE = 8
D = 1024
NB = 8
T = 512
NT = 8
TOK = NT * T
H, HD = 16, 64
FF = 4096
NFB = 32
EPS = 1e-5
M32 = 12582912.0
M64 = 6755399441055744.0

DBG = os.environ.get("MK_DBG", "") == "1"


class Ctr:

    def __init__(self, name):
        self.name = name
        self.n = 0


class B:
    self_sync = {"vector", "scalar", "gpsimd"}

    def __init__(self):
        self.nc = bass.Bass("TRN2", target_bir_lowering=False, debug=False)
        self.items = []
        self.sems = {}
        self.eng_ctr = {}

    def op(self, eng, fn, waits=(), incs=(), is_dma=False):
        incs = list(incs)
        ws = [(c, v) for (c, v) in waits if v > 0]
        if eng in B.self_sync and fn is not None and not is_dma:
            ctr = self.eng_ctr.get(eng)
            if ctr is not None:
                ws.append((ctr, ctr.n))
                if not any(c is ctr for c, _ in incs):
                    incs.append((ctr, 1))
        for c, d in incs:
            c.n += d
        self.items.append((eng, fn, ws, incs))

    def dma(self, eng, out_ap, in_ap, sem, waits=(), slow=False):
        self.op(eng, lambda e, o=out_ap, i=in_ap, s=slow: e.dma_start(
            out=o, in_=i, allow_slow_non_contiguous=s),
                waits=waits, incs=[(sem, 16)], is_dma=True)

    def replay(self, eng_name, eng):
        for name, fn, waits, incs in self.items:
            if name != eng_name:
                continue
            for c, v in waits:
                eng.wait_ge(self.sems[c.name], v)
            instr = fn(eng) if fn is not None else None
            if incs:
                if instr is None:
                    instr = eng.nop()
                for c, d in incs:
                    instr = instr.then_inc(self.sems[c.name], d)


def _fm(v):
    return np.ascontiguousarray(np.asarray(v, np.float32).reshape(NB, 128).T)


def _sw(w):
    return np.ascontiguousarray(
        np.asarray(w, np.float32).reshape(8, 128, 8, 128).transpose(2, 1, 0, 3)
    ).reshape(8, 128, 1024)


def build_program():
    b = B()
    nc = b.nc
    es = ExitStack()
    with es:
        _build(b, es)
    return b


def _build(b, es):
    nc = b.nc
    dget = lambda n, sh, dt=F32, k="ExternalInput": nc.dram_tensor(n, sh, dt, kind=k).ap()
    d_x = dget("xT", [D, TOK + 1])
    d_w = {n: dget(n, [8, 128, 1024]) for n in ("WgT", "WrT", "WkT", "WvT", "WoT")}
    d_wmod = dget("wmodS", [D, 3 * D])
    d_mgi = dget("mgi", [3 * D, 1], F32, "Internal")
    d_mgo = dget("mgo", [6 * D, 1], F32, "Internal")
    d_wfc = dget("WfcT", [D, FF])
    d_wpj = dget("WprojT", [FF, D])
    d_afc = dget("Afc", [16, D])
    d_bfcT = dget("BfcT", [16, FF])
    d_apj = dget("Aproj", [16, FF])
    d_bpjT = dget("BprojT", [16, D])
    d_vpack = dget("vpack_in", [128, 26 * NB])
    d_bfc = dget("bfc", [128, NFB])
    d_bpj = dget("bproj", [128, NB])
    d_misc = dget("misc", [128, 16])
    d_onesK1 = dget("onesK1", [1, 128])
    d_ramp = dget("ramp", [1, T])
    d_gsel = dget("gsel", [128, NB * 16])
    d_gselT = dget("gselT", [16, NB * 128])
    d_out = dget("outT", [D, TOK], F32, "ExternalOutput")

    kd = "ExternalOutput" if DBG else "Internal"
    d_hsp = dget("hsp", [NT, D, T + 1], F32, kd)
    d_scsp = dget("scsp", [NT, D, T], F32, kd)
    d_x2sp = dget("x2sp", [D, TOK], F32, kd)
    d_a2dbg = dget("a2dbg", [5, D, T], F32, kd)
    d_wbf = dget("wbf", [5, 8, 128, 1024], BF16, "Internal")
    d_wpjbf = dget("wpjbf", [NB, 128, NFB * 128], BF16, "Internal")
    d_powsp = dget("powsp", [D, T], F32, "Internal")
    d_ccin = dget("ccin", [D, 1], F32, "Internal")
    d_ccout = dget("ccout", [2 * D, 1], F32, "Internal")
    d_bounce = dget("bounce", [8192, 1], F32, "Internal")

    sT = Ctr("sT"); sV = Ctr("sV"); sS = Ctr("sS"); sG = Ctr("sG")
    scc = Ctr("scc"); bar = Ctr("bar")
    sxb = Ctr("sxb"); sxx = Ctr("sxx"); sxh = Ctr("sxh"); sxhs = Ctr("sxhs")
    sxc = [Ctr("sxc0"), Ctr("sxc1")]
    sx2 = Ctr("sx2"); sout = Ctr("sout"); sxr = Ctr("sxr")
    sxp = [Ctr("sxp0"), Ctr("sxp1")]
    sxq = [Ctr("sxq0"), Ctr("sxq1")]
    swm = [Ctr("swm0"), Ctr("swm1")]
    swS = [Ctr(f"swS{i}") for i in range(6)]
    swp = [Ctr("swp0"), Ctr("swp1")]
    sgw = [Ctr(f"sgw{i}") for i in range(5)]
    sgc = Ctr("sgc"); sgq = Ctr("sgq"); sgq2 = Ctr("sgq2")
    sgl = [Ctr("sgl0"), Ctr("sgl1"), Ctr("sgl2")]
    sxps = Ctr("sxps")
    ctrs = ([sT, sV, sS, sG, scc, bar, sxb, sxx, sxh, sxhs, sx2, sout, sxr,
             sgc, sgq, sgq2, sxps] + sxp + sxc + sxq + swm + swS + swp + sgw + sgl)
    b.eng_ctr = {"vector": sV, "scalar": sS, "gpsimd": sG}

    sb = lambda n, sh, dt=F32: es.enter_context(nc.sbuf_tensor(n, sh, dt))
    misc = sb("s_misc", [128, 16])
    onesK1 = sb("s_onesK1", [1, 128])
    ones128b = sb("ones128b", [128, 1], BF16)
    gsel = sb("s_gsel", [128, NB * 16], BF16)
    rows = sb("rows", [128, 2048])
    _vec_names = ("mg", "mr", "mk", "mv", "bg2", "br", "bk", "bv", "uv",
                  "ca1", "sa", "cb1", "sb_", "ga", "gb", "gah", "gabo", "gbbp",
                  "e", "dec", "d512", "carry", "Din", "hlast", "recvmask",
                  "modsc")
    vpack = sb("vpack", [128, len(_vec_names) * NB])
    vec = {n: vpack[:, i * NB:(i + 1) * NB]
           for i, n in enumerate(_vec_names)}
    siluc = sb("siluc", [128, NB])
    bfcv = sb("bfcv", [128, NFB])
    bpjv = sb("bpjv", [128, NB])
    q_ = {}
    for n in ("delta", "Mpp", "sfc", "invfc", "spj", "invpj", "mxfc", "mxpj",
              "t1", "t2", "m11", "s11f", "s11p", "i11f", "i11p"):
        q_[n] = sb("q_" + n, [128, 1])


    pmm = es.enter_context(nc.psum_tensor("pmm", [128, 4 * 512], F32))
    pbc = es.enter_context(nc.psum_tensor("pbc", [128, 2 * 512], F32))
    pst = es.enter_context(nc.psum_tensor("pst", [128, 512], F32))
    pmo = es.enter_context(nc.psum_tensor("pmo", [128, 512], F32))

    block = es.enter_context(nc.Block())

    mc = lambda c: misc[:, c:c + 1]
    vb = lambda name, bb: vec[name][:, bb:bb + 1]

    def bc3(ps_slice):
        return ps_slice.unsqueeze(1).broadcast_to([128, NB, T])

    es_a = ExitStack()
    with es_a:
        sba = lambda n, sh, dt=F32: es_a.enter_context(nc.sbuf_tensor(n, sh, dt))
        xbuf = [sba(f"xb{i}", [128, NB, T + 1]) for i in range(2)]
        hb = sba("hb", [128, NB, T + 1])
        hm = [(sba("hmA0", [128, NB, T], BF16), sba("hmB0", [128, NB, T], BF16)),
              (sba("hmA1", [128, NB, T], BF16), sba("hmB1", [128, NB, T], BF16))]
        ka = sba("ka", [128, NB, T])
        va = sba("va", [128, NB, T])
        scb = [sba(f"sc{i}", [128, NB, T]) for i in range(2)]
        x2b = sba("x2b", [128, NB, T])
        pow_ = xbuf[1][:, :, 0:T]
        wch = sba("wch", [128, 6 * 1024], BF16)
        ramp = sba("s_ramp", [1, T])
        gselT = sba("s_gselT", [16, NB * 128])
        qst = sba("qst", [128, 2 * 1024])
        tmp = sba("tmp", [128, 2 * T])

        b.dma("sync", misc[:], d_misc, sxb)
        b.dma("sync", onesK1[:], d_onesK1, sxb)
        b.dma("sync", ramp[:], d_ramp, sxb)
        b.dma("sync", gselT[:], d_gselT, sxb)
        b.dma("sync", vpack[:], d_vpack, sxb)
        b.dma("sync", bfcv[:], d_bfc, sxb)
        b.dma("sync", bpjv[:], d_bpj, sxb)
        n_setup_sync = sxb.n
        b.dma("gpsimd", gsel[:], d_gsel, sgc)
        b.dma("gpsimd", ones128b[:], d_onesK1.rearrange("one p -> p one"), sgc)
        sgc_const = sgc.n
        b.dma("gpsimd", d_wbf[2], d_w["WkT"], sgw[2])
        b.dma("gpsimd", d_wbf[3], d_w["WvT"], sgw[3])
        b.dma("gpsimd", d_wbf[0], d_w["WgT"], sgw[0])
        b.dma("gpsimd", d_wbf[1], d_w["WrT"], sgw[1])
        b.dma("gpsimd", d_wbf[4], d_w["WoT"], sgw[4])
        sg_w = {"g": (sgw[0], 16), "r": (sgw[1], 16), "k": (sgw[2], 16),
                "v": (sgw[3], 16), "o": (sgw[4], 16)}

        wait_c = [(sxb, n_setup_sync)]

        def v(fn, waits=(), incs=((sV, 1),)):
            b.op("vector", fn, waits=waits, incs=incs)

        def s(fn, waits=(), incs=((sS, 1),)):
            b.op("scalar", fn, waits=waits, incs=incs)

        def t(fn, waits=(), incs=()):
            b.op("tensor", fn, waits=waits, incs=incs)

        def g(fn, waits=(), incs=((sG, 1),)):
            b.op("gpsimd", fn, waits=waits, incs=incs)

        v(lambda e: e.tensor_scalar(q_["t1"][:], mc(0), 2.0 ** -30, None, ALU.add),
          waits=wait_c)
        v(lambda e: e.tensor_scalar(q_["delta"][:], q_["t1"][:], 1.0, 2.0 ** 30,
                                    ALU.subtract, ALU.mult))
        v(lambda e: e.tensor_scalar(q_["Mpp"][:], q_["delta"][:], mc(6), mc(5),
                                    ALU.mult, ALU.add))
        v(lambda e: e.tensor_scalar_mul(vec["bg2"], vec["bg2"], 0.5), waits=wait_c)
        s(lambda e: e.activation(vec["e"], vec["e"], AF.Exp), waits=wait_c)
        s(lambda e: e.activation(vec["dec"], vec["e"], AF.Exp, scale=-1.0))
        sS_dec = sS.n
        s(lambda e: e.activation(siluc[:], vec["modsc"], AF.Silu), waits=wait_c)
        sS_silu = sS.n
        v(lambda e: e.memset(vec["carry"], 0.0))
        v(lambda e: e.memset(q_["mxfc"][:], 0.0))
        v(lambda e: e.memset(q_["mxpj"][:], 0.0))

        b.dma("sync", d_bounce[6144:7168].rearrange("(b p) one -> p (b one)", p=128),
              vec["e"], sxb, waits=[(sS, sS_dec)], slow=True)
        sx_eb = sxb.n
        b.dma("sync", rows[0:1, 1024:2048],
              d_bounce[6144:7168].rearrange("(b p) one -> one (b p)", p=128),
              sxb, waits=[(sxb, sx_eb)], slow=True)
        sx_erow = sxb.n
        for bb in range(NB):
            half = bb % 2
            t(lambda e, bb=bb, half=half: e.matmul(
                pbc[:, half * 512:(half + 1) * 512],
                rows[0:1, 1024 + bb * 128:1024 + (bb + 1) * 128], ramp[:],
                start=True, stop=True),
              waits=[(sxb, sx_erow), (sS, sS.n)], incs=[(sT, 1)])
            s(lambda e, bb=bb, half=half: e.activation(
                pow_[:, bb, :], pbc[:, half * 512:(half + 1) * 512], AF.Exp,
                scale=-1.0), waits=[(sT, sT.n)])
        sS_pow = sS.n
        v(lambda e: e.tensor_copy(vec["d512"], pow_[:, :, T - 1]),
          waits=[(sS, sS_pow)])
        sV_d512 = sV.n
        b.dma("sync", d_powsp.rearrange("(b p) t -> p b t", p=128), pow_,
              sxps, waits=[(sS, sS_pow), (sV, sV_d512)])
        sx_powsp = sxps.n

        wm_used = [0, 0]
        for sl in range(6):
            for kc in range(8):
                wh = (sl * 8 + kc) % 2
                b.dma("scalar", tmp[:, wh * 512:(wh + 1) * 512],
                      d_wmod[kc * 128:(kc + 1) * 128, sl * 512:(sl + 1) * 512],
                      swm[wh], waits=[(sT, wm_used[wh])])
                t(lambda e, kc=kc, wh=wh: e.matmul(
                    pmo[0:1, :], siluc[:, kc:kc + 1], tmp[:, wh * 512:(wh + 1) * 512],
                    start=(kc == 0), stop=(kc == 7)),
                  waits=[(swm[wh], swm[wh].n), (sS, sS_silu)] +
                        ([(sS, sS.n)] if kc == 0 else []),
                  incs=[(sT, 1)])
                wm_used[wh] = sT.n
            s(lambda e, sl=sl: e.activation(rows[0:1, 0:512], pmo[0:1, :], AF.Identity),
              waits=[(sT, sT.n)] + ([(sxb, sxb.n)] if sl > 0 else []))
            b.dma("sync",
                  d_mgi[sl * 512:(sl + 1) * 512].rearrange("t one -> one t"),
                  rows[0:1, 0:512], sxb, waits=[(sS, sS.n)])
        sx_modb = sxb.n
        b.op("gpsimd",
             lambda e: e.collective_compute(
                 "AllGather", ALU.bypass,
                 replica_groups=[[0, 1], [2, 3], [4, 5], [6, 7]],
                 ins=[d_mgi], outs=[d_mgo]),
             waits=[(sxb, sx_modb), (sG, sG.n)], incs=[(scc, 1)], is_dma=True)
        mod_names = ["sa", "ca1", "ga", "sb_", "cb1", "gb"]
        for j, nm in enumerate(mod_names):
            b.dma("sync", vec[nm],
                  d_mgo[j * 1024:(j + 1) * 1024].rearrange("(b p) one -> p (b one)", p=128),
                  sxb, waits=[(scc, 1)], slow=True)
        sx_mod = sxb.n
        v(lambda e: e.tensor_scalar_add(vec["ca1"], vec["ca1"], 1.0),
          waits=[(sxb, sx_mod)])
        v(lambda e: e.tensor_scalar_add(vec["cb1"], vec["cb1"], 1.0))
        for nm in ("ga", "gb"):
            v(lambda e, nm=nm: e.tensor_scalar_max(tmp[:, 0:NB], vec[nm], 0.25))
            v(lambda e, nm=nm: e.tensor_scalar_min(tmp[:, NB:2 * NB], vec[nm], -0.25))
            v(lambda e, nm=nm: e.tensor_single_scalar(tmp[:, 2 * NB:3 * NB], vec[nm],
                                                      0.0, ALU.is_ge))
            v(lambda e: e.tensor_sub(tmp[:, 3 * NB:4 * NB], tmp[:, 0:NB], tmp[:, NB:2 * NB]))
            v(lambda e: e.tensor_mul(tmp[:, 2 * NB:3 * NB], tmp[:, 2 * NB:3 * NB],
                                     tmp[:, 3 * NB:4 * NB]))
            v(lambda e, nm=nm: e.tensor_add(vec[nm], tmp[:, NB:2 * NB],
                                            tmp[:, 2 * NB:3 * NB]))
        v(lambda e: e.tensor_scalar_mul(vec["gah"], vec["ga"], 0.5))
        v(lambda e: e.tensor_mul(vec["gabo"], vec["ga"], vec["gabo"]))
        v(lambda e: e.tensor_mul(vec["gbbp"], vec["gb"], bpjv[:]))
        sV_mod = sV.n

        x_cols = lambda j: d_x[:, j * T: j * T + T + 1].rearrange(
            "(b p) t -> p b t", p=128)
        x_cols_b = lambda j: d_x[:, j * T + 1: (j + 1) * T + 1].rearrange(
            "(b p) t -> p b t", p=128)

        def ln_stats(src_bfA, src_bfB):
            for bb in range(NB):
                t(lambda e, bb=bb: e.matmul(pst[0:1, :], ones128b[:], src_bfA[:, bb, :],
                                            start=(bb == 0), stop=(bb == 7)),
                  waits=[(sS, sS.n), (sV, sV.n), (sgc, sgc_const)] if bb == 0 else [],
                  incs=[])
            for bb in range(NB):
                t(lambda e, bb=bb: e.matmul(pmo[0:1, :], ones128b[:], src_bfB[:, bb, :],
                                            start=(bb == 0), stop=(bb == 7)),
                  incs=[(sT, 1)] if bb == 7 else [])
            snap = sT.n
            v(lambda e: e.tensor_scalar_mul(rows[0:1, 1024:1536], pst[0:1, :], 1.0 / D),
              waits=[(sT, snap), (sxr, sxr.n)])
            v(lambda e: e.tensor_scalar_mul(rows[0:1, 1536:2048], pmo[0:1, :], 1.0 / D))
            v(lambda e: e.tensor_mul(rows[0:1, 0:512], rows[0:1, 1024:1536],
                                     rows[0:1, 1024:1536]))
            v(lambda e: e.tensor_sub(rows[0:1, 1536:2048], rows[0:1, 1536:2048],
                                     rows[0:1, 0:512]))
            snapV = sV.n
            s(lambda e: e.activation(rows[0:1, 0:512], rows[0:1, 1536:2048], AF.Ln,
                                     bias=misc[0:1, 2:3]), waits=[(sV, snapV)])
            s(lambda e: e.activation(rows[0:1, 0:512], rows[0:1, 0:512], AF.Exp,
                                     scale=-0.5))
            snapS = sS.n
            v(lambda e: e.tensor_mul(rows[0:1, 512:1024], rows[0:1, 1024:1536],
                                     rows[0:1, 0:512]),
              waits=[(sS, snapS)])
            return sV.n

        def bcast_rows(cA, cB, waitV):
            t(lambda e, cA=cA: e.matmul(pbc[:, 0:512], onesK1[:],
                                        rows[0:1, cA:cA + 512],
                                        start=True, stop=True),
              waits=[(sV, waitV), (sV, pbc_free[0])], incs=[])
            t(lambda e, cB=cB: e.matmul(pbc[:, 512:1024], onesK1[:],
                                        rows[0:1, cB:cB + 512],
                                        start=True, stop=True), incs=[(sT, 1)])
            return sT.n

        pbc_free = [0]

        def mm_group(dst_bank, wsrc, kslices, moving_blocks, wait_extra=()):
            for kc in range(8):
                wts = list(wait_extra) if kc == 0 else []
                t(lambda e, kc=kc: e.matmul(
                    pmm[:, dst_bank * 512:(dst_bank + 1) * 512],
                    wsrc[:, kslices + kc * 128: kslices + (kc + 1) * 128],
                    moving_blocks[:, kc, :],
                    start=(kc == 0), stop=(kc == 7)),
                  waits=wts, incs=[(sT, 1)] if kc == 7 else [])

        wl_snap = {i: (swS[i], 0) for i in range(6)}

        def wload(proj_idx, m, slot, wait_snap):
            b.dma("scalar", wch[:, slot * 1024:(slot + 1) * 1024],
                  d_wbf[proj_idx, m], swS[slot],
                  waits=[sg_w[("g", "r", "k", "v", "o")[proj_idx]],
                         (sT, wait_snap)])
            wl_snap[slot] = (swS[slot], swS[slot].n)

        wstream = []
        for _ in range(NT):
            wstream += [(2, c) for c in range(NB)] + [(3, c) for c in range(NB)]
        for _ in range(NT):
            wstream += ([(0, c) for c in range(NB)] + [(1, c) for c in range(NB)]
                        + [(4, c) for c in range(NB)])
        wpos = [0]

        def wslot():
            return wpos[0] % 6

        def wadvance():
            p = wpos[0]
            if p + 6 < len(wstream):
                pr, c = wstream[p + 6]
                wload(pr, c, (p + 6) % 6, sT.n)
            wpos[0] += 1

        b.dma("sync", xbuf[0][:], x_cols(0), sxx)
        sx_x0 = sxx.n
        s(lambda e: e.activation(hm[0][0][:, :, 0:1], xbuf[0][:, :, 0:1], AF.Copy),
          waits=[(sxx, sx_x0)])
        s(lambda e: e.activation(hm[0][1][:, :, 0:1], xbuf[0][:, :, 0:1], AF.Square))
        sS_b0 = sS.n
        for bb in range(NB):
            t(lambda e, bb=bb: e.matmul(pst[0:1, 0:1], ones128b[:], hm[0][0][:, bb, 0:1],
                                        start=(bb == 0), stop=(bb == 7)),
              waits=[(sS, sS_b0), (sgc, sgc_const)] if bb == 0 else [], incs=[])
        for bb in range(NB):
            t(lambda e, bb=bb: e.matmul(pmo[0:1, 0:1], ones128b[:], hm[0][1][:, bb, 0:1],
                                        start=(bb == 0), stop=(bb == 7)),
              incs=[(sT, 1)] if bb == 7 else [])
        sT_b0 = sT.n
        v(lambda e: e.tensor_scalar_mul(rows[0:1, 1024:1025], pst[0:1, 0:1], 1.0 / D),
          waits=[(sT, sT_b0)])
        v(lambda e: e.tensor_scalar_mul(rows[0:1, 1536:1537], pmo[0:1, 0:1], 1.0 / D))
        v(lambda e: e.tensor_mul(rows[0:1, 0:1], rows[0:1, 1024:1025],
                                 rows[0:1, 1024:1025]))
        v(lambda e: e.tensor_sub(rows[0:1, 1536:1537], rows[0:1, 1536:1537],
                                 rows[0:1, 0:1]))
        sV_b1 = sV.n
        s(lambda e: e.activation(rows[0:1, 0:1], rows[0:1, 1536:1537], AF.Ln,
                                 bias=misc[0:1, 2:3]), waits=[(sV, sV_b1)])
        s(lambda e: e.activation(rows[0:1, 0:1], rows[0:1, 0:1], AF.Exp, scale=-0.5))
        sS_b1 = sS.n
        v(lambda e: e.tensor_mul(rows[0:1, 512:513], rows[0:1, 1024:1025],
                                 rows[0:1, 0:1]),
          waits=[(sS, sS_b1)])
        sV_b2 = sV.n
        t(lambda e: e.matmul(pbc[:, 0:1], onesK1[:], rows[0:1, 0:1], start=True,
                             stop=True), waits=[(sV, sV_b2)], incs=[])
        t(lambda e: e.matmul(pbc[:, 512:513], onesK1[:], rows[0:1, 512:513], start=True,
                             stop=True), incs=[(sT, 1)])
        sT_b2 = sT.n
        v(lambda e: e.tensor_tensor(vec["hlast"], xbuf[0][:, :, 0],
                                    pbc[:, 0:1].broadcast_to([128, NB]), ALU.mult),
          waits=[(sT, sT_b2)])
        v(lambda e: e.tensor_tensor(vec["hlast"], vec["hlast"],
                                    pbc[:, 512:513].broadcast_to([128, NB]),
                                    ALU.subtract))
        v(lambda e: e.tensor_mul(vec["hlast"], vec["hlast"], vec["ca1"]))
        v(lambda e: e.tensor_add(vec["hlast"], vec["hlast"], vec["sa"]))
        pbc_free[0] = sV.n

        stats_psum_free = [sV.n]
        x2b_free = [0]
        xsum_snap = [0, 0]
        h_snap = [0, 0]
        hm_free = [0, 0]
        hsp_snap = [0]
        uadd_snap = [0]
        kv_snap = [0]
        sV_hmk = [0]
        sV_hmv = [0]
        sc_spilled = [0, 0]
        sx_x = [sx_x0, 0]

        def a1_sq(k, sq_waits=()):
            s(lambda e, k=k: e.activation(x2b[:], xbuf[k % 2][:, :, 1:T + 1],
                                          AF.Square),
              waits=list(sq_waits) + [(sT, x2b_free[0])])

        def a1_stats_rest(k):
            par = k % 2
            xcur = xbuf[par]
            sS_sq = sS.n
            for bb in range(NB):
                t(lambda e, bb=bb: e.matmul(pst[0:1, :], mc(0), xcur[:, bb, 1:T + 1],
                                            start=(bb == 0), stop=(bb == 7)),
                  waits=[(sS, sS_sq), (sV, stats_psum_free[0])] if bb == 0 else [],
                  incs=[])
            for bb in range(NB):
                t(lambda e, bb=bb: e.matmul(pmo[0:1, :], mc(0), x2b[:, bb, :],
                                            start=(bb == 0), stop=(bb == 7)),
                  incs=[(sT, 1)] if bb == 7 else [])
            snap = sT.n
            x2b_free[0] = snap
            xsum_snap[par] = snap
            v(lambda e: e.tensor_scalar_mul(rows[0:1, 1024:1536], pst[0:1, :], 1.0 / D),
              waits=[(sT, snap)])
            v(lambda e: e.tensor_scalar_mul(rows[0:1, 1536:2048], pmo[0:1, :], 1.0 / D))
            stats_psum_free[0] = sV.n
            v(lambda e: e.tensor_mul(rows[0:1, 0:512], rows[0:1, 1024:1536],
                                     rows[0:1, 1024:1536]))
            v(lambda e: e.tensor_sub(rows[0:1, 1536:2048], rows[0:1, 1536:2048],
                                     rows[0:1, 0:512]))
            snapV = sV.n
            s(lambda e: e.activation(rows[0:1, 0:512], rows[0:1, 1536:2048], AF.Ln,
                                     bias=misc[0:1, 2:3]), waits=[(sV, snapV)])
            s(lambda e: e.activation(rows[0:1, 0:512], rows[0:1, 0:512], AF.Exp,
                                     scale=-0.5))
            snapS = sS.n
            v(lambda e: e.tensor_mul(rows[0:1, 512:1024], rows[0:1, 1024:1536],
                                     rows[0:1, 0:512]),
              waits=[(sS, snapS)])
            return bcast_rows(0, 512, sV.n)

        def a1_happly(k, bsn):
            par = k % 2
            xcur = xbuf[par]
            v(lambda e: e.tensor_tensor(hb[:, :, 1:T + 1], xcur[:, :, 1:T + 1],
                                        bc3(pbc[:, 0:512]), ALU.mult),
              waits=[(sT, bsn), (sxhs, hsp_snap[0]),
                     ((sxx, sxh)[par], sx_x[par])])
            v(lambda e: e.tensor_tensor(hb[:, :, 1:T + 1], hb[:, :, 1:T + 1],
                                        bc3(pbc[:, 512:1024]), ALU.subtract))
            pbc_free[0] = sV.n
            for bb in range(NB):
                v(lambda e, bb=bb: e.tensor_scalar(hb[:, bb, 1:T + 1],
                                                   hb[:, bb, 1:T + 1],
                                                   vb("ca1", bb), vb("sa", bb),
                                                   ALU.mult, ALU.add))
            v(lambda e: e.tensor_copy(hb[:, :, 0:1], vec["hlast"].unsqueeze(2)))
            v(lambda e: e.tensor_copy(vec["hlast"].unsqueeze(2), hb[:, :, T:T + 1]))
            h_snap[par] = sV.n
            b.dma("sync", d_hsp[k].rearrange("(b p) t -> p b t", p=128), hb[:],
                  sxhs, waits=[(sV, sV.n)])
            hsp_snap[0] = sxhs.n

        def a1_hmix(k):
            par = k % 2
            db = scb[par]
            v(lambda e, db=db: e.tensor_tensor(db[:], hb[:, :, 1:T + 1],
                                               hb[:, :, 0:T], ALU.subtract),
              waits=[(sxc[par], sc_spilled[par])])
            for bb in range(NB):
                v(lambda e, bb=bb, par=par, db=db: e.scalar_tensor_tensor(
                    hm[par][0][:, bb, :], db[:, bb, :], vb("mk", bb),
                    hb[:, bb, 0:T], ALU.mult, ALU.add),
                  waits=[(sT, hm_free[par])] if bb == 0 else [])
            sV_hmk[0] = sV.n
            for bb in range(NB):
                v(lambda e, bb=bb, par=par, db=db: e.scalar_tensor_tensor(
                    hm[par][1][:, bb, :], db[:, bb, :], vb("mv", bb),
                    hb[:, bb, 0:T], ALU.mult, ALU.add))
            sV_hmv[0] = sV.n

        for p0 in range(6):
            pr, c = wstream[p0]
            wload(pr, c, p0, 0)

        qmax_it_issue = [0]
        qmax_it_reduce = [0]
        qmax_vfree = [0, 0]
        qmax_snap = {}

        def qmax_issue():
            it = qmax_it_issue[0]
            if it >= 64:
                return
            assert it - qmax_it_reduce[0] <= 1
            qmax_it_issue[0] += 1
            if it < 32:
                src = d_wfc[(it // 4) * 128:(it // 4 + 1) * 128,
                            (it % 4) * 1024:(it % 4 + 1) * 1024]
            else:
                i2 = it - 32
                src = d_wpj[i2 * 128:(i2 + 1) * 128, :]
            half = it % 2
            qsem = (sgq, sgq2)[half]
            b.dma("gpsimd", qst[:, half * 1024:(half + 1) * 1024],
                  src, qsem, waits=[(sV, qmax_vfree[half])])
            qmax_snap[it] = (qsem, qsem.n)

        def qmax_reduce():
            it = qmax_it_reduce[0]
            if it >= 64 or qmax_it_issue[0] == it:
                return
            qmax_it_reduce[0] += 1
            acc = q_["mxfc"] if it < 32 else q_["mxpj"]
            half = it % 2
            v(lambda e, half=half: e.reduce_max(q_["t2"][:],
                                                qst[:, half * 1024:(half + 1) * 1024],
                                                AX.X, apply_absolute_value=True),
              waits=[qmax_snap[it]])
            v(lambda e, acc=acc: e.tensor_max(acc[:], acc[:], q_["t2"][:]))
            qmax_vfree[half] = sV.n

        a1_sq(0, sq_waits=[(sxx, sx_x0)])
        bsn0 = a1_stats_rest(0)
        a1_happly(0, bsn0)
        b.dma("sync", xbuf[1][:], x_cols(1), sxh, waits=[(sxps, sx_powsp)])
        sx_x[1] = sxh.n
        hm_free[0] = sT_b2
        a1_hmix(0)
        a1_sq(1, sq_waits=[(sxh, sx_x[1])])
        kevac = {m: 0 for m in range(NB)}
        vevac = {m: 0 for m in range(NB)}
        bsn_next = [0]

        for j in range(NT):
            p = j % 2
            hmA_j, hmB_j = hm[p]
            sV_hmk_j, sV_hmv_j = sV_hmk[0], sV_hmv[0]
            if j + 2 < NT:
                b.dma("sync", xbuf[p][:], x_cols(j + 2), (sxx, sxh)[p],
                      waits=[(sV, h_snap[p]), (sT, xsum_snap[p])])
                sx_x[p] = (sxx, sxh)[p].n
            if j + 1 < NT:
                bsn_next[0] = a1_stats_rest(j + 1)
            if j + 2 < NT:
                a1_sq(j + 2, sq_waits=[((sxx, sxh)[p], sx_x[p])])
            qmax_issue()
            qmax_issue()
            for m in range(NB):
                slot = wslot()
                mm_group(m % 4, wch, slot * 1024, hmA_j,
                         wait_extra=[wl_snap[slot], (sV, sV_hmk_j),
                                     (sS, vevac[4 + m % 4] if m < 4
                                      else kevac[m - 4])])
                wadvance()
                snap = sT.n
                s(lambda e, m=m: e.activation(ka[:, m, :],
                                              pmm[:, (m % 4) * 512:(m % 4 + 1) * 512],
                                              AF.Identity, bias=vb("bk", m)),
                  waits=[(sT, snap)] + ([(sV, kv_snap[0])] if m == 0 else []))
                kevac[m] = sS.n
            for m in range(NB):
                slot = wslot()
                mm_group(m % 4, wch, slot * 1024, hmB_j,
                         wait_extra=[wl_snap[slot], (sV, sV_hmv_j),
                                     (sS, kevac[NB - 1] if m < 4 else vevac[m - 4])])
                wadvance()
                snap = sT.n
                s(lambda e, m=m: e.activation(va[:, m, :],
                                              pmm[:, (m % 4) * 512:(m % 4 + 1) * 512],
                                              AF.Identity, bias=vb("bv", m)),
                  waits=[(sT, snap)] + ([(sV, kv_snap[0])] if m == 0 else []))
                vevac[m] = sS.n
            hm_free[p] = sT.n
            if j + 1 < NT:
                a1_happly(j + 1, bsn_next[0])
                a1_hmix(j + 1)
            qmax_reduce()
            qmax_reduce()
            qmax_issue()
            qmax_issue()
            v(lambda e, p=p: e.tensor_tensor(scb[1 - p][:], ka[:], va[:],
                                             ALU.mult),
              waits=[(sS, vevac[NB - 1])] +
                    ([(sxc[1 - p], sc_spilled[1 - p])] if j + 1 >= NT else []))
            kv_snap[0] = sV.n
            for bb in range(NB):
                v(lambda e, bb=bb, p=p: e.tensor_tensor_scan(
                    scb[p][:, bb, :], vb("dec", bb).broadcast_to([128, T]),
                    scb[1 - p][:, bb, :], vb("carry", bb), ALU.mult, ALU.add))
            v(lambda e, p=p: e.tensor_copy(vec["carry"], scb[p][:, :, T - 1]))
            for bb in range(NB):
                v(lambda e, bb=bb, p=p: e.scalar_tensor_tensor(
                    scb[p][:, bb, :], scb[1 - p][:, bb, :], vb("uv", bb),
                    scb[p][:, bb, :], ALU.mult, ALU.add))
            uadd_snap[0] = sV.n
            b.dma("sync", d_scsp[j].rearrange("(b p) t -> p b t", p=128), scb[p][:],
                  sxc[p], waits=[(sV, uadd_snap[0])])
            sc_spilled[p] = sxc[p].n
            qmax_reduce()
            qmax_reduce()

        b.dma("sync", d_ccin.rearrange("(b p) one -> p (b one)", p=128),
              vec["carry"], sxb, waits=[(sV, sV.n)], slow=True)
        sx_cc = sxb.n
        b.op("gpsimd",
             lambda e: e.collective_compute(
                 "AllGather", ALU.bypass,
                 replica_groups=[[0, 1], [2, 3], [4, 5], [6, 7]],
                 ins=[d_ccin], outs=[d_ccout]),
             waits=[(sxb, sx_cc), (sG, sG.n)], incs=[(scc, 1)], is_dma=True)
        b.dma("sync", vec["Din"],
              d_ccout[0:D].rearrange("(b p) one -> p (b one)", p=128), sxb,
              waits=[(scc, 2)], slow=True)
        sx_D = sxb.n
        v(lambda e: e.tensor_mul(vec["Din"], vec["Din"], vec["recvmask"]),
          waits=[(sxb, sx_D)])
        sV_D = sV.n

        b.dma("sync", pow_, d_powsp.rearrange("(b p) t -> p b t", p=128), sxps,
              waits=[(sV, h_snap[1]), (sT, xsum_snap[1]), (sxps, sx_powsp)])
        sx_powrl = sxps.n

        sx_h = [0]
        sx_sc = [0, 0]
        sx_x2free = [0]
        sx_xA2 = [0]
        sG_rw = [0, 0]
        corr_snapA = [0, 0]
        hmg_snapA = [0, 0]
        hmr_snapA = [0, 0]
        gnchain_snap = [0]
        gncast_snap = [0]
        ln2cast_snap = [0]
        ln2sums_snap = [0]
        ln2_vsnap = [0]
        x2add_snap = [0]
        woevac = {m: 0 for m in range(NB)}
        olad_free = [hm_free[0], hm_free[1]]
        sV_D_cur = [sV_D]

        b.dma("sync", hb[:], d_hsp[0].rearrange("(b p) t -> p b t", p=128), sxh,
              waits=[(sV, sV.n), (sxhs, sxhs.n)])
        sx_h[0] = sxh.n
        b.dma("sync", scb[0][:], d_scsp[0].rearrange("(b p) t -> p b t", p=128),
              sxc[0], waits=[(sxc[0], sxc[0].n), (sV, sV.n)])
        sx_sc[0] = sxc[0].n
        b.dma("sync", xbuf[0][:, :, 1:T + 1], x_cols_b(0), sxx,
              waits=[(sV, h_snap[0]), (sT, xsum_snap[0])])
        sx_xA2[0] = sxx.n

        def a2_corr(k):
            par = k % 2
            for bb in range(NB):
                v(lambda e, bb=bb, par=par: e.scalar_tensor_tensor(
                    scb[par][:, bb, :], pow_[:, bb, :], vb("Din", bb),
                    scb[par][:, bb, :], ALU.mult, ALU.add),
                  waits=([(sxc[par], sx_sc[par]), (sV, sV_D_cur[0])] +
                         ([(sxps, sx_powrl)] if k == 0 else [])) if bb == 0 else [])
            corr_snapA[par] = sV.n
            v(lambda e: e.tensor_mul(vec["Din"], vec["Din"], vec["d512"]))
            sV_D_cur[0] = sV.n

        def a2_delta_hmix(k):
            par = k % 2
            v(lambda e: e.tensor_tensor(x2b[:], hb[:, :, 1:T + 1],
                                        hb[:, :, 0:T], ALU.subtract),
              waits=[(sxh, sx_h[0]), (sS, ln2cast_snap[0]),
                     (sx2, sx_x2free[0]), (sT, x2b_free[0])])
            for bb in range(NB):
                v(lambda e, bb=bb, par=par: e.scalar_tensor_tensor(
                    hm[par][0][:, bb, :], x2b[:, bb, :], vb("mg", bb),
                    hb[:, bb, 0:T], ALU.mult, ALU.add),
                  waits=[(sT, olad_free[par])] if bb == 0 else [])
            hmg_snapA[par] = sV.n
            for bb in range(NB):
                v(lambda e, bb=bb, par=par: e.scalar_tensor_tensor(
                    hm[par][1][:, bb, :], x2b[:, bb, :], vb("mr", bb),
                    hb[:, bb, 0:T], ALU.mult, ALU.add))
            hmr_snapA[par] = sV.n

        a2_corr(0)
        a2_delta_hmix(0)
        b.dma("sync", hb[:], d_hsp[1].rearrange("(b p) t -> p b t", p=128), sxh,
              waits=[(sV, hmr_snapA[0])])
        sx_h[0] = sxh.n
        b.dma("sync", scb[1][:], d_scsp[1].rearrange("(b p) t -> p b t", p=128),
              sxc[1], waits=[(sxc[1], sxc[1].n)])
        sx_sc[1] = sxc[1].n

        for j in range(NT):
            p = j % 2
            hmc = hm[p]
            qmax_issue()
            qmax_issue()
            if j + 1 < NT:
                a2_corr(j + 1)
                a2_delta_hmix(j + 1)
            if j >= 1:
                v(lambda e: e.tensor_tensor(x2b[:], xbuf[0][:, :, 1:T + 1],
                                            va[:], ALU.add),
                  waits=[(sS, woevac[NB - 1]), (sxx, sx_xA2[0]),
                         (sx2, sx_x2free[0])])
                if DBG and j == 1:
                    b.dma("scalar", d_a2dbg[4][0:512, :].rearrange(
                        "(b p) t -> p b t", p=128),
                        va[:, 4:8, :], swm[1], waits=[(sV, sV.n)])
                    b.op("vector", None, waits=[(swm[1], swm[1].n)],
                         incs=[(sV, 1)])
                x2add_snap[0] = sV.n
                b.dma("sync",
                      d_x2sp[:, (j - 1) * T:j * T].rearrange("(b p) t -> p b t", p=128),
                      x2b[:], sx2, waits=[(sV, x2add_snap[0])])
                sx_x2free[0] = sx2.n
                b.dma("sync", xbuf[0][:, :, 1:T + 1], x_cols_b(j), sxx,
                      waits=[(sV, x2add_snap[0])])
                sx_xA2[0] = sxx.n
            gevac = {}
            for m in range(NB):
                slot = wslot()
                mm_group(m % 4, wch, slot * 1024, hmc[0],
                         wait_extra=[wl_snap[slot], (sV, hmg_snapA[p])] +
                                    ([(sV, gnchain_snap[0]),
                                      (sS, woevac[4 + m % 4])] if m < 4
                                     else [(sS, gevac[m - 4])]))
                wadvance()
                snap = sT.n
                s(lambda e, m=m: e.activation(ka[:, m, :],
                                              pmm[:, (m % 4) * 512:(m % 4 + 1) * 512],
                                              AF.Tanh, scale=0.5, bias=vb("bg2", m)),
                  waits=[(sT, snap)] + ([(sV, gnchain_snap[0])] if m == 0 else []))
                gevac[m] = sS.n
            revac = {}
            for m in range(NB):
                slot = wslot()
                mm_group(m % 4, wch, slot * 1024, hmc[1],
                         wait_extra=[wl_snap[slot], (sV, hmr_snapA[p]),
                                     (sS, gevac[NB - 1] if m < 4 else revac[m - 4])])
                wadvance()
                snap = sT.n
                s(lambda e, m=m: e.activation(va[:, m, :],
                                              pmm[:, (m % 4) * 512:(m % 4 + 1) * 512],
                                              AF.Identity, bias=vb("br", m)),
                  waits=[(sT, snap)] +
                        ([(sV, x2add_snap[0] if j >= 1 else sV.n)] if m == 0 else []))
                revac[m] = sS.n
            rlad_snap = sT.n
            if j >= 1:
                s(lambda e, hmc=hmc: e.activation(hmc[0][:], x2b[:], AF.Copy),
                  waits=[(sV, x2add_snap[0]), (sT, rlad_snap)])
                s(lambda e, hmc=hmc: e.activation(hmc[1][:], x2b[:], AF.Square))
                ln2cast_snap[0] = sS.n
                ln2_vsnap[0] = ln_stats(hmc[0], hmc[1])
                ln2sums_snap[0] = sT.n
                b.dma("sync",
                      d_bounce[(j - 1) * 512:j * 512].rearrange("t one -> one t"),
                      rows[0:1, 0:512], sxr, waits=[(sV, sV.n)])
                b.dma("sync",
                      d_bounce[(8 + j - 1) * 512:(9 + j - 1) * 512].rearrange("t one -> one t"),
                      rows[0:1, 512:1024], sxr)
            g(lambda e, p=p: e.tensor_tensor(scb[p][:], va[:], scb[p][:], ALU.mult),
              waits=[(sS, revac[NB - 1]), (sV, corr_snapA[p])])
            sG_rw[p] = sG.n
            s(lambda e, p=p, hmc=hmc: e.activation(hmc[0][:], scb[p][:], AF.Copy),
              waits=[(sG, sG_rw[p]), (sT, ln2sums_snap[0] if j >= 1 else rlad_snap)])
            s(lambda e, p=p, hmc=hmc: e.activation(hmc[1][:], scb[p][:], AF.Square))
            sS_rwc = sS.n
            for bb in range(NB):
                t(lambda e, bb=bb, hmc=hmc: e.matmul(pst[0:16, :], gsel[:, bb * 16:(bb + 1) * 16],
                                            hmc[0][:, bb, :], start=(bb == 0),
                                            stop=(bb == 7)),
                  waits=[(sS, sS_rwc), (sV, ln2_vsnap[0] if j >= 1 else sV.n),
                         (sgc, sgc_const)] if bb == 0 else [],
                  incs=[])
            for bb in range(NB):
                t(lambda e, bb=bb, hmc=hmc: e.matmul(pmo[0:16, :], gsel[:, bb * 16:(bb + 1) * 16],
                                            hmc[1][:, bb, :], start=(bb == 0),
                                            stop=(bb == 7)),
                  incs=[(sT, 1)] if bb == 7 else [])
            sT_gn = sT.n
            v(lambda e: e.tensor_scalar_mul(rows[0:16, 1024:1536], pst[0:16, :],
                                            1.0 / HD),
              waits=[(sT, sT_gn), (sxr, sxr.n)])
            v(lambda e: e.tensor_scalar_mul(rows[0:16, 1536:2048], pmo[0:16, :],
                                            1.0 / HD))
            v(lambda e: e.tensor_mul(rows[0:16, 0:512], rows[0:16, 1024:1536],
                                     rows[0:16, 1024:1536]))
            v(lambda e: e.tensor_sub(rows[0:16, 1536:2048], rows[0:16, 1536:2048],
                                     rows[0:16, 0:512]))
            sV_gn1 = sV.n
            s(lambda e: e.activation(rows[0:16, 0:512], rows[0:16, 1536:2048], AF.Ln,
                                     bias=misc[0:16, 2:3]), waits=[(sV, sV_gn1)])
            s(lambda e: e.activation(rows[0:16, 0:512], rows[0:16, 0:512], AF.Exp,
                                     scale=-0.5))
            sS_gn = sS.n
            v(lambda e: e.tensor_mul(rows[0:16, 512:1024], rows[0:16, 1024:1536],
                                     rows[0:16, 0:512]),
              waits=[(sS, sS_gn)])
            sV_gn2 = sV.n
            if DBG and j == 0:
                b.dma("sync", d_a2dbg[4][0:16, :], rows[0:16, 0:512], sxb,
                      waits=[(sV, sV_gn2)])
                b.dma("sync", d_a2dbg[4][16:32, :], rows[0:16, 512:1024], sxb)
                b.op("vector", None, waits=[(sxb, sxb.n)], incs=[(sV, 1)])
                b.op("scalar", None, waits=[(sxb, sxb.n)], incs=[(sS, 1)])
            s(lambda e: e.activation(ka[:], ka[:], AF.Identity, bias=mc(0)),
              waits=[(sT, sT.n)])
            sS_ka1 = sS.n
            if DBG and j == 0:
                b.dma("sync", d_a2dbg[0].rearrange("(b p) t -> p b t", p=128),
                      ka[:], sxb, waits=[(sS, sS_ka1)])
                b.dma("sync", d_a2dbg[1].rearrange("(b p) t -> p b t", p=128),
                      va[:], sxb, waits=[(sS, revac[NB - 1])])
                b.dma("sync", d_a2dbg[2].rearrange("(b p) t -> p b t", p=128),
                      scb[p][:], sxb, waits=[(sG, sG_rw[p])])
                b.op("scalar", None, waits=[(sxb, sxb.n)], incs=[(sS, 1)])
                b.op("vector", None, waits=[(sxb, sxb.n)], incs=[(sV, 1)])
                b.op("gpsimd", None, waits=[(sxb, sxb.n)], incs=[(sG, 1)])
            gnchain = {}
            for bb in range(NB):
                rot = bb % 3
                tgt = (pbc, pmm, pmm)[rot]
                o0 = (0, 0, 2 * 512)[rot]
                o1 = o0 + 512
                t(lambda e, bb=bb, tgt=tgt, o0=o0: e.matmul(
                    tgt[:, o0:o0 + 512], gselT[:, bb * 128:(bb + 1) * 128],
                    rows[0:16, 0:512], start=True, stop=True),
                  waits=[(sV, sV_gn2 if bb < 3 else gnchain[bb - 3])], incs=[])
                t(lambda e, bb=bb, tgt=tgt, o1=o1: e.matmul(
                    tgt[:, o1:o1 + 512], gselT[:, bb * 128:(bb + 1) * 128],
                    rows[0:16, 512:1024], start=True, stop=True), incs=[(sT, 1)])
                snapT = sT.n
                v(lambda e, bb=bb, p=p, tgt=tgt, o0=o0: e.tensor_tensor(
                    tmp[:, 0:T], scb[p][:, bb, :], tgt[:, o0:o0 + 512], ALU.mult),
                  waits=[(sT, snapT), (sG, sG_rw[p])] +
                        ([(sS, sS_ka1)] if bb == 0 else []))
                v(lambda e, tgt=tgt, o1=o1: e.tensor_tensor(
                    tmp[:, T:2 * T], tmp[:, 0:T], tgt[:, o1:o1 + 512], ALU.subtract))
                v(lambda e, bb=bb, hmc=hmc: e.tensor_tensor(
                    hmc[1][:, bb, :], tmp[:, T:2 * T], ka[:, bb, :], ALU.mult))
                gnchain[bb] = sV.n
            sV_apre = sV.n
            gnchain_snap[0] = sV_apre
            if DBG and j == 0:
                for bb2 in range(NB):
                    s(lambda e, bb2=bb2, hmc=hmc: e.activation(x2b[:, bb2, :],
                                                      hmc[1][:, bb2, :], AF.Copy),
                      waits=[(sV, sV_apre)] if bb2 == 0 else [])
                b.dma("sync", d_a2dbg[3].rearrange("(b p) t -> p b t", p=128),
                      x2b[:], sxb, waits=[(sS, sS.n)])
                b.op("vector", None, waits=[(sxb, sxb.n)], incs=[(sV, 1)])
                b.op("tensor", None, waits=[(sxb, sxb.n)])
            for m in range(NB):
                slot = wslot()
                mm_group(m % 4, wch, slot * 1024, hmc[1],
                         wait_extra=[wl_snap[slot], (sV, sV_apre),
                                     (sS, sS_rwc if m < 4 else woevac[m - 4])])
                wadvance()
                snap = sT.n
                s(lambda e, m=m: e.activation(va[:, m, :],
                                              pmm[:, (m % 4) * 512:(m % 4 + 1) * 512],
                                              AF.Identity, scale=vb("gah", m),
                                              bias=vb("gabo", m)),
                  waits=[(sT, snap), (sV, gnchain[NB - 1]), (sG, sG_rw[p])])
                woevac[m] = sS.n
            olad_free[p] = sT.n
            if DBG and j == 0:
                b.dma("scalar", d_a2dbg[4][512:1024, :].rearrange(
                    "(b p) t -> p b t", p=128),
                    va[:, 0:4, :], swm[1], waits=[(sS, woevac[NB - 1])])
                b.op("vector", None, waits=[(swm[1], swm[1].n)], incs=[(sV, 1)])
            qmax_reduce()
            qmax_reduce()
            qmax_issue()
            qmax_issue()
            if j + 2 < NT:
                b.dma("sync", hb[:], d_hsp[j + 2].rearrange("(b p) t -> p b t", p=128),
                      sxh, waits=[(sV, hmr_snapA[(j + 1) % 2])])
                sx_h[0] = sxh.n
                b.dma("sync", scb[p][:],
                      d_scsp[j + 2].rearrange("(b p) t -> p b t", p=128), sxc[p],
                      waits=[(sxc[p], sxc[p].n), (sV, gnchain_snap[0]),
                             (sS, sS_rwc), (sG, sG_rw[p])])
                sx_sc[p] = sxc[p].n
            qmax_reduce()
            qmax_reduce()

        v(lambda e: e.tensor_tensor(x2b[:], xbuf[0][:, :, 1:T + 1], va[:], ALU.add),
          waits=[(sS, woevac[NB - 1]), (sxx, sx_xA2[0]), (sx2, sx_x2free[0])])
        x2add_snap[0] = sV.n
        b.dma("sync", d_x2sp[:, 7 * T:8 * T].rearrange("(b p) t -> p b t", p=128),
              x2b[:], sx2, waits=[(sV, x2add_snap[0])])
        sx_x2free[0] = sx2.n
        s(lambda e: e.activation(hm[1][0][:], x2b[:], AF.Copy),
          waits=[(sV, x2add_snap[0]), (sT, sT.n)])
        s(lambda e: e.activation(hm[1][1][:], x2b[:], AF.Square))
        _ = ln_stats(hm[1][0], hm[1][1])
        b.dma("sync",
              d_bounce[7 * 512:8 * 512].rearrange("t one -> one t"),
              rows[0:1, 0:512], sxr, waits=[(sV, sV.n)])
        b.dma("sync",
              d_bounce[15 * 512:16 * 512].rearrange("t one -> one t"),
              rows[0:1, 512:1024], sxr)

        for _ in range(128):
            qmax_reduce()
            qmax_issue()
        qmax_reduce()
        b.dma("sync", d_bounce[7168:7296], q_["mxfc"][:], sxb, waits=[(sV, sV.n)])
        b.dma("sync", d_bounce[7296:7424], q_["mxpj"][:], sxb)
        sx_mxw = sxb.n
        b.dma("sync", rows[0:1, 0:128],
              d_bounce[7168:7296].rearrange("p one -> one p"), sxb,
              waits=[(sxb, sx_mxw), (sV, sV.n), (sxr, sxr.n)], slow=True)
        b.dma("sync", rows[0:1, 128:256],
              d_bounce[7296:7424].rearrange("p one -> one p"), sxb, slow=True)
        sx_mx = sxb.n
        v(lambda e: e.reduce_max(rows[0:1, 300:301], rows[0:1, 0:128], AX.X),
          waits=[(sxb, sx_mx)])
        v(lambda e: e.reduce_max(rows[0:1, 301:302], rows[0:1, 128:256], AX.X))
        v(lambda e: e.tensor_scalar_mul(rows[0:1, 302:303], rows[0:1, 300:301], 1.0 / 127.0))
        v(lambda e: e.tensor_scalar_mul(rows[0:1, 303:304], rows[0:1, 301:302], 1.0 / 127.0))
        v(lambda e: e.reciprocal(rows[0:1, 304:305], rows[0:1, 300:301]))
        v(lambda e: e.reciprocal(rows[0:1, 305:306], rows[0:1, 301:302]))
        v(lambda e: e.tensor_scalar_mul(rows[0:1, 304:305], rows[0:1, 304:305], 127.0))
        v(lambda e: e.tensor_scalar_mul(rows[0:1, 305:306], rows[0:1, 305:306], 127.0))
        sV_q = sV.n
        t(lambda e: e.matmul(pmo[:, 0:1], onesK1[:], rows[0:1, 302:303], start=True,
                             stop=True), waits=[(sV, sV_q)], incs=[])
        t(lambda e: e.matmul(pmo[:, 1:2], onesK1[:], rows[0:1, 304:305], start=True,
                             stop=True), incs=[])
        t(lambda e: e.matmul(pmo[:, 2:3], onesK1[:], rows[0:1, 303:304], start=True,
                             stop=True), incs=[])
        t(lambda e: e.matmul(pmo[:, 3:4], onesK1[:], rows[0:1, 305:306], start=True,
                             stop=True), incs=[(sT, 1)])
        sT_q = sT.n
        s(lambda e: e.activation(q_["sfc"][:], pmo[:, 0:1], AF.Identity),
          waits=[(sT, sT_q)])
        s(lambda e: e.activation(q_["invfc"][:], pmo[:, 1:2], AF.Identity))
        s(lambda e: e.activation(q_["spj"][:], pmo[:, 2:3], AF.Identity))
        s(lambda e: e.activation(q_["invpj"][:], pmo[:, 3:4], AF.Identity))
        sS_q = sS.n
        b.op("sync", None,
             waits=[(c, c.n) for c in (sxb, sxx, sxh, sxhs, sxc[0], sxc[1],
                                       sx2, sxr, sxps)], incs=[(bar, 1)])
        b.op("gpsimd", None,
             waits=[(sG, sG.n)] + [(c, c.n) for c in sgw + [sgc, sgq, sgq2]],
             incs=[(bar, 1)])
        b.op("vector", None, waits=[(sV, sV.n)], incs=[(bar, 1)])
        b.op("scalar", None,
             waits=[(sS, sS_q)] + [(c, c.n) for c in swS + swm],
             incs=[(bar, 1)])
        b.op("tensor", None, waits=[(sT, sT.n)], incs=[(bar, 1)])

    es_b = ExitStack()
    with es_b:
        sbb = lambda n, sh, dt=F32: es_b.enter_context(nc.sbuf_tensor(n, sh, dt))
        wfce = sbb("wfce", [128, NB * FF], BF16)
        wpjc = sbb("wpjc", [128, 2 * NFB * 128], BF16)
        wps = sbb("wps", [128, 2 * 1024], BF16)
        h1 = sbb("h1", [128, NFB * 512], BF16)
        h2 = sbb("h2", [128, NB, T], BF16)
        x2t = sbb("x2t", [128, NB, T])
        outt = sbb("outt", [128, NB, T])
        qw = sbb("qw", [128, 2 * 1024])
        qt1 = sbb("qt1", [128, 1024])
        lafc = sbb("lafc", [16, D], BF16)
        lbpjT = sbb("lbpjT", [16, D], BF16)
        lbf_s = sbb("lbf_s", [16, 1024], BF16)
        lap_s = sbb("lap_s", [16, 2 * 128], BF16)

        bw = [(bar, 5)]
        first = {"sync": bw, "gpsimd": bw, "vector": bw, "scalar": bw, "tensor": bw}

        def v(fn, waits=(), incs=((sV, 1),)):
            w = list(waits) + first.pop("vector", [])
            b.op("vector", fn, waits=w, incs=incs)

        def s(fn, waits=(), incs=((sS, 1),)):
            w = list(waits) + first.pop("scalar", [])
            b.op("scalar", fn, waits=w, incs=incs)

        def t(fn, waits=(), incs=()):
            w = list(waits) + first.pop("tensor", [])
            b.op("tensor", fn, waits=w, incs=incs)

        def g(fn, waits=(), incs=((sG, 1),)):
            w = list(waits) + first.pop("gpsimd", [])
            b.op("gpsimd", fn, waits=w, incs=incs)

        def dma_q(eng, out_ap, in_ap, sem, waits=()):
            w = list(waits) + first.pop(eng, [])
            b.op(eng, lambda e, o=out_ap, i=in_ap: e.dma_start(out=o, in_=i),
                 waits=w, incs=[(sem, 16)], is_dma=True)

        dma_q("gpsimd", lafc[:], d_afc, sgc)
        dma_q("gpsimd", lbpjT[:], d_bpjT, sgc)
        sg_lora = sgc.n
        v(lambda e: e.tensor_scalar_mul(lafc[:], lafc[:], 2.0), waits=[(sgc, sg_lora)])
        v(lambda e: e.tensor_scalar_mul(lbpjT[:], lbpjT[:], 2.0))
        sV_lora = sV.n

        qunit_v = {}
        lbf_snap = [0]

        def quant_unit(kind, og, i, uidx):
            if kind == "fc":
                wsrc = d_wfc[i * 128:(i + 1) * 128, og * 1024:(og + 1) * 1024]
                stat = lafc[:, i * 128:(i + 1) * 128]
                mov0, mov1 = lbf_s[:, 0:512], lbf_s[:, 512:1024]
                dst = wfce[:, i * FF + og * 1024: i * FF + (og + 1) * 1024]
                sc, iv = q_["sfc"], q_["invfc"]
                wmov = [(sgl[2], lbf_snap[0])]
            else:
                wsrc = d_wpj[i * 128:(i + 1) * 128, 0:1024]
                half2 = i % 2
                dma_q("gpsimd", lap_s[:, half2 * 128:(half2 + 1) * 128],
                      d_apj[:, i * 128:(i + 1) * 128], sgl[half2],
                      waits=[(sT, qunit_t.get(uidx - 2, 0))])
                stat = lap_s[:, half2 * 128:(half2 + 1) * 128]
                mov0, mov1 = lbpjT[:, 0:512], lbpjT[:, 512:1024]
                dst = wps[:, half2 * 1024:(half2 + 1) * 1024]
                sc, iv = q_["spj"], q_["invpj"]
                wmov = [(sgl[half2], sgl[half2].n), (sV, sV_lora),
                        (sxp[half2], wps_free[half2])]
            half = uidx % 2
            dma_q(("sync", "scalar")[half],
                  qw[:, half * 1024:(half + 1) * 1024], wsrc, sxq[half],
                  waits=[(sV, qunit_v.get(uidx - 2, 0))])
            snapx = (sxq[half], sxq[half].n)
            t(lambda e, stat=stat, mov0=mov0: e.matmul(
                pbc[:, 0:512], stat, mov0, start=True, stop=True),
              waits=[(sV, qunit_v.get(uidx - 1, 0)), (sV, sV_lora)] + wmov, incs=[])
            t(lambda e, stat=stat, mov1=mov1: e.matmul(
                pbc[:, 512:1024], stat, mov1, start=True, stop=True),
              incs=[(sT, 1)])
            snapT = sT.n
            qunit_t[uidx] = snapT
            s(lambda e, half=half, iv=iv: e.activation(
                qt1[:], qw[:, half * 1024:(half + 1) * 1024], AF.Identity,
                scale=iv[:], bias=q_["Mpp"][:]),
              waits=[snapx, (sV, qunit_v.get(uidx - 1, 0))])
            snapS = sS.n
            v(lambda e, sc=sc: e.tensor_scalar(qt1[:], qt1[:], q_["Mpp"][:], sc[:],
                                               ALU.subtract, ALU.mult),
              waits=[(sS, snapS)])
            v(lambda e, dst=dst: e.tensor_tensor(dst, qt1[:], pbc[:, 0:1024], ALU.add),
              waits=[(sT, snapT)])
            qunit_v[uidx] = sV.n
            if kind == "pj":
                for mo in range(NB):
                    b.dma("sync", d_wpjbf[mo, :, i * 128:(i + 1) * 128],
                          dst[:, mo * 128:(mo + 1) * 128], sxp[half2],
                          waits=[(sV, qunit_v[uidx])] if mo == 0 else [])
                wps_free[half2] = sxp[half2].n

        qunit_t = {}
        wps_free = [0, 0]
        uidx = 0
        for og in range(4):
            dma_q("gpsimd", lbf_s[:], d_bfcT[:, og * 1024:(og + 1) * 1024], sgl[2],
                  waits=[(sT, qunit_t.get(uidx - 1, 0))])
            lbf_snap[0] = sgl[2].n
            for i in range(NB):
                quant_unit("fc", og, i, uidx)
                uidx += 1
        fc_ready = {og: qunit_v[(og + 1) * NB - 1] for og in range(4)}
        for i in range(NFB):
            quant_unit("pj", 0, i, uidx)
            uidx += 1
        pj_ready = (sxp[0].n, sxp[1].n)

        sx_x2 = [0]
        wpjc_used = [0, 0]
        dma_q("sync", x2t[:], d_x2sp[:, 0:T].rearrange("(b p) t -> p b t", p=128), sxx)
        sx_x2[0] = sxx.n
        outfree = [0]
        rowsnap = [0]
        for j in range(NT):
            sx_xj = sx_x2[0]
            dma_q("sync", rows[0:1, 0:512],
                  d_bounce[j * 512:(j + 1) * 512].rearrange("t one -> one t"),
                  sxr, waits=[(sT, rowsnap[0])])
            dma_q("sync", rows[0:1, 512:1024],
                  d_bounce[(8 + j) * 512:(9 + j) * 512].rearrange("t one -> one t"),
                  sxr)
            sx_rows = sxr.n
            t(lambda e: e.matmul(pmo[:, 0:512], onesK1[:], rows[0:1, 0:512],
                                 start=True, stop=True),
              waits=[(sxr, sx_rows)], incs=[(sT, 1)])
            snapA = sT.n
            v(lambda e: e.tensor_tensor(outt[:], x2t[:], bc3(pmo[:, 0:512]), ALU.mult),
              waits=[(sT, snapA), (sxx, sx_xj), (sout, outfree[0])])
            snapVA = sV.n
            t(lambda e: e.matmul(pmo[:, 0:512], onesK1[:], rows[0:1, 512:1024],
                                  start=True, stop=True),
              waits=[(sV, snapVA)], incs=[(sT, 1)])
            snapB = sT.n
            rowsnap[0] = sT.n
            v(lambda e: e.tensor_tensor(outt[:], outt[:], bc3(pmo[:, 0:512]),
                                        ALU.subtract), waits=[(sT, snapB)])
            for bb in range(NB):
                v(lambda e, bb=bb: e.tensor_scalar(h2[:, bb, :], outt[:, bb, :],
                                                   vb("cb1", bb), vb("sb_", bb),
                                                   ALU.mult, ALU.add))
            sV_h2 = sV.n
            fcev = {}
            for m in range(NFB):
                og = m // 8
                wts = [(sV, sV_h2), (sV, fc_ready[og])]
                if m >= 4:
                    wts.append((sS, fcev[m - 4]))
                else:
                    wts.append((sS, sS.n))
                for kc in range(NB):
                    t(lambda e, m=m, kc=kc: e.matmul(
                        pmm[:, (m % 4) * 512:(m % 4 + 1) * 512],
                        wfce[:, kc * FF + m * 128: kc * FF + (m + 1) * 128],
                        h2[:, kc, :], start=(kc == 0), stop=(kc == 7)),
                      waits=wts if kc == 0 else [], incs=[(sT, 1)] if kc == 7 else [])
                snap = sT.n
                s(lambda e, m=m: e.activation(
                    h1[:, m * 512:(m + 1) * 512],
                    pmm[:, (m % 4) * 512:(m % 4 + 1) * 512], AF.Gelu,
                    bias=bfcv[:, m:m + 1]), waits=[(sT, snap)])
                fcev[m] = sS.n
            pjev = {}
            for mo in range(NB):
                slot = mo % 2
                b.op(("scalar", "sync")[slot],
                     lambda e, mo=mo, slot=slot: e.dma_start(
                         out=wpjc[:, slot * NFB * 128:(slot + 1) * NFB * 128],
                         in_=d_wpjbf[mo]),
                     waits=[(sxp[0], pj_ready[0]), (sxp[1], pj_ready[1]),
                            (sT, wpjc_used[slot])],
                     incs=[(swp[slot], 16)], is_dma=True)
                wts = [(sS, fcev[NFB - 1]), (swp[slot], swp[slot].n)]
                if mo >= 4:
                    wts.append((sS, pjev[mo - 4]))
                for kc in range(NFB):
                    t(lambda e, mo=mo, kc=kc, slot=slot: e.matmul(
                        pmm[:, (mo % 4) * 512:(mo % 4 + 1) * 512],
                        wpjc[:, slot * NFB * 128 + kc * 128:
                             slot * NFB * 128 + (kc + 1) * 128],
                        h1[:, kc * 512:(kc + 1) * 512], start=(kc == 0),
                        stop=(kc == 31)),
                      waits=wts if kc == 0 else [],
                      incs=[(sT, 1)] if kc == 31 else [])
                wpjc_used[slot] = sT.n
                snap = sT.n
                s(lambda e, mo=mo: e.activation(
                    outt[:, mo, :], pmm[:, (mo % 4) * 512:(mo % 4 + 1) * 512],
                    AF.Identity, scale=vb("gb", mo), bias=vb("gbbp", mo)),
                  waits=[(sT, snap), (sV, sV_h2)])
                pjev[mo] = sS.n
            v(lambda e: e.tensor_tensor(outt[:], outt[:], x2t[:], ALU.add),
              waits=[(sS, pjev[NB - 1])])
            sV_out = sV.n
            if j + 1 < NT:
                dma_q("sync", x2t[:],
                      d_x2sp[:, (j + 1) * T:(j + 2) * T].rearrange("(b p) t -> p b t", p=128),
                      sxx, waits=[(sV, sV_out)])
                sx_x2[0] = sxx.n
            dma_q("sync", d_out[:, j * T:(j + 1) * T].rearrange("(b p) t -> p b t", p=128),
                  outt[:], sout, waits=[(sV, sV_out)])
            outfree[0] = sout.n

        b.op("sync", None, waits=[(sout, sout.n), (sxx, sxx.n)], incs=[(bar, 1)])

    nc2 = b.nc
    for c in ctrs:
        b.sems[c.name] = es.enter_context(nc2.semaphore("sem_" + c.name))

    @block.sync
    def _(eng):
        b.replay("sync", eng)

    @block.gpsimd
    def _(eng):
        b.replay("gpsimd", eng)

    @block.vector
    def _(eng):
        b.replay("vector", eng)

    @block.scalar
    def _(eng):
        b.replay("scalar", eng)

    @block.tensor
    def _(eng):
        b.replay("tensor", eng)


def make_in_maps(inputs, ncore=NCORE):
    x = np.asarray(inputs["L0_attn_x"])
    assert x.shape == (4, 8192, 1024)

    misc = np.zeros((128, 16), np.float32)
    misc[:, 0] = 1.0
    misc[:, 2] = EPS
    misc[:, 5] = M32
    misc[:, 6] = M64 - M32

    gsel = np.zeros((128, NB * 16), np.float32)
    gselT = np.zeros((16, NB * 128), np.float32)
    for bb in range(NB):
        for j2 in range(2):
            head = 2 * bb + j2
            gsel[j2 * 64:(j2 + 1) * 64, bb * 16 + head] = 1.0
            gselT[head, bb * 128 + j2 * 64: bb * 128 + (j2 + 1) * 64] = 1.0

    common = {

        "WfcT": np.ascontiguousarray(np.asarray(inputs["L0_ffn_W_fc"], np.float32).T),
        "WprojT": np.ascontiguousarray(np.asarray(inputs["L0_ffn_W_proj"], np.float32).T),
        "Afc": np.ascontiguousarray(inputs["L0_ffn_A_fc"], np.float32),
        "BfcT": np.ascontiguousarray(np.asarray(inputs["L0_ffn_B_fc"], np.float32).T),
        "Aproj": np.ascontiguousarray(inputs["L0_ffn_A_proj"], np.float32),
        "BprojT": np.ascontiguousarray(np.asarray(inputs["L0_ffn_B_proj"], np.float32).T),

        "bfc": np.ascontiguousarray(
            np.asarray(inputs["L0_ffn_b_fc"], np.float32).reshape(NFB, 128).T),
        "bproj": _fm(inputs["L0_ffn_b_proj"]),
        "misc": misc,
        "onesK1": np.ones((1, 128), np.float32),
        "ramp": np.arange(1, T + 1, dtype=np.float32).reshape(1, T),
        "gsel": gsel,
        "gselT": gselT,
    }
    for nm, key in (("WgT", "L0_attn_Wg_w"), ("WrT", "L0_attn_Wr_w"),
                    ("WkT", "L0_attn_Wk_w"), ("WvT", "L0_attn_Wv_w"),
                    ("WoT", "L0_attn_Wo_w")):
        common[nm] = _sw(np.asarray(inputs[key], np.float32).T).reshape(8, 128, 1024)

    vec_names = ("mg", "mr", "mk", "mv", "bg2", "br", "bk", "bv", "uv",
                 "ca1", "sa", "cb1", "sb_", "ga", "gb", "gah", "gabo", "gbbp",
                 "e", "dec", "d512", "carry", "Din", "hlast", "recvmask",
                 "modsc")
    vslot = {n: i for i, n in enumerate(vec_names)}
    vbase = np.zeros((128, 26 * NB), np.float32)

    def vput(vp, name, val):
        vp[:, vslot[name] * NB:(vslot[name] + 1) * NB] = val

    vput(vbase, "mg", _fm(inputs["L0_attn_mg"]))
    vput(vbase, "mr", _fm(inputs["L0_attn_mr"]))
    vput(vbase, "mk", _fm(inputs["L0_attn_mk"]))
    vput(vbase, "mv", _fm(inputs["L0_attn_mv"]))
    vput(vbase, "bg2", _fm(inputs["L0_attn_Wg_b"]))
    vput(vbase, "br", _fm(inputs["L0_attn_Wr_b"]))
    vput(vbase, "bk", _fm(inputs["L0_attn_Wk_b"]))
    vput(vbase, "bv", _fm(inputs["L0_attn_Wv_b"]))
    vput(vbase, "uv", _fm(np.asarray(inputs["L0_attn_u"]).reshape(-1)))
    vput(vbase, "gabo", _fm(inputs["L0_attn_Wo_b"]))
    vput(vbase, "e", _fm(np.asarray(inputs["L0_attn_w"]).reshape(-1)))

    wmod_f = np.asarray(inputs["L0_wmod"], np.float32)
    in_maps = []
    for c in range(NCORE):
        q, half = c // 2, c % 2
        t0 = half * TOK
        xT = np.ascontiguousarray(x[q].T)
        lead = xT[:, (t0 - 1) % 8192:(t0 - 1) % 8192 + 1]
        m = dict(common)
        m["wmodS"] = np.ascontiguousarray(wmod_f[:, half * 3072:(half + 1) * 3072])
        m["xT"] = np.ascontiguousarray(np.concatenate([lead, xT[:, t0:t0 + TOK]], axis=1))
        vp = vbase.copy()
        vput(vp, "modsc", _fm(np.asarray(inputs["L0_c"])[q]))
        vput(vp, "recvmask", np.full((128, NB), float(half), np.float32))
        m["vpack_in"] = vp
        in_maps.append(m)
    return in_maps[:ncore]


def kernel(**inputs):
    in_maps = make_in_maps(inputs)
    bprog = build_program()
    res = run_bass_kernel_spmd(bprog.nc, in_maps, core_ids=list(range(NCORE)))

    out = np.empty((4, 8192, 1024), np.float32)
    for c in range(NCORE):
        q, half = c // 2, c % 2
        t0 = half * TOK
        out[q, t0:t0 + TOK, :] = res.results[c]["outT"].T
    if DBG:
        kernel._dbg = res.results
    return out
